# version 16
# baseline (speedup 1.0000x reference)
import contextlib
import numpy as np
import ml_dtypes

import concourse.bass as bass
import concourse.mybir as mybir
from concourse.bass_utils import run_bass_kernel_spmd

F32 = mybir.dt.float32
BF16 = mybir.dt.bfloat16
I32 = mybir.dt.int32
U8 = mybir.dt.uint8
AF = mybir.ActivationFunctionType
ALU = mybir.AluOpType
AX = mybir.AxisListType

BQ, T, D, H, DH = 2, 2048, 1024, 16, 64
NRC = 8
TILE, NTILE = 512, 4
NG, NCH = 4, 8
FF = 4096
LN127 = float(np.log(np.float64(127.0)))

O_X = 0
O_RF = 4096
O_KF = 8192
O_A4 = 12288
O_A5 = 16448
O_TS = 20544
O_QI = 20544
O_SM = 14336
O_ESC = 21056
O_SMALL = 21568
O_GB2 = 21952
ARNF_COLS = 30144
O_XLNF = 0
O_SF = 0
O_B1 = 2112
O_B2 = 4160
O_B3 = 6208
O_B4 = 8256
O_B5 = 10304
O_B6 = 12352
O_B7 = 14400
O_B8 = 16448
O_XSV = 18496
O_WWIN = O_B5
ARNB_COLS = 37008
S_TINY, S_TINY2, S_ST1, S_ST2, S_LNS, S_LNB = 0, 8, 16, 24, 32, 40
S_GAM, S_EGP, S_E2C, S_D1 = 48, 112, 176, 240
S_MAX, S_DQ, S_DQ2, S_XSV, S_ACC = 304, 312, 320, 328, 336
S_RCC = 370

GBIG = [O_RF, O_KF, O_GB2, O_GB2 + 4096]
GQB = [O_B1, O_B2, O_B3, O_B4]
WDQ = [O_B5, O_B7, O_B8, O_XLNF]


class Prog:
    def __init__(self, nc, es):
        self.nc, self.es = nc, es
        self.sem, self.cnt, self.hwm = {}, {}, {}

    def new(self, name):
        self.sem[name] = self.es.enter_context(self.nc.semaphore(name))
        self.cnt[name] = 0

    def inc(self, instr, name, k):
        self.cnt[name] += k
        instr.then_inc(self.sem[name], k)
        return self.cnt[name]

    def wait(self, eng, ekey, name, val=None):
        if val is None:
            val = self.cnt[name]
        assert val <= self.cnt[name], f"forward wait {name}:{val}>{self.cnt[name]}"
        if val <= 0:
            return
        if self.hwm.get((ekey, name), -1) >= val:
            return
        eng.wait_ge(self.sem[name], val)
        self.hwm[(ekey, name)] = val


def build(n_seq=BQ, n_tiles=NTILE, last_phase="G", dbg=()):
    nc = bass.Bass("TRN2", target_bir_lowering=False, debug=False)
    dt = nc.dram_tensor
    x_h = dt("x", [BQ, T, D], F32, kind="ExternalInput").ap()
    wr_h = dt("wr", [D, D], F32, kind="ExternalInput").ap()
    wk_h = dt("wk", [D, D], F32, kind="ExternalInput").ap()
    wv_h = dt("wv", [D, D], F32, kind="ExternalInput").ap()
    ww_h = dt("ww", [D, D], F32, kind="ExternalInput").ap()
    wo_h = dt("wo", [D, D], F32, kind="ExternalInput").ap()
    wg_h = dt("wg", [FF, D], F32, kind="ExternalInput").ap()
    wu_h = dt("wu", [FF, D], F32, kind="ExternalInput").ap()
    wd_h = dt("wd", [D, FF], F32, kind="ExternalInput").ap()
    u_h = dt("u_p", [128, 8], F32, kind="ExternalInput").ap()
    tm_h = dt("tm", [128, 32], F32, kind="ExternalInput").ap()
    lnwb_h = dt("lnwb", [128, 2], F32, kind="ExternalInput").ap()
    mstrf_h = dt("mstrf", [128, 1024], BF16, kind="ExternalInput").ap()
    mdiaf_h = dt("mdiaf", [128, 1024], U8, kind="ExternalInput").ap()
    bdA_h = dt("bdA", [128, 64], BF16, kind="ExternalInput").ap()
    bdB_h = dt("bdB", [128, 64], BF16, kind="ExternalInput").ap()
    sel2f_h = dt("sel2f", [128, 128], F32, kind="ExternalInput").ap()
    hsel_h = dt("hsel", [128, 2], BF16, kind="ExternalInput").ap()
    onesf_h = dt("onesf", [128, 128], F32, kind="ExternalInput").ap()
    cpool_h = dt("cpool", [128, 10], F32, kind="ExternalInput").ap()
    out_h = dt("out", [BQ, T, D], F32, kind="ExternalOutput").ap()
    wtr_h = dt("wtr", [D, D], BF16, kind="Internal").ap()
    wtk_h = dt("wtk", [D, D], BF16, kind="Internal").ap()
    wtv_h = dt("wtv", [D, D], BF16, kind="Internal").ap()
    wtw_h = dt("wtw", [D, D], BF16, kind="Internal").ap()
    wto_h = dt("wto", [D, D], BF16, kind="Internal").ap()
    wqtg_h = dt("wqtg", [D, FF], BF16, kind="Internal").ap()
    wqtu_h = dt("wqtu", [D, FF], BF16, kind="Internal").ap()
    wqtd_h = dt("wqtd", [FF, D], BF16, kind="Internal").ap()
    dbg_h = {n: dt("dbg_" + n, sh, dy, kind="ExternalOutput").ap() for n, sh, dy in dbg}

    es = contextlib.ExitStack()
    with es:
        sbt = lambda n, s, d: es.enter_context(nc.sbuf_tensor(n, s, d))
        arn = sbt("arn", [128, ARNF_COLS], F32)
        arnb = sbt("arnb", [128, ARNB_COLS], BF16)
        u_sb = sbt("u_sb", [128, 8], F32)
        tm_sb = sbt("tm_sb", [128, 32], F32)
        lnwb = sbt("lnwb_s", [128, 2], F32)
        mstrf = sbt("mstrf_s", [128, 1024], BF16)
        mdiaf = sbt("mdiaf_s", [128, 1024], U8)
        bdA = sbt("bdA_s", [128, 64], BF16)
        bdB = sbt("bdB_s", [128, 64], BF16)
        sel2f = sbt("sel2f_s", [128, 128], F32)
        hsel = sbt("hsel_s", [128, 2], BF16)
        onesf = sbt("onesf_s", [128, 128], F32)
        cpool = sbt("cpool_s", [128, 10], F32)
        state = sbt("state", [128, 512], BF16)
        thr2 = sbt("thr2", [128, 4], F32)
        lnmw = sbt("lnmw", [128, 4], F32)
        ps = es.enter_context(nc.psum_tensor("ps", [128, 4096], F32))

        AR = arn.ap()
        AR16 = arnb.ap()
        ARI = arn.bitcast(I32).ap()
        PIT = ARNF_COLS
        PIT16 = ARNB_COLS

        def f32v(off, cols, p0=0, pn=128):
            return bass.AP(AR.tensor, p0 * PIT + off, [[PIT, pn], [1, cols]])

        def f32d(off, dims, p0=0, pn=128):
            return bass.AP(AR.tensor, p0 * PIT + off, [[PIT, pn]] + dims)

        def bfv(off, cols, p0=0, pn=128):
            return bass.AP(AR16.tensor, p0 * PIT16 + 2 * off, [[PIT16, pn], [1, cols]])

        def bfd(off, dims, p0=0, pn=128):
            return bass.AP(AR16.tensor, p0 * PIT16 + 2 * off, [[PIT16, pn]] + dims)

        def i32v(off, cols, p0=0, pn=128):
            return bass.AP(ARI.tensor, p0 * PIT + off, [[PIT, pn], [1, cols]])

        def smv(col, n, p0=0, pn=128):
            return f32v(O_SMALL + col, n, p0, pn)

        def xsv_ap(n=8):
            return bass.AP(AR16.tensor, 2 * O_XSV, [[PIT16, 128], [1, n]])

        for i, v in enumerate([0.5, -0.5, 1e-5, -LN127, 1.0, 0.0, LN127, -1.0]):
            nc.const_aps.register(F32, float(v), cpool[:, i:i + 1])

        blk = es.enter_context(nc.Block())
        P = Prog(nc, es)
        for s in ("q1", "q2", "pe", "v", "s", "gp", "st", "qx", "qxl", "qwv",
                  "qww", "qwo", "qkht", "qhqt", "qwinA", "qwinB", "qwd", "qgqt",
                  "qldA", "qldB", "qc", "qat", "qwk", "qwr",
                  "qht0", "qht1", "qht2", "qht3"):
            P.new(s)

        eps_c = cpool[:, 2:3]
        one_c = cpool[:, 4:5]
        ln127_c = cpool[:, 6:7]
        neg1_c = cpool[:, 7:8]

        def wload(sy, dst_off, dst_bs, src_h, src_off, pstride, nblk, inner,
                  sem, pieces=1):
            per = max(1, nblk // pieces)
            for s in range(0, nblk, per):
                n = min(per, nblk - s)
                P.inc(sy.dma_start(
                    out=bass.AP(AR16.tensor, dst_off + s * dst_bs,
                                [[PIT16, 128], [dst_bs, n], [1, inner]]),
                    in_=bass.AP(src_h.tensor, src_off + s * 128 * pstride,
                                [[pstride, 128], [128 * pstride, n],
                                 [1, inner]])), sem, 16)

        @blk.sync
        def _(sy):
            for dst, src in ((u_sb, u_h), (tm_sb, tm_h), (lnwb, lnwb_h),
                             (mstrf, mstrf_h), (mdiaf, mdiaf_h), (bdA, bdA_h),
                             (bdB, bdB_h), (sel2f, sel2f_h), (hsel, hsel_h),
                             (onesf, onesf_h), (cpool, cpool_h)):
                P.inc(sy.dma_start(out=dst[:], in_=src), "qc", 16)
        consts_q1 = P.cnt["qc"]
        for eng, key in ((blk.vector, "v"), (blk.scalar, "sc"),
                         (blk.gpsimd, "gp"), (blk.tensor, "pe")):
            @eng
            def _(e, key=key):
                P.wait(e, key, "qc", consts_q1)

        @blk.vector
        def _(v):
            v.tensor_scalar(smv(S_RCC + 1, 1), cpool[:, 4:5], 127.0, None,
                            ALU.mult)
            v.drain()
            v.reciprocal(smv(S_RCC, 1), smv(S_RCC + 1, 1))
            v.drain()
            P.inc(v.engine_nop(), "v", 1)

        att_list = [(wr_h, None, wtr_h), (wk_h, None, wtk_h),
                    (wv_h, None, wtv_h), (ww_h, None, wtw_h), (wo_h, None, wto_h)]
        for i, (src_h, res_t, hbm_t) in enumerate(att_list):
            @blk.gpsimd
            def _(gp, i=i, src_h=src_h):
                if i > 0:
                    P.wait(gp, "gp", "qat")
                for oc in range(8):
                    P.inc(gp.dma_start(
                        out=bass.AP(AR16.tensor, 2 * O_B1 + oc * 1024,
                                    [[PIT16, 128], [1, 1024]]),
                        in_=src_h[oc * 128:(oc + 1) * 128, :]), "q2", 16)

            @blk.sync
            def _(sy, i=i, res_t=res_t, hbm_t=hbm_t):
                P.wait(sy, "sy", "q2")
                if True:
                    for oc in range(8):
                        P.wait(sy, "sy", "qat")
                        for dc in range(8):
                            P.inc(sy.dma_start_transpose(
                                out=bass.AP(AR16.tensor, 2 * O_B7 + dc * 128,
                                            [[PIT16, 128], [1, 128]]),
                                in_=bass.AP(AR16.tensor, 2 * O_B1 + oc * 1024 + dc * 128,
                                            [[PIT16, 128], [1, 128]])), "qat", 16)
                        P.wait(sy, "sy", "qat")
                        src = bass.AP(AR16.tensor, 2 * O_B7,
                                      [[PIT16, 128], [128, 8], [1, 128]])
                        dst = bass.AP(hbm_t.tensor, oc * 128,
                                      [[D, 128], [128 * D, 8], [1, 128]])
                        P.inc(sy.dma_start(out=dst, in_=src), "qat", 16)
        attw_q1 = P.cnt["qat"]

        wstg = [f32v(O_A4, 4096), f32v(O_A5, 4096)]
        absout = f32v(O_X, 4096)
        sqout = f32v(O_RF, 4096)
        sgout = bfv(O_B5, 4096)
        wqnat = bfv(O_B6, 4096)
        accw = smv(S_ACC, 32)
        ffn_specs = [(wg_h, wqtg_h, 32, 1024, FF), (wu_h, wqtu_h, 32, 1024, FF),
                     (wd_h, wqtd_h, 8, 4096, D)]
        for j, (src_h, dst_h, nchunk, ccols, wid) in enumerate(ffn_specs):
            for ch in range(nchunk):
                stg = wstg[ch % 2]

                qld = "qldA" if ch % 2 == 0 else "qldB"

                @blk.sync
                def _(sy, ch=ch, stg=stg, src_h=src_h, ccols=ccols, qld=qld):
                    if ch >= 2 or j > 0:
                        P.wait(sy, "sy", "v")
                        P.wait(sy, "sy", "s")
                    qc_ = ccols // 4
                    for q in range(4):
                        P.inc(sy.dma_start(
                            out=bass.AP(stg.tensor, stg.offset + q * qc_,
                                        [[PIT, 128], [1, qc_]]),
                            in_=src_h[ch * 128:(ch + 1) * 128,
                                      q * qc_:(q + 1) * qc_]), qld, 16)
                ld = P.cnt[qld]

                @blk.vector
                def _(v, ch=ch, stg=stg, ccols=ccols, ld=ld, qld=qld):
                    P.wait(v, "v", qld, ld)
                    stgv = bass.AP(stg.tensor, stg.offset, [[PIT, 128], [1, ccols]])
                    P.inc(v.scalar_tensor_tensor(stgv, stgv, -1.0, stgv,
                                                 ALU.mult, ALU.max,
                                                 accum_out=smv(S_ACC + ch, 1)),
                          "v", 1)

            @blk.vector
            def _(v, j=j, nchunk=nchunk):
                v.drain()
                P.inc(v.reduce_sum(smv(S_TINY, 1), smv(S_ACC, nchunk), axis=AX.X), "v", 1)

            @blk.tensor
            def _(t, j=j):
                P.wait(t, "pe", "v")
                P.wait(t, "pe", "qc", consts_q1)
                P.inc(t.matmul(ps[0:1, 0:1], smv(S_TINY, 1), cpool[:, 4:5],
                               start=True, stop=True), "pe", 1)

            @blk.vector
            def _(v, j=j):
                P.wait(v, "v", "pe")
                v.tensor_scalar(smv(S_TINY, 1, 0, 1), ps[0:1, 0:1],
                                1.0 / (FF * D), 1e-5, ALU.mult, ALU.max)
                v.drain()
                v.tensor_scalar(smv(S_TINY + 1, 1, 0, 1), smv(S_TINY, 1, 0, 1),
                                0.5, None, ALU.mult)
                v.drain()
                v.tensor_mul(smv(S_TINY + 2, 1, 0, 1), smv(S_TINY + 1, 1, 0, 1),
                             smv(S_TINY + 1, 1, 0, 1))
                P.inc(v.tensor_mul(smv(S_TINY + 3, 1, 0, 1), smv(S_TINY, 1, 0, 1),
                                   smv(S_RCC, 1, 0, 1)), "v", 1)

            @blk.tensor
            def _(t, j=j):
                P.wait(t, "pe", "s")
                P.wait(t, "pe", "v")
                P.inc(t.matmul(ps[:, 512:514], onesf[0:1, :], smv(S_TINY + 2, 2, 0, 1),
                               start=True, stop=True), "pe", 1)

            @blk.vector
            def _(v, j=j):
                P.wait(v, "v", "pe")
                v.tensor_copy(thr2[:, j:j + 1], ps[:, 512:513])
                v.tensor_copy(lnmw[:, j:j + 1], ps[:, 513:514])
                v.drain()
                P.inc(v.engine_nop(), "v", 1)
            thr_v = P.cnt["v"]

            npiece = ccols // 128
            for ch in range(nchunk):
                stg = wstg[ch % 2]

                qld = "qldA" if ch % 2 == 0 else "qldB"

                @blk.sync
                def _(sy, ch=ch, stg=stg, src_h=src_h, ccols=ccols, qld=qld):
                    P.wait(sy, "sy", "s")
                    qc_ = ccols // 4
                    for q in range(4):
                        P.inc(sy.dma_start(
                            out=bass.AP(stg.tensor, stg.offset + q * qc_,
                                        [[PIT, 128], [1, qc_]]),
                            in_=src_h[ch * 128:(ch + 1) * 128,
                                      q * qc_:(q + 1) * qc_]), qld, 16)
                ld = P.cnt[qld]

                @blk.scalar
                def _(sc, ch=ch, stg=stg, ccols=ccols, ld=ld, thr_v=thr_v, qld=qld):
                    P.wait(sc, "sc", qld, ld)
                    P.wait(sc, "sc", "v", thr_v)
                    if ch >= 1:
                        P.wait(sc, "sc", "v")
                    sc.activation(bass.AP(sqout.tensor, sqout.offset,
                                          [[PIT, 128], [1, ccols]]),
                                  bass.AP(stg.tensor, stg.offset,
                                          [[PIT, 128], [1, ccols]]), AF.Square)
                    P.inc(sc.activation(bass.AP(sgout.tensor, sgout.offset,
                                                [[PIT16, 128], [1, ccols]]),
                                        bass.AP(stg.tensor, stg.offset,
                                                [[PIT, 128], [1, ccols]]), AF.Sign),
                          "s", 1)
                sq_s = P.cnt["s"]

                @blk.vector
                def _(v, ch=ch, j=j, ccols=ccols, sq_s=sq_s, thr_v=thr_v):
                    P.wait(v, "v", "s", sq_s)
                    P.wait(v, "v", "v", thr_v)
                    P.wait(v, "v", "q1")
                    P.inc(v.scalar_tensor_tensor(
                        bass.AP(wqnat.tensor, wqnat.offset, [[PIT16, 128], [1, ccols]]),
                        bass.AP(sqout.tensor, sqout.offset, [[PIT, 128], [1, ccols]]),
                        thr2[:, j:j + 1],
                        bass.AP(sgout.tensor, sgout.offset, [[PIT16, 128], [1, ccols]]),
                        ALU.is_gt, ALU.mult), "v", 1)
                wq_v = P.cnt["v"]

                @blk.sync
                def _(sy, ch=ch, j=j, npiece=npiece, wid=wid, dst_h=dst_h, wq_v=wq_v):
                    P.wait(sy, "sy", "v", wq_v)
                    P.wait(sy, "sy", "qat")
                    P.wait(sy, "sy", "q1")
                    for pc in range(npiece):
                        P.inc(sy.dma_start_transpose(
                            out=bass.AP(AR16.tensor, 2 * O_B7 + pc * 128,
                                        [[PIT16, 128], [1, 128]]),
                            in_=bass.AP(AR16.tensor, 2 * O_B6 + pc * 128,
                                        [[PIT16, 128], [1, 128]])), "q1", 16)
                    P.wait(sy, "sy", "q1")
                    src = bass.AP(AR16.tensor, 2 * O_B7,
                                  [[PIT16, 128], [128, npiece], [1, 128]])
                    dst = bass.AP(dst_h.tensor, ch * 128,
                                  [[wid, 128], [128 * wid, npiece], [1, 128]])
                    P.inc(sy.dma_start(out=dst, in_=src), "q1", 16)
        ffnw_q1 = P.cnt["q1"]

        @blk.vector
        def _(v):
            P.wait(v, "v", "pe")
            P.inc(v.tensor_copy(smv(S_TINY2, 3), lnmw[:, 0:3]), "v", 1)
        tiny2_v = P.cnt["v"]

        PH = ["A", "B", "C", "D", "E", "F", "G"]
        phlim = PH.index(last_phase)

        def phase_on(p):
            return PH.index(p) <= phlim


        for s_i in range(n_seq):
            @blk.vector
            def _(v, s_i=s_i):
                if s_i > 0:
                    P.wait(v, "v", "pe")
                P.inc(v.memset(state[:], 0.0), "v", 1)

            for t_i in range(n_tiles):
                tok0 = t_i * TILE
                @blk.sync
                def _(sy, s_i=s_i, t_i=t_i, tok0=tok0):
                    P.wait(sy, "sy", "v")
                    P.wait(sy, "sy", "st")
                    src = bass.AP(x_h.tensor, s_i * T * D + tok0 * D,
                                  [[D, 128], [128 * D, NG], [1, D]])
                    P.inc(sy.dma_start(out=f32d(O_X, [[1024, NG], [1, 1024]]), in_=src),
                          "qx", 16)
                xload = P.cnt["qx"]

                @blk.vector
                def _(v, xload=xload):
                    P.wait(v, "v", "qx", xload)
                    P.wait(v, "v", "v", tiny2_v)
                    for g in range(NG):
                        xg = f32v(O_X + g * 1024, 1024)
                        v.reduce_sum(smv(S_ST1 + g, 1), xg, axis=AX.X)
                        v.scalar_tensor_tensor(f32v(O_A5 + g * 1024, 1024), xg, 1.0, xg,
                                               ALU.mult, ALU.mult,
                                               accum_out=smv(S_ST2 + g, 1))
                    v.drain()
                    v.tensor_scalar(smv(S_LNB, NG), smv(S_ST1, NG), 1.0 / D, None,
                                    ALU.mult)
                    v.drain()
                    v.tensor_mul(smv(S_TINY, NG), smv(S_LNB, NG), smv(S_LNB, NG))
                    v.drain()
                    P.inc(v.scalar_tensor_tensor(smv(S_LNS, NG), smv(S_ST2, NG),
                                                 1.0 / D, smv(S_TINY, NG),
                                                 ALU.mult, ALU.subtract), "v", 1)
                lnst_v = P.cnt["v"]

                @blk.scalar
                def _(sc, lnst_v=lnst_v):
                    P.wait(sc, "sc", "v", lnst_v)
                    sc.activation(smv(S_TINY, NG), smv(S_LNS, NG), AF.Ln, bias=eps_c)
                    sc.drain()
                    P.inc(sc.activation(smv(S_LNS, NG), smv(S_TINY, NG), AF.Exp,
                                        scale=-0.5), "s", 1)

                @blk.vector
                def _(v):
                    P.wait(v, "v", "s")
                    P.inc(v.scalar_tensor_tensor(smv(S_LNB, NG), smv(S_LNB, NG), -1.0,
                                                 smv(S_LNS, NG), ALU.mult, ALU.mult),
                          "v", 1)
                lnab_v = P.cnt["v"]

                @blk.scalar
                def _(sc, lnab_v=lnab_v):
                    P.wait(sc, "sc", "v", lnab_v)
                    P.wait(sc, "sc", "pe")
                    for g in range(NG):
                        P.inc(sc.activation(bfv(O_B1 + g * 512, 1024),
                                            f32v(O_X + g * 1024, 1024), AF.Identity,
                                            bias=smv(S_LNB + g, 1),
                                            scale=smv(S_LNS + g, 1)), "s", 1)
                lnap_s = P.cnt["s"]

                @blk.vector
                def _(v, t_i=t_i):
                    bnd = bass.AP(AR16.tensor, 2 * O_XLNF + 15,
                                  [[PIT16, 128], [528, 8], [1, 1]])
                    if t_i == 0:
                        P.inc(v.memset(bnd, 0.0), "v", 1)
                    else:
                        P.inc(v.tensor_copy(bnd, xsv_ap()), "v", 1)
                bnd_v = P.cnt["v"]

                @blk.sync
                def _(sy, lnap_s=lnap_s, bnd_v=bnd_v):
                    P.wait(sy, "sy", "s", lnap_s)
                    P.wait(sy, "sy", "v", bnd_v)
                    P.wait(sy, "sy", "gp")
                    for g in range(NG):
                        for dc in range(8):
                            P.inc(sy.dma_start_transpose(
                                out=bass.AP(AR16.tensor,
                                            2 * O_XLNF + dc * 528 + 16 + g * 128,
                                            [[PIT16, 128], [1, 128]]),
                                in_=bass.AP(AR16.tensor,
                                            2 * O_B1 + g * 1024 + dc * 128,
                                            [[PIT16, 128], [1, 128]])), "qxl", 16)
                    P.wait(sy, "sy", "pe")
                    P.wait(sy, "sy", "q1", ffnw_q1)
                    wload(sy, 2 * O_WWIN, 1024, wtr_h, 0, D, 8, 1024, "qwr")
                    P.wait(sy, "sy", "qxl")
                    P.wait(sy, "sy", "qgqt")
                    wload(sy, 2 * O_B1, 1024, wtk_h, 0, D, 4, 1024, "qwk")
                    wload(sy, 2 * O_B8, 1024, wtk_h, 512 * D, D, 4, 1024, "qwk")
                xlnf_q1 = P.cnt["qxl"]
                wr_q1 = P.cnt["qwr"]
                wk_q1 = P.cnt["qwk"]
                if not phase_on("B"):
                    continue

                @blk.vector
                def _(v, xlnf_q1=xlnf_q1):
                    P.wait(v, "v", "qxl", xlnf_q1)
                    last = bass.AP(AR16.tensor, 2 * O_XLNF + 527,
                                   [[PIT16, 128], [528, 8], [1, 1]])
                    v.tensor_copy(xsv_ap(), last)
                    cur = bass.AP(AR16.tensor, 2 * O_XLNF + 16,
                                  [[PIT16, 128], [528, 8], [1, 512]])
                    prv = bass.AP(AR16.tensor, 2 * O_XLNF + 15,
                                  [[PIT16, 128], [528, 8], [1, 512]])
                    P.inc(v.tensor_sub(bfd(O_B4, [[512, 8], [1, 512]]), cur, prv),
                          "v", 1)
                dlt_v = P.cnt["v"]

                def emit_mix(eng, dst_off, mcol, is_gp=False):
                    for dc in range(8):
                        xprev = bass.AP(AR16.tensor, 2 * O_XLNF + dc * 528 + 15,
                                        [[PIT16, 128], [1, 512]])
                        if is_gp:
                            tmp = f32v(O_ESC, 512)
                            eng.tensor_scalar(tmp, bfv(O_B4 + dc * 256, 512),
                                              tm_sb[:, mcol * 8 + dc: mcol * 8 + dc + 1],
                                              None, ALU.mult)
                            ins = eng.tensor_add(bfv(dst_off + dc * 256, 512),
                                                 tmp, xprev)
                        else:
                            ins = eng.scalar_tensor_tensor(
                                bfv(dst_off + dc * 256, 512),
                                bfv(O_B4 + dc * 256, 512),
                                tm_sb[:, mcol * 8 + dc: mcol * 8 + dc + 1],
                                xprev, ALU.mult, ALU.add)
                    return ins

                @blk.vector
                def _(v, dlt_v=dlt_v):
                    v.drain()
                    P.inc(emit_mix(v, O_B2, 0), "v", 1)
                mixr_v = P.cnt["v"]

                @blk.vector
                def _(v):
                    P.inc(emit_mix(v, O_B3, 1), "v", 1)
                mixk_gp = P.cnt["v"]

                def emit_proj(wsrc, mix_off, msem, mval, evac_fn, extra_pe_wait=None):
                    s_entry = P.cnt["s"]
                    for oc in range(8):
                        b = oc % 2

                        @blk.tensor
                        def _(t, oc=oc, b=b, wsrc=wsrc, mix_off=mix_off,
                              msem=msem, mval=mval, extra_pe_wait=extra_pe_wait,
                              s_entry=s_entry):
                            P.wait(t, "pe", msem, mval)
                            P.wait(t, "pe", "qat", attw_q1)
                            if extra_pe_wait is not None:
                                P.wait(t, "pe", *extra_pe_wait)
                            if oc < 2:
                                P.wait(t, "pe", "s", s_entry)
                            else:
                                P.wait(t, "pe", "s", P.cnt["s"] - 1)
                            for kc in range(8):
                                m = t.matmul(
                                    ps[:, b * 512:(b + 1) * 512],
                                    wsrc(kc, oc),
                                    bfv(mix_off + kc * 256, 512),
                                    start=(kc == 0), stop=(kc == 7))
                            P.inc(m, "pe", 1)
                        pm = P.cnt["pe"]

                        @blk.scalar
                        def _(sc, oc=oc, b=b, pm=pm):
                            P.wait(sc, "sc", "pe", pm)
                            P.inc(evac_fn(sc, oc, ps[:, b * 512:(b + 1) * 512]), "s", 1)
                    return P.cnt["s"]

                wsrc_win = lambda kc, oc: bass.AP(AR16.tensor,
                                                  2 * O_WWIN + kc * 1024 + oc * 128,
                                                  [[PIT16, 128], [1, 128]])
                wsrc_win2 = lambda kc, oc: bass.AP(
                    AR16.tensor,
                    2 * (O_B1 if kc < 4 else O_B8) + (kc % 4) * 1024 + oc * 128,
                    [[PIT16, 128], [1, 128]])

                r_done = emit_proj(wsrc_win, O_B2, "v", mixr_v,
                                   lambda sc, oc, pb: sc.activation(
                                       f32v(O_RF + oc * 512, 512), pb, AF.Sigmoid),
                                   extra_pe_wait=("qwr", wr_q1))

                @blk.sync
                def _(sy):
                    P.wait(sy, "sy", "pe")
                    wload(sy, 2 * O_WWIN, 1024, wtv_h, 0, D, 8, 1024, "qwv")
                wv_q1 = P.cnt["qwv"]
                k_done = emit_proj(wsrc_win2, O_B3, "v", mixk_gp,
                                   lambda sc, oc, pb: sc.activation(
                                       f32v(O_KF + oc * 512, 512), pb, AF.Identity),
                                   extra_pe_wait=("qwk", wk_q1))

                @blk.sync
                def _(sy):
                    P.wait(sy, "sy", "pe")
                    wload(sy, 2 * O_B1, 1024, wtw_h, 0, D, 4, 1024, "qww")
                    wload(sy, 2 * O_B8, 1024, wtw_h, 512 * D, D, 4, 1024, "qww")
                ww_q1 = P.cnt["qww"]

                @blk.vector
                def _(v):
                    P.wait(v, "v", "pe")
                    v.drain()
                    P.inc(emit_mix(v, O_B3, 2), "v", 1)
                mixv_gp = P.cnt["v"]


                for g in range(NG):
                    @blk.tensor
                    def _(t, g=g, mixv_gp=mixv_gp, wv_q1=wv_q1):
                        P.wait(t, "pe", "v", mixv_gp)
                        P.wait(t, "pe", "qwv", wv_q1)
                        if g >= 1:
                            P.wait(t, "pe", "v", P.cnt["v"])
                        for half in range(2):
                            b = 2 + half
                            for kc in range(8):
                                m = t.matmul(
                                    ps[:, b * 512:(b + 1) * 512],
                                    bass.AP(AR16.tensor,
                                            2 * O_B3 + kc * 512 + g * 128,
                                            [[PIT16, 128], [1, 128]]),
                                    bass.AP(AR16.tensor,
                                            2 * O_WWIN + kc * 1024 + half * 512,
                                            [[PIT16, 128], [1, 512]]),
                                    start=(kc == 0), stop=(kc == 7))
                        P.inc(m, "pe", 1)
                    vg_pe = P.cnt["pe"]

                    @blk.vector
                    def _(v, g=g, vg_pe=vg_pe):
                        P.wait(v, "v", "pe", vg_pe)
                        P.wait(v, "v", "q1", ffnw_q1)
                        P.inc(v.tensor_copy(bfv(O_B7 + g * 512, 1024),
                                            ps[:, 1024:2048]), "v", 1)
                vtok_v = P.cnt["v"]

                @blk.vector
                def _(v):
                    P.wait(v, "v", "pe")
                    v.drain()
                    P.inc(emit_mix(v, O_B2, 3), "v", 1)
                mixw_v = P.cnt["v"]

                w_done = emit_proj(wsrc_win2, O_B2, "v", mixw_v,
                                   lambda sc, oc, pb: sc.activation(
                                       f32v(O_A5 + oc * 512, 512), pb, AF.Exp,
                                       scale=-1.0),
                                   extra_pe_wait=("qww", ww_q1))
                if not phase_on("C"):
                    continue

                @blk.scalar
                def _(sc, w_done=w_done):
                    P.wait(sc, "sc", "s", w_done)
                    sc.drain()
                    P.inc(sc.activation(f32v(O_A5, 4096), f32v(O_A5, 4096), AF.Ln,
                                        bias=one_c), "s", 1)
                lp_s = P.cnt["s"]

                @blk.vector
                def _(v, lp_s=lp_s):
                    P.wait(v, "v", "s", lp_s)
                    P.wait(v, "v", "pe")
                    v.tensor_scalar(f32v(O_A4 + 1, 4096), f32v(O_A5, 4096), -1.0,
                                    -1e-4, ALU.mult, ALU.add)
                    v.memset(f32v(O_A4, 1), 0.0)
                    v.drain()
                    P.inc(v.tensor_tensor_scan(f32v(O_A4 + 1, 4096), f32v(O_A4 + 1, 4096),
                                               cpool[:, 5:6].broadcast_to((128, 4096)),
                                               0.0, ALU.add, ALU.add), "v", 1)
                    a_end = f32d(O_A4 + 64, [[64, 64]])
                    a_prev = f32d(O_A4, [[64, 64]])
                    v.drain()
                    v.tensor_sub(smv(S_D1, 64), a_end, a_prev)
                    v.drain()
                    P.inc(v.scalar_tensor_tensor(smv(S_GAM, 64), smv(S_D1, 64), 0.5,
                                                 a_prev, ALU.mult, ALU.add), "v", 1)
                scan_v = P.cnt["v"]

                @blk.scalar
                def _(sc, scan_v=scan_v):
                    P.wait(sc, "sc", "v", scan_v)
                    sc.activation(smv(S_EGP, 64), smv(S_D1, 64), AF.Exp, scale=0.5)
                    P.inc(sc.activation(smv(S_E2C, 64), smv(S_D1, 64), AF.Exp), "s", 1)
                egp_s = P.cnt["s"]

                gam_rep = bass.AP(AR.tensor, O_SMALL + S_GAM,
                                  [[PIT, 128], [1, 64], [0, 64]])
                egp_rep = bass.AP(AR.tensor, O_SMALL + S_EGP,
                                  [[PIT, 128], [1, 64], [0, 64]])

                @blk.vector
                def _(v, egp_s=egp_s):
                    P.wait(v, "v", "s", egp_s)
                    P.inc(v.tensor_sub(f32d(O_A5, [[64, 64], [1, 64]]),
                                       f32d(O_A4, [[64, 64], [1, 64]]), gam_rep), "v", 1)
                arg1_v = P.cnt["v"]

                @blk.scalar
                def _(sc, arg1_v=arg1_v):
                    P.wait(sc, "sc", "v", arg1_v)
                    P.inc(sc.activation(f32v(O_A5, 4096), f32v(O_A5, 4096), AF.Exp),
                          "s", 1)
                ea_s = P.cnt["s"]

                @blk.vector
                def _(v, ea_s=ea_s, r_done=r_done):
                    P.wait(v, "v", "s", ea_s)
                    P.wait(v, "v", "s", r_done)
                    P.inc(v.tensor_mul(bfv(O_B4, 4096), f32v(O_RF, 4096),
                                       f32v(O_A5, 4096)), "v", 1)
                rt_v = P.cnt["v"]

                @blk.gpsimd
                def _(gp, rt_v=rt_v):
                    P.wait(gp, "gp", "v", rt_v)
                    P.wait(gp, "gp", "q1", ffnw_q1)
                    P.inc(gp.tensor_tensor(bfd(O_B6, [[64, 64], [1, 64]]),
                                           bfd(O_B4, [[64, 64], [1, 64]]),
                                           egp_rep, ALU.mult), "gp", 1)
                rh_gp = P.cnt["gp"]

                @blk.vector
                def _(v):
                    v.drain()
                    P.inc(v.tensor_sub(f32d(O_A5, [[64, 64], [1, 64]]), gam_rep,
                                       f32d(O_A4 + 1, [[64, 64], [1, 64]])), "v", 1)
                arg2_v = P.cnt["v"]

                @blk.scalar
                def _(sc, arg2_v=arg2_v):
                    P.wait(sc, "sc", "v", arg2_v)
                    P.inc(sc.activation(f32v(O_A5, 4096), f32v(O_A5, 4096), AF.Exp),
                          "s", 1)
                ek_s = P.cnt["s"]

                @blk.vector
                def _(v, ek_s=ek_s, k_done=k_done):
                    P.wait(v, "v", "s", ek_s)
                    P.wait(v, "v", "s", k_done)
                    P.inc(v.tensor_mul(bfv(O_B5, 4096), f32v(O_KF, 4096),
                                       f32v(O_A5, 4096)), "v", 1)
                kt_v = P.cnt["v"]

                @blk.gpsimd
                def _(gp, kt_v=kt_v):
                    P.wait(gp, "gp", "v", kt_v)
                    P.inc(gp.tensor_tensor(bfd(O_B2, [[64, 64], [1, 64]]),
                                           bfd(O_B5, [[64, 64], [1, 64]]),
                                           egp_rep, ALU.mult), "gp", 1)
                kh_gp = P.cnt["gp"]

                for rc in range(NRC):
                    @blk.scalar
                    def _(sc, rc=rc, k_done=k_done):
                        P.wait(sc, "sc", "s", k_done)
                        if rc > 0:
                            P.wait(sc, "sc", "v", P.cnt["v"])
                        P.inc(sc.activation(f32v(O_ESC, 512),
                                            f32v(O_KF + rc * 512, 512), AF.Exp,
                                            bias=u_sb[:, rc:rc + 1]), "s", 1)
                    e_s = P.cnt["s"]

                    @blk.vector
                    def _(v, rc=rc, e_s=e_s, r_done=r_done):
                        P.wait(v, "v", "s", e_s)
                        P.wait(v, "v", "s", r_done)
                        P.inc(v.tensor_mul(bfv(O_B3 + rc * 256, 512),
                                           f32v(O_RF + rc * 512, 512),
                                           f32v(O_ESC, 512)), "v", 1)
                re_v = P.cnt["v"]

                @blk.sync
                def _(sy, kh_gp=kh_gp):
                    P.wait(sy, "sy", "gp", kh_gp)
                    P.wait(sy, "sy", "pe")
                    for rc in range(NRC):
                        for cp in range(4):
                            o = rc * 512 + cp * 128
                            P.inc(sy.dma_start_transpose(
                                out=bass.AP(AR16.tensor, 2 * O_B1 + o,
                                            [[PIT16, 128], [1, 128]]),
                                in_=bass.AP(AR16.tensor, 2 * O_B2 + o,
                                            [[PIT16, 128], [1, 128]])), "qkht", 16)
                kht_q1 = P.cnt["qkht"]
                if not phase_on("D"):
                    continue

                import os
                nch_lim = int(os.environ.get("KCH", NCH))
                dstop = os.environ.get("KDSTOP", "")
                sel_m, oev_m, stup_m = {}, {}, {}
                for c in range(nch_lim):
                    q = c % 2
                    rq = 64 * q
                    scb = 0 if q == 0 else 2

                    @blk.tensor
                    def _(t, c=c, q=q, rq=rq, scb=scb):
                        P.wait(t, "pe", "v", rt_v)
                        P.wait(t, "pe", "v", kt_v)
                        P.wait(t, "pe", "v", re_v)
                        if c >= 2:
                            P.wait(t, "pe", "v", sel_m[c - 2])
                            P.wait(t, "pe", "v", stup_m[c - 2])
                        for rc in range(NRC):
                            co = rc * 512 + c * 64
                            t.matmul(ps[rq:rq + 64, scb * 512 + rc * 64: scb * 512 + rc * 64 + 64],
                                     bass.AP(AR16.tensor, 2 * O_B5 + co,
                                             [[PIT16, 64], [1, 64]]),
                                     bass.AP(AR16.tensor, 2 * O_B4 + co,
                                             [[PIT16, 64], [1, 64]]),
                                     start=True, stop=True, tile_position=(0, rq))
                            t.matmul(ps[rq:rq + 64, (scb + 1) * 512 + rc * 64: (scb + 1) * 512 + rc * 64 + 64],
                                     bass.AP(AR16.tensor, 64 * PIT16 + 2 * O_B5 + co,
                                             [[PIT16, 64], [1, 64]]),
                                     bass.AP(AR16.tensor, 64 * PIT16 + 2 * O_B4 + co,
                                             [[PIT16, 64], [1, 64]]),
                                     start=True, stop=True, tile_position=(64, rq))
                        if c >= 1:
                            P.wait(t, "pe", "v", sel_m[c - 1])
                        for rc in range(NRC):
                            co = rc * 512 + c * 64
                            t.matmul(ps[rq:rq + 64, 4 * 512 + rc * 64: 4 * 512 + rc * 64 + 64],
                                     bdA[:, 0:64],
                                     bass.AP(AR16.tensor, 2 * O_B3 + co,
                                             [[PIT16, 128], [1, 64]]),
                                     start=True, stop=True, tile_position=(0, rq))
                            m = t.matmul(ps[rq:rq + 64, 5 * 512 + rc * 64: 5 * 512 + rc * 64 + 64],
                                         bdB[:, 0:64],
                                         bass.AP(AR16.tensor, 2 * O_B3 + co,
                                                 [[PIT16, 128], [1, 64]]),
                                         start=True, stop=True, tile_position=(0, rq))
                        P.inc(m, "pe", 1)
                    sc_pe = P.cnt["pe"]
                    if dstop == "scores":
                        sel_m[c] = P.cnt["v"]; oev_m[c] = P.cnt["s"]; stup_m[c] = P.cnt["v"]
                        continue

                    @blk.vector
                    def _(v, c=c, rq=rq, scb=scb, sc_pe=sc_pe):
                        P.wait(v, "v", "pe", sc_pe)
                        sfv = bass.AP(AR16.tensor, rq * PIT16 + 2 * O_SF,
                                      [[PIT16, 64], [1, 1024]])
                        psc = bass.AP(ps.ap().tensor, rq * 4096 + scb * 512,
                                      [[4096, 64], [1, 1024]])
                        pdd = bass.AP(ps.ap().tensor, rq * 4096 + 4 * 512,
                                      [[4096, 64], [1, 1024]])
                        v.tensor_tensor(sfv, psc, mstrf[rq:rq + 64, :], ALU.mult)
                        v.drain()
                        P.inc(v.select(sfv, mdiaf[rq:rq + 64, :], pdd, sfv,
                                       add_drain=True), "v", 1)
                    sel_m[c] = P.cnt["v"]
                    if dstop == "sel":
                        oev_m[c] = P.cnt["s"]; stup_m[c] = P.cnt["v"]
                        continue

                    @blk.tensor
                    def _(t, c=c, q=q, rq=rq, scb=scb):
                        P.wait(t, "pe", "v", sel_m[c])
                        P.wait(t, "pe", "v", vtok_v)
                        P.wait(t, "pe", "qkht", kht_q1)
                        if c >= 1:
                            P.wait(t, "pe", "v", oev_m[c - 1])
                            P.wait(t, "pe", "v", stup_m[c - 1])
                        g = c // 2
                        for rc in range(NRC):
                            vA = bass.AP(AR16.tensor,
                                         rq * PIT16 + 2 * O_B7 + g * 1024 + rc * 128,
                                         [[PIT16, 64], [1, 64]])
                            vB = bass.AP(AR16.tensor,
                                         rq * PIT16 + 2 * O_B7 + g * 1024 + rc * 128 + 64,
                                         [[PIT16, 64], [1, 64]])
                            sfA = bass.AP(AR16.tensor, rq * PIT16 + 2 * O_SF + rc * 64,
                                          [[PIT16, 64], [1, 64]])
                            sfB = bass.AP(AR16.tensor,
                                          rq * PIT16 + 2 * O_SF + 512 + rc * 64,
                                          [[PIT16, 64], [1, 64]])
                            t.matmul(ps[0:64, 6 * 512 + rc * 64: 6 * 512 + rc * 64 + 64],
                                     state[0:64, rc * 64: rc * 64 + 64],
                                     bass.AP(AR16.tensor, 2 * O_B6 + rc * 512 + c * 64,
                                             [[PIT16, 64], [1, 64]]),
                                     start=True, stop=True, tile_position=(0, 0))
                            t.matmul(ps[0:64, 7 * 512 + rc * 64: 7 * 512 + rc * 64 + 64],
                                     vA, sfA, start=True, stop=True,
                                     tile_position=(rq, 0))
                            t.matmul(ps[64:128, 6 * 512 + rc * 64: 6 * 512 + rc * 64 + 64],
                                     state[64:128, rc * 64: rc * 64 + 64],
                                     bass.AP(AR16.tensor,
                                             64 * PIT16 + 2 * O_B6 + rc * 512 + c * 64,
                                             [[PIT16, 64], [1, 64]]),
                                     start=True, stop=True, tile_position=(64, 64))
                            t.matmul(ps[64:128, 7 * 512 + rc * 64: 7 * 512 + rc * 64 + 64],
                                     vB, sfB, start=True, stop=True,
                                     tile_position=(rq, 64))
                            kA = bass.AP(AR16.tensor,
                                         rq * PIT16 + 2 * O_B1 + rc * 512 + (c // 2) * 128,
                                         [[PIT16, 64], [1, 64]])
                            kB = bass.AP(AR16.tensor,
                                         rq * PIT16 + 2 * O_B1 + rc * 512 + (c // 2) * 128 + 64,
                                         [[PIT16, 64], [1, 64]])
                            t.matmul(ps[0:64, scb * 512 + rc * 64: scb * 512 + rc * 64 + 64],
                                     kA, vA, start=True, stop=True,
                                     tile_position=(rq, 0))
                            m = t.matmul(ps[64:128, scb * 512 + rc * 64: scb * 512 + rc * 64 + 64],
                                         kB, vB, start=True, stop=True,
                                         tile_position=(rq, 64))
                        P.inc(m, "pe", 1)
                    av_pe = P.cnt["pe"]
                    if dstop == "av":
                        oev_m[c] = P.cnt["s"]; stup_m[c] = P.cnt["v"]
                        continue

                    @blk.scalar
                    def _(sc, c=c, av_pe=av_pe):
                        P.wait(sc, "sc", "pe", av_pe)
                        srcC = bass.AP(ps.ap().tensor, 6 * 512,
                                       [[4096, 128], [64, 8], [1, 64]])
                        P.inc(sc.activation(f32v(O_TS, 512), srcC, AF.Identity),
                              "s", 1)
                    oevs = P.cnt["s"]

                    @blk.vector
                    def _(v, c=c, av_pe=av_pe, oevs=oevs):
                        P.wait(v, "v", "pe", av_pe)
                        P.wait(v, "v", "s", oevs)
                        dst = bass.AP(AR16.tensor, 2 * (O_B8) + c * 64,
                                      [[PIT16, 128], [512, 8], [1, 64]])
                        srcI = bass.AP(ps.ap().tensor, 7 * 512,
                                       [[4096, 128], [64, 8], [1, 64]])
                        P.inc(v.tensor_add(dst, f32v(O_TS, 512), srcI), "v", 1)
                    oev_m[c] = P.cnt["v"]

                    @blk.vector
                    def _(v, c=c, av_pe=av_pe, scb=scb):
                        P.wait(v, "v", "pe", av_pe)
                        e2c_rep = bass.AP(AR.tensor, O_SMALL + S_E2C + c,
                                          [[PIT, 128], [8, 8], [0, 64]])
                        v.drain()
                        v.tensor_tensor(f32d(O_A5, [[64, 8], [1, 64]]),
                                        state[:].rearrange("p (a b) -> p a b", a=8),
                                        e2c_rep, ALU.mult)
                        v.drain()
                        P.inc(v.tensor_add(state[:], f32v(O_A5, 512),
                                           ps[:, scb * 512: (scb + 1) * 512]), "v", 1)
                    stup_m[c] = P.cnt["v"]

                @blk.sync
                def _(sy):
                    P.wait(sy, "sy", "pe")
                    wload(sy, 2 * O_B1, 1024, wto_h, 0, D, 8, 1024, "qwo")
                wo_q1 = P.cnt["qwo"]
                if not phase_on("E"):
                    continue

                gst_done = None
                for rc in range(NRC):
                    @blk.scalar
                    def _(sc, rc=rc):
                        P.wait(sc, "sc", "v", oev_m[nch_lim - 1])
                        if rc > 0:
                            P.wait(sc, "sc", "pe", P.cnt["pe"])
                        P.inc(sc.activation(bass.AP(AR16.tensor, 2 * O_SF,
                                                    [[PIT16, 128], [1, 512]]),
                                            bfv(O_B8 + rc * 256, 512), AF.Square),
                              "s", 1)
                    o2_s = P.cnt["s"]

                    @blk.tensor
                    def _(t, rc=rc, o2_s=o2_s):
                        P.wait(t, "pe", "s", o2_s)
                        P.wait(t, "pe", "v", sel_m[nch_lim - 1])
                        P.wait(t, "pe", "v", stup_m[nch_lim - 1])
                        b1 = rc // 4
                        pb = 32 * (rc % 4)
                        t.matmul(ps[pb:pb + 2, b1 * 512:(b1 + 1) * 512],
                                 hsel[:, 0:2], bfv(O_B8 + rc * 256, 512),
                                 start=True, stop=True, tile_position=(0, pb))
                        m = t.matmul(ps[pb:pb + 2, (2 + b1) * 512:(3 + b1) * 512],
                                     hsel[:, 0:2],
                                     bass.AP(AR16.tensor, 2 * O_SF,
                                             [[PIT16, 128], [1, 512]]),
                                     start=True, stop=True, tile_position=(0, pb))
                        P.inc(m, "pe", 1)
                gst_pe = P.cnt["pe"]

                og_v = None
                for b1 in range(2):
                    @blk.vector
                    def _(v, b1=b1, gst_pe=gst_pe):
                        P.wait(v, "v", "pe", gst_pe)
                        mu = f32v(O_SM, 512)
                        v.tensor_scalar(mu, ps[:, b1 * 512:(b1 + 1) * 512], 1.0 / DH,
                                        1e6, ALU.mult, ALU.min)
                        v.drain()
                        v.tensor_scalar_max(mu, mu, -1e6)
                        v.drain()
                        v.tensor_mul(f32v(O_A5, 512), mu, mu)
                        v.drain()
                        P.inc(v.scalar_tensor_tensor(f32v(O_SM + 512, 512),
                                                     ps[:, (2 + b1) * 512:(3 + b1) * 512],
                                                     1.0 / DH, f32v(O_A5, 512),
                                                     ALU.mult, ALU.subtract), "v", 1)
                    gv_v = P.cnt["v"]

                    @blk.vector
                    def _(v, b1=b1, gv_v=gv_v):
                        v.drain()
                        P.inc(v.tensor_scalar(f32v(O_SM + 512, 512),
                                              f32v(O_SM + 512, 512), 0.0, 1e18,
                                              ALU.max, ALU.min), "v", 1)
                    gvc_v = P.cnt["v"]

                    @blk.scalar
                    def _(sc, b1=b1, gvc_v=gvc_v):
                        P.wait(sc, "sc", "v", gvc_v)
                        sc.activation(f32v(O_A5, 512), f32v(O_SM + 512, 512), AF.Ln,
                                      bias=eps_c)
                        sc.drain()
                        P.inc(sc.activation(f32v(O_SM + 512, 512), f32v(O_A5, 512),
                                            AF.Exp, scale=-0.5), "s", 1)
                    gs_s = P.cnt["s"]

                    @blk.vector
                    def _(v, b1=b1, gs_s=gs_s):
                        P.wait(v, "v", "s", gs_s)
                        P.inc(v.tensor_mul(f32v(O_SM, 512), f32v(O_SM, 512),
                                           f32v(O_SM + 512, 512)), "v", 1)
                    gm2_v = P.cnt["v"]

                    for rr in range(4):
                        rc = b1 * 4 + rr
                        pb = 32 * rr

                        @blk.tensor
                        def _(t, rc=rc, pb=pb, gm2_v=gm2_v):
                            P.wait(t, "pe", "v", gm2_v)
                            if rc >= 2:
                                P.wait(t, "pe", "v", P.cnt["v"])
                            bs = 4 + (rc % 2)
                            bm = 6 + (rc % 2)
                            t.matmul(ps[:, bs * 512:(bs + 1) * 512], sel2f[pb:pb + 2, :],
                                     f32v(O_SM + 512, 512, pb, 2),
                                     start=True, stop=True, tile_position=(pb, 0))
                            m = t.matmul(ps[:, bm * 512:(bm + 1) * 512],
                                         sel2f[pb:pb + 2, :], f32v(O_SM, 512, pb, 2),
                                         start=True, stop=True, tile_position=(pb, 0))
                            P.inc(m, "pe", 1)
                        gb_pe = P.cnt["pe"]

                        @blk.vector
                        def _(v, rc=rc, gb_pe=gb_pe):
                            P.wait(v, "v", "pe", gb_pe)
                            v.drain()
                            bs = 4 + (rc % 2)
                            bm = 6 + (rc % 2)
                            v.tensor_mul(f32v(O_A5, 512), bfv(O_B8 + rc * 256, 512),
                                         ps[:, bs * 512:(bs + 1) * 512])
                            v.drain()
                            P.inc(v.tensor_sub(f32v(O_A5 + 512, 512), f32v(O_A5, 512),
                                               ps[:, bm * 512:(bm + 1) * 512]), "v", 1)
                        ga_v = P.cnt["v"]

                        @blk.scalar
                        def _(sc, rc=rc, ga_v=ga_v):
                            P.wait(sc, "sc", "v", ga_v)
                            P.inc(sc.activation(f32v(O_A5, 512), f32v(O_A5 + 512, 512),
                                                AF.Identity, bias=lnwb[:, 1:2],
                                                scale=lnwb[:, 0:1]), "s", 1)
                        gf_s = P.cnt["s"]

                        @blk.vector
                        def _(v, rc=rc, gf_s=gf_s):
                            P.wait(v, "v", "s", gf_s)
                            P.inc(v.tensor_mul(bfv(O_B8 + rc * 256, 512),
                                               f32v(O_A5, 512),
                                               f32v(O_RF + rc * 512, 512)), "v", 1)
                og_v = P.cnt["v"]

                for g in range(NG):
                    @blk.tensor
                    def _(t, g=g, og_v=og_v, wo_q1=wo_q1):
                        P.wait(t, "pe", "v", og_v)
                        P.wait(t, "pe", "qwo", wo_q1)
                        if g >= 1:
                            P.wait(t, "pe", "v", P.cnt["v"])
                        for half in range(2):
                            b = half
                            for kc in range(8):
                                m = t.matmul(
                                    ps[:, b * 512:(b + 1) * 512],
                                    bass.AP(AR16.tensor,
                                            2 * O_B8 + kc * 512 + g * 128,
                                            [[PIT16, 128], [1, 128]]),
                                    bass.AP(AR16.tensor,
                                            2 * O_B1 + kc * 1024 + half * 512,
                                            [[PIT16, 128], [1, 512]]),
                                    start=(kc == 0), stop=(kc == 7))
                        P.inc(m, "pe", 1)
                    op_pe = P.cnt["pe"]

                    @blk.vector
                    def _(v, g=g, op_pe=op_pe):
                        P.wait(v, "v", "pe", op_pe)
                        P.inc(v.tensor_add(f32v(O_X + g * 1024, 1024),
                                           f32v(O_X + g * 1024, 1024),
                                           ps[:, 0:1024]), "v", 1)
                x1_v = P.cnt["v"]

                @blk.vector
                def _(v, x1_v=x1_v):
                    P.wait(v, "v", "v", x1_v)
                    for g in range(NG):
                        xg = f32v(O_X + g * 1024, 1024)
                        v.reduce_sum(smv(S_ST1 + g, 1), xg, axis=AX.X)
                        v.scalar_tensor_tensor(f32v(O_A5 + g * 1024, 1024), xg, 1.0, xg,
                                               ALU.mult, ALU.mult,
                                               accum_out=smv(S_ST2 + g, 1))
                    v.drain()
                    v.tensor_scalar(smv(S_LNB, NG), smv(S_ST1, NG), 1.0 / D, None,
                                    ALU.mult)
                    v.drain()
                    v.tensor_mul(smv(S_TINY, NG), smv(S_LNB, NG), smv(S_LNB, NG))
                    v.drain()
                    P.inc(v.scalar_tensor_tensor(smv(S_LNS, NG), smv(S_ST2, NG),
                                                 1.0 / D, smv(S_TINY, NG),
                                                 ALU.mult, ALU.subtract), "v", 1)
                l2st_v = P.cnt["v"]

                @blk.scalar
                def _(sc, l2st_v=l2st_v):
                    P.wait(sc, "sc", "v", l2st_v)
                    sc.activation(smv(S_TINY, NG), smv(S_LNS, NG), AF.Ln, bias=eps_c)
                    sc.drain()
                    P.inc(sc.activation(smv(S_LNS, NG), smv(S_TINY, NG), AF.Exp,
                                        scale=-0.5), "s", 1)

                @blk.vector
                def _(v):
                    P.wait(v, "v", "s")
                    P.inc(v.scalar_tensor_tensor(smv(S_LNB, NG), smv(S_LNB, NG), -1.0,
                                                 smv(S_LNS, NG), ALU.mult, ALU.mult),
                          "v", 1)
                l2ab_v = P.cnt["v"]

                tq_h = []
                for p in range(2):
                    ga, gb = 2 * p, 2 * p + 1

                    @blk.scalar
                    def _(sc, p=p, ga=ga, gb=gb, l2ab_v=l2ab_v):
                        P.wait(sc, "sc", "v", l2ab_v)
                        if p > 0:
                            P.wait(sc, "sc", "v", P.cnt["v"])
                        for i, g in enumerate((ga, gb)):
                            P.inc(sc.activation(f32v(O_A5 + i * 1024, 1024),
                                                f32v(O_X + g * 1024, 1024),
                                                AF.Identity,
                                                bias=smv(S_LNB + g, 1),
                                                scale=smv(S_LNS + g, 1)), "s", 1)
                    h_s = P.cnt["s"]

                    @blk.vector
                    def _(v, p=p, ga=ga, gb=gb, h_s=h_s):
                        P.wait(v, "v", "s", h_s)
                        for i, g in enumerate((ga, gb)):
                            v.reduce_max(smv(S_MAX + g, 1),
                                         f32v(O_A5 + i * 1024, 1024), axis=AX.X,
                                         apply_absolute_value=True)
                        v.drain()
                        v.tensor_scalar_max(smv(S_MAX + ga, 2), smv(S_MAX + ga, 2),
                                            1e-5)
                        v.drain()
                        v.reciprocal(smv(S_TINY + 4, 2), smv(S_MAX + ga, 2))
                        v.drain()
                        v.tensor_scalar(smv(S_TINY + 6, 2), smv(S_TINY + 4, 2),
                                        127.0, None, ALU.mult)
                        for g in (ga, gb):
                            v.tensor_mul(smv(S_DQ + g, 1), smv(S_MAX + g, 1),
                                         smv(S_TINY2, 1))
                            v.tensor_mul(smv(S_DQ2 + g, 1), smv(S_MAX + g, 1),
                                         smv(S_TINY2 + 1, 1))
                        v.drain()
                        P.inc(v.engine_nop(), "v", 1)
                    hm_v = P.cnt["v"]

                    @blk.scalar
                    def _(sc, p=p, ga=ga, gb=gb, hm_v=hm_v):
                        P.wait(sc, "sc", "v", hm_v)
                        P.wait(sc, "sc", "gp", P.cnt["gp"])
                        P.wait(sc, "sc", "pe")
                        for i in range(2):
                            P.inc(sc.activation(f32v(O_A4 + i * 1024, 1024),
                                                f32v(O_A5 + i * 1024, 1024),
                                                AF.Identity,
                                                scale=smv(S_TINY + 6 + i, 1),
                                                bias=cpool[:, 8:9]), "s", 1)
                    hv_s = P.cnt["s"]

                    @blk.vector
                    def _(v, p=p, ga=ga, gb=gb, hv_s=hv_s):
                        P.wait(v, "v", "s", hv_s)
                        Y0 = [f32v(O_A4, 1024), f32v(O_A4 + 1024, 1024)]
                        YF = [f32v(O_A5 + 2048, 1024), f32v(O_A5 + 3072, 1024)]
                        BF = [f32v(O_A4 + 2048, 1024), f32v(O_A4 + 3072, 1024)]
                        for i in range(2):
                            v.tensor_scalar(BF[i], Y0[i], 128.0, -128.0,
                                            ALU.is_ge, ALU.mult)
                        v.drain()
                        for i in range(2):
                            v.tensor_add(YF[i], Y0[i], BF[i])
                        for m in (64.0, 32.0, 16.0, 8.0, 4.0, 2.0):
                            v.drain()
                            for i in range(2):
                                v.tensor_scalar(BF[i], YF[i], m, -m, ALU.is_ge,
                                                ALU.mult)
                            v.drain()
                            for i in range(2):
                                v.tensor_add(YF[i], YF[i], BF[i])
                        v.drain()
                        for i in range(2):
                            v.scalar_tensor_tensor(BF[i], YF[i], 1.0, YF[i],
                                                   ALU.is_ge, ALU.subtract)
                        v.drain()
                        for i, g in enumerate((ga, gb)):
                            P.inc(v.scalar_tensor_tensor(
                                bfv(O_B4 + g * 512, 1024), Y0[i], -128.0, BF[i],
                                ALU.add, ALU.add), "v", 1)
                    lad_h = P.cnt["v"]

                    for g in (ga, gb):
                        @blk.sync
                        def _(sy, g=g, lad_h=lad_h):
                            P.wait(sy, "sy", "v", lad_h)
                            if g == 0:
                                P.wait(sy, "sy", "pe")
                            for dc in range(8):
                                P.inc(sy.dma_start_transpose(
                                    out=bass.AP(AR16.tensor,
                                                2 * O_B6 + dc * 512 + g * 128,
                                                [[PIT16, 128], [1, 128]]),
                                    in_=bass.AP(AR16.tensor,
                                                2 * O_B4 + g * 1024 + dc * 128,
                                                [[PIT16, 128], [1, 128]])),
                                    "qht%d" % g, 16)
                        tq_h.append(P.cnt["qht%d" % g])
                hq_v = P.cnt["v"]
                if not phase_on("F"):
                    continue

                win_of = [(O_B7, O_B8), (O_B3, O_B4)]
                for fc in range(8):
                    wgo, wuo = win_of[fc % 2]
                    qwin = "qwinA" if fc % 2 == 0 else "qwinB"

                    @blk.sync
                    def _(sy, fc=fc, wgo=wgo, wuo=wuo, qwin=qwin):
                        P.wait(sy, "sy", "q1", ffnw_q1)
                        if fc > 0:
                            P.wait(sy, "sy", "v")
                        P.wait(sy, "sy", "pe")
                        wload(sy, 2 * wgo, 512, wqtg_h, fc * 512, FF, 8, 512,
                              qwin)
                        wload(sy, 2 * wuo, 512, wqtu_h, fc * 512, FF, 8, 512,
                              qwin)
                    win_q1 = P.cnt[qwin]

                    for g in range(NG):
                        @blk.tensor
                        def _(t, fc=fc, g=g, wgo=wgo, wuo=wuo, win_q1=win_q1,
                              tqh=tq_h[g], qwin=qwin):
                            P.wait(t, "pe", qwin, win_q1)
                            P.wait(t, "pe", "qht%d" % g, tqh)
                            bg = g % 2
                            if fc + g >= 1:
                                P.wait(t, "pe", "v", P.cnt["v"])
                            for kc in range(8):
                                t.matmul(ps[:, bg * 512:(bg + 1) * 512],
                                         bass.AP(AR16.tensor,
                                                 2 * O_B6 + kc * 512 + g * 128,
                                                 [[PIT16, 128], [1, 128]]),
                                         bass.AP(AR16.tensor, 2 * wgo + kc * 512, [[PIT16, 128], [1, 512]]),
                                         start=(kc == 0), stop=(kc == 7))
                            for kc in range(8):
                                m = t.matmul(ps[:, (2 + bg) * 512:(3 + bg) * 512],
                                             bass.AP(AR16.tensor,
                                                     2 * O_B6 + kc * 512 + g * 128,
                                                     [[PIT16, 128], [1, 128]]),
                                             bass.AP(AR16.tensor, 2 * wuo + kc * 512, [[PIT16, 128], [1, 512]]),
                                             start=(kc == 0), stop=(kc == 7))
                            P.inc(m, "pe", 1)
                        gu_pe = P.cnt["pe"]

                        @blk.scalar
                        def _(sc, g=g, gu_pe=gu_pe):
                            P.wait(sc, "sc", "pe", gu_pe)
                            bg = g % 2
                            P.inc(sc.activation(f32v(O_TS, 512),
                                                ps[:, bg * 512:(bg + 1) * 512],
                                                AF.Silu, scale=smv(S_DQ + g, 1)),
                                  "s", 1)
                        si_s = P.cnt["s"]

                        @blk.vector
                        def _(v, fc=fc, g=g, si_s=si_s):
                            P.wait(v, "v", "s", si_s)
                            bg = g % 2
                            P.inc(v.scalar_tensor_tensor(
                                f32v(GBIG[g] + fc * 512, 512),
                                ps[:, (2 + bg) * 512:(3 + bg) * 512],
                                smv(S_DQ2 + g, 1), f32v(O_TS, 512),
                                ALU.mult, ALU.mult), "v", 1)
                gu_done_v = P.cnt["v"]

                @blk.sync
                def _(sy):
                    P.wait(sy, "sy", "pe")
                    for qr in range(NG):
                        wload(sy, 2 * WDQ[qr], 512, wqtd_h, qr * 1024 * D,
                              D, 8, 512, "qwd", pieces=1)
                wdw_h0 = P.cnt["qwd"]

                GT = [O_B6, O_B1, O_B2, O_B3]
                tq_g = []

                for g in range(NG):
                    @blk.vector
                    def _(v, g=g, gu_done_v=gu_done_v):
                        P.wait(v, "v", "v", gu_done_v)
                        v.reduce_max(smv(S_MAX + g, 1), f32v(GBIG[g], 4096), axis=AX.X,
                                     apply_absolute_value=True)
                        v.drain()
                        v.tensor_scalar_max(smv(S_MAX + g, 1), smv(S_MAX + g, 1),
                                            1e-5)
                        v.drain()
                        v.reciprocal(smv(S_TINY + 4, 1), smv(S_MAX + g, 1))
                        v.drain()
                        v.tensor_scalar(smv(S_TINY + 5, 1), smv(S_TINY + 4, 1),
                                        127.0, None, ALU.mult)
                        v.tensor_mul(smv(S_DQ + g, 1), smv(S_MAX + g, 1),
                                     smv(S_TINY2 + 2, 1))
                        v.drain()
                        P.inc(v.engine_nop(), "v", 1)
                    gm_v = P.cnt["v"]
                    gsc_s = P.cnt["s"]

                    for hf in range(2):
                        @blk.scalar
                        def _(sc, g=g, hf=hf, gm_v=gm_v):
                            if hf == 0:
                                P.wait(sc, "sc", "v", gm_v)
                                P.wait(sc, "sc", "gp", P.cnt["gp"])
                            sc.drain()
                            P.inc(sc.activation(f32v(GBIG[g] + hf * 2048, 2048),
                                                f32v(GBIG[g] + hf * 2048, 2048),
                                                AF.Identity, scale=smv(S_TINY + 5, 1),
                                                bias=cpool[:, 8:9]), "s", 1)
                    gv_s = P.cnt["s"]

                    @blk.vector
                    def _(v, g=g, gv_s=gv_s):
                        P.wait(v, "v", "s", gv_s)
                        Y0 = [f32v(GBIG[g], 2048), f32v(GBIG[g] + 2048, 2048)]
                        YF = [f32v(O_A5, 2048), f32v(O_A5 + 2048, 2048)]
                        BF = [f32v(O_A4, 2048), f32v(O_A4 + 2048, 2048)]
                        for h in range(2):
                            v.tensor_scalar(BF[h], Y0[h], 128.0, -128.0,
                                            ALU.is_ge, ALU.mult)
                        v.drain()
                        for h in range(2):
                            v.tensor_add(YF[h], Y0[h], BF[h])
                        for m in (64.0, 32.0, 16.0, 8.0, 4.0, 2.0):
                            v.drain()
                            for h in range(2):
                                v.tensor_scalar(BF[h], YF[h], m, -m, ALU.is_ge,
                                                ALU.mult)
                            v.drain()
                            for h in range(2):
                                v.tensor_add(YF[h], YF[h], BF[h])
                        v.drain()
                        for h in range(2):
                            v.scalar_tensor_tensor(BF[h], YF[h], 1.0, YF[h],
                                                   ALU.is_ge, ALU.subtract)
                        v.drain()
                        for h in range(2):
                            P.inc(v.scalar_tensor_tensor(
                                bass.AP(AR16.tensor, 2 * GQB[g] + h * 2048,
                                        [[PIT16, 128], [1, 2048]]), Y0[h],
                                -128.0, BF[h], ALU.add, ALU.add),
                                "v", 1)
                    lad_v = P.cnt["v"]

                    @blk.sync
                    def _(sy, g=g, lad_v=lad_v):
                        P.wait(sy, "sy", "v", lad_v)
                        if g == 0:
                            P.wait(sy, "sy", "pe")
                        else:
                            P.wait(sy, "sy", "qgqt", P.cnt["qgqt"])
                        for fcq in range(32):
                            P.inc(sy.dma_start_transpose(
                                out=bass.AP(AR16.tensor, 2 * GT[g] + fcq * 128,
                                            [[PIT16, 128], [1, 128]]),
                                in_=bass.AP(AR16.tensor, 2 * GQB[g] + fcq * 128,
                                            [[PIT16, 128], [1, 128]])), "qgqt", 16)
                    tq_g.append(P.cnt["qgqt"])

                for half in range(2):
                    if half == 0:
                        wdw_q1 = wdw_h0
                    else:
                        @blk.sync
                        def _(sy):
                            P.wait(sy, "sy", "pe")
                            for qr in range(2):
                                wload(sy, 2 * WDQ[qr], 512, wqtd_h,
                                      qr * 1024 * D + 512, D, 8, 512, "qwd")

                        @blk.scalar
                        def _(sc):
                            P.wait(sc, "sc", "pe")
                            for qr in range(2, NG):
                                wload(sc, 2 * WDQ[qr], 512, wqtd_h,
                                      qr * 1024 * D + 512, D, 8, 512, "qwd")
                        wdw_q1 = P.cnt["qwd"]

                    for g in range(NG):
                        @blk.tensor
                        def _(t, g=g, half=half, tq=tq_g[g], wdw_q1=wdw_q1):
                            P.wait(t, "pe", "qgqt", tq)
                            P.wait(t, "pe", "qwd", wdw_q1)
                            if g >= 2:
                                P.wait(t, "pe", "v", P.cnt["v"])
                            b = 4 + (g % 2) * 2 + half
                            for kc in range(32):
                                m = t.matmul(
                                    ps[:, b * 512:(b + 1) * 512],
                                    bass.AP(AR16.tensor, 2 * GT[g] + kc * 128,
                                            [[PIT16, 128], [1, 128]]),
                                    bass.AP(AR16.tensor,
                                            2 * WDQ[kc // 8] + (kc % 8) * 512,
                                            [[PIT16, 128], [1, 512]]),
                                    start=(kc == 0), stop=(kc == 31))
                            P.inc(m, "pe", 1)
                        dn_pe = P.cnt["pe"]

                        @blk.vector
                        def _(v, g=g, half=half, dn_pe=dn_pe):
                            P.wait(v, "v", "pe", dn_pe)
                            b = 4 + (g % 2) * 2 + half
                            P.inc(v.scalar_tensor_tensor(
                                f32v(O_X + g * 1024 + half * 512, 512),
                                ps[:, b * 512:(b + 1) * 512],
                                smv(S_DQ + g, 1),
                                f32v(O_X + g * 1024 + half * 512, 512),
                                ALU.mult, ALU.add), "v", 1)
                out_v = P.cnt["v"]
                if not phase_on("G"):
                    continue

                @blk.sync
                def _(sy, s_i=s_i, t_i=t_i, tok0=tok0, out_v=out_v):
                    P.wait(sy, "sy", "v", out_v)
                    dst = bass.AP(out_h.tensor, s_i * T * D + tok0 * D,
                                  [[D, 128], [128 * D, NG], [1, D]])
                    P.inc(sy.dma_start(out=dst, in_=f32d(O_X, [[1024, NG], [1, 1024]])),
                          "st", 16)

        if dbg_h:
            @blk.sync
            def _(sy):
                P.wait(sy, "sy", "v")
                P.wait(sy, "sy", "s")
                P.wait(sy, "sy", "gp")
                P.wait(sy, "sy", "pe")
                for n, ap in dbg_h.items():
                    kind, off = n.split("_", 1)
                    off = int(off)
                    cols = ap.ap[-1][1] if len(ap.ap) >= 2 else 1
                    nrow = ap.ap[0][1]
                    if kind == "f":
                        src = f32v(off, cols, 0, nrow)
                    elif kind == "b":
                        src = bfv(off, cols, 0, nrow)
                    else:
                        src = i32v(off, cols, 0, nrow)
                    P.inc(sy.dma_start(out=ap, in_=src), "st", 16)

        @blk.sync
        def _(sy):
            P.wait(sy, "sy", "st")

    return nc


def _prep_inputs(ip):
    x = np.asarray(ip["L0_attn_x"], np.float32)
    u = np.asarray(ip["L0_attn_u"], np.float32)
    u_p = np.zeros((128, 8), np.float32)
    for rc in range(8):
        u_p[0:64, rc] = u[2 * rc]
        u_p[64:128, rc] = u[2 * rc + 1]
    tm = np.zeros((128, 32), np.float32)
    for m, nm in enumerate(["tm_r", "tm_k", "tm_v", "tm_w"]):
        t = np.asarray(ip["L0_attn_" + nm], np.float32)
        for dc in range(8):
            tm[:, m * 8 + dc] = t[dc * 128:(dc + 1) * 128]
    lnwb = np.zeros((128, 2), np.float32)
    lnwb[:, 0] = np.tile(np.asarray(ip["L0_attn_ln_w"], np.float32), 2)
    lnwb[:, 1] = np.tile(np.asarray(ip["L0_attn_ln_b"], np.float32), 2)
    pp = np.arange(128) % 64
    cc = np.arange(64)
    mstrict = (cc[None, :] > pp[:, None]).astype(np.float32)
    mdiag = (cc[None, :] == pp[:, None]).astype(np.uint8)
    mstrf = np.tile(mstrict, (1, 16)).astype(ml_dtypes.bfloat16)
    mdiaf = np.tile(mdiag, (1, 16)).astype(np.uint8)
    bdA = np.zeros((128, 64), ml_dtypes.bfloat16)
    bdA[0:64, :] = 1
    bdB = np.zeros((128, 64), ml_dtypes.bfloat16)
    bdB[64:128, :] = 1
    sel2 = np.zeros((128, 128), ml_dtypes.bfloat16)
    for qq in range(4):
        sel2[32 * qq, 0:64] = 1
        sel2[32 * qq + 1, 64:128] = 1
    sel2f = sel2.astype(np.float32)
    hsel = np.zeros((128, 2), ml_dtypes.bfloat16)
    hsel[0:64, 0] = 1
    hsel[64:128, 1] = 1
    onesf = np.ones((128, 128), np.float32)
    cpool = np.zeros((128, 10), np.float32)
    cpool[:, 8] = 128.5
    cpool[:, 0] = 0.5
    cpool[:, 1] = -0.5
    cpool[:, 2] = 1e-5
    cpool[:, 3] = -LN127
    cpool[:, 4] = 1.0
    cpool[:, 5] = 0.0
    cpool[:, 6] = LN127
    cpool[:, 7] = -1.0
    base = {
        "wr": np.asarray(ip["L0_attn_W_r"], np.float32),
        "wk": np.asarray(ip["L0_attn_W_k"], np.float32),
        "wv": np.asarray(ip["L0_attn_W_v"], np.float32),
        "ww": np.asarray(ip["L0_attn_W_w"], np.float32),
        "wo": np.asarray(ip["L0_attn_W_o"], np.float32),
        "wg": np.asarray(ip["L0_ffn_w_gate"], np.float32),
        "wu": np.asarray(ip["L0_ffn_w_up"], np.float32),
        "wd": np.asarray(ip["L0_ffn_w_down"], np.float32),
        "u_p": u_p, "tm": tm, "lnwb": lnwb, "mstrf": mstrf, "mdiaf": mdiaf,
        "bdA": bdA, "bdB": bdB, "sel2f": sel2f, "hsel": hsel,
        "onesf": onesf, "cpool": cpool,
    }
    return [dict(base, x=np.ascontiguousarray(x[2 * c:2 * c + 2])) for c in range(8)]


def kernel(**inputs):
    in_maps = _prep_inputs(inputs)
    nc = build()
    res = run_bass_kernel_spmd(nc, in_maps, core_ids=list(range(8)))
    return np.concatenate([r["out"] for r in res.results], axis=0).astype(np.float32)


# version 17
# speedup vs baseline: 1.0141x; 1.0141x over previous
import contextlib
import numpy as np
import ml_dtypes

import concourse.bass as bass
import concourse.mybir as mybir
from concourse.bass_utils import run_bass_kernel_spmd

F32 = mybir.dt.float32
BF16 = mybir.dt.bfloat16
I32 = mybir.dt.int32
U8 = mybir.dt.uint8
AF = mybir.ActivationFunctionType
ALU = mybir.AluOpType
AX = mybir.AxisListType

BQ, T, D, H, DH = 2, 2048, 1024, 16, 64
NRC = 8
TILE, NTILE = 512, 4
NG, NCH = 4, 8
FF = 4096
LN127 = float(np.log(np.float64(127.0)))

O_X = 0
O_RF = 4096
O_KF = 8192
O_A4 = 12288
O_A5 = 16448
O_TS = 20544
O_QI = 20544
O_SM = 14336
O_ESC = 21056
O_SMALL = 21568
O_GB2 = 21952
ARNF_COLS = 30144
O_XLNF = 0
O_SF = 0
O_B1 = 2112
O_B2 = 4160
O_B3 = 6208
O_B4 = 8256
O_B5 = 10304
O_B6 = 12352
O_B7 = 14400
O_B8 = 16448
O_XSV = 18496
O_WWIN = O_B5
ARNB_COLS = 37008
S_TINY, S_TINY2, S_ST1, S_ST2, S_LNS, S_LNB = 0, 8, 16, 24, 32, 40
S_GAM, S_EGP, S_E2C, S_D1 = 48, 112, 176, 240
S_MAX, S_DQ, S_DQ2, S_XSV, S_ACC = 304, 312, 320, 328, 336
S_RCC = 370

GBIG = [O_RF, O_KF, O_GB2, O_GB2 + 4096]
GQB = [O_B1, O_B2, O_B3, O_B4]
WDQ = [O_B5, O_B7, O_B8, O_XLNF]


class Prog:
    def __init__(self, nc, es):
        self.nc, self.es = nc, es
        self.sem, self.cnt, self.hwm = {}, {}, {}

    def new(self, name):
        self.sem[name] = self.es.enter_context(self.nc.semaphore(name))
        self.cnt[name] = 0

    def inc(self, instr, name, k):
        self.cnt[name] += k
        instr.then_inc(self.sem[name], k)
        return self.cnt[name]

    def wait(self, eng, ekey, name, val=None):
        if val is None:
            val = self.cnt[name]
        assert val <= self.cnt[name], f"forward wait {name}:{val}>{self.cnt[name]}"
        if val <= 0:
            return
        if self.hwm.get((ekey, name), -1) >= val:
            return
        eng.wait_ge(self.sem[name], val)
        self.hwm[(ekey, name)] = val


def build(n_seq=BQ, n_tiles=NTILE, last_phase="G", dbg=()):
    nc = bass.Bass("TRN2", target_bir_lowering=False, debug=False)
    dt = nc.dram_tensor
    x_h = dt("x", [BQ, T, D], F32, kind="ExternalInput").ap()
    wr_h = dt("wr", [D, D], F32, kind="ExternalInput").ap()
    wk_h = dt("wk", [D, D], F32, kind="ExternalInput").ap()
    wv_h = dt("wv", [D, D], F32, kind="ExternalInput").ap()
    ww_h = dt("ww", [D, D], F32, kind="ExternalInput").ap()
    wo_h = dt("wo", [D, D], F32, kind="ExternalInput").ap()
    wg_h = dt("wg", [FF, D], F32, kind="ExternalInput").ap()
    wu_h = dt("wu", [FF, D], F32, kind="ExternalInput").ap()
    wd_h = dt("wd", [D, FF], F32, kind="ExternalInput").ap()
    u_h = dt("u_p", [128, 8], F32, kind="ExternalInput").ap()
    tm_h = dt("tm", [128, 32], F32, kind="ExternalInput").ap()
    lnwb_h = dt("lnwb", [128, 2], F32, kind="ExternalInput").ap()
    mstrf_h = dt("mstrf", [128, 1024], BF16, kind="ExternalInput").ap()
    mdiaf_h = dt("mdiaf", [128, 1024], U8, kind="ExternalInput").ap()
    bdA_h = dt("bdA", [128, 64], BF16, kind="ExternalInput").ap()
    bdB_h = dt("bdB", [128, 64], BF16, kind="ExternalInput").ap()
    sel2f_h = dt("sel2f", [128, 128], F32, kind="ExternalInput").ap()
    hsel_h = dt("hsel", [128, 2], BF16, kind="ExternalInput").ap()
    onesf_h = dt("onesf", [128, 128], F32, kind="ExternalInput").ap()
    cpool_h = dt("cpool", [128, 10], F32, kind="ExternalInput").ap()
    out_h = dt("out", [BQ, T, D], F32, kind="ExternalOutput").ap()
    wtr_h = dt("wtr", [D, D], BF16, kind="Internal").ap()
    wtk_h = dt("wtk", [D, D], BF16, kind="Internal").ap()
    wtv_h = dt("wtv", [D, D], BF16, kind="Internal").ap()
    wtw_h = dt("wtw", [D, D], BF16, kind="Internal").ap()
    wto_h = dt("wto", [D, D], BF16, kind="Internal").ap()
    wqtg_h = dt("wqtg", [D, FF], BF16, kind="Internal").ap()
    wqtu_h = dt("wqtu", [D, FF], BF16, kind="Internal").ap()
    wqtd_h = dt("wqtd", [FF, D], BF16, kind="Internal").ap()
    dbg_h = {n: dt("dbg_" + n, sh, dy, kind="ExternalOutput").ap() for n, sh, dy in dbg}

    es = contextlib.ExitStack()
    with es:
        sbt = lambda n, s, d: es.enter_context(nc.sbuf_tensor(n, s, d))
        arn = sbt("arn", [128, ARNF_COLS], F32)
        arnb = sbt("arnb", [128, ARNB_COLS], BF16)
        u_sb = sbt("u_sb", [128, 8], F32)
        tm_sb = sbt("tm_sb", [128, 32], F32)
        lnwb = sbt("lnwb_s", [128, 2], F32)
        mstrf = sbt("mstrf_s", [128, 1024], BF16)
        mdiaf = sbt("mdiaf_s", [128, 1024], U8)
        bdA = sbt("bdA_s", [128, 64], BF16)
        bdB = sbt("bdB_s", [128, 64], BF16)
        sel2f = sbt("sel2f_s", [128, 128], F32)
        hsel = sbt("hsel_s", [128, 2], BF16)
        onesf = sbt("onesf_s", [128, 128], F32)
        cpool = sbt("cpool_s", [128, 10], F32)
        state = sbt("state", [128, 512], BF16)
        thr2 = sbt("thr2", [128, 4], F32)
        lnmw = sbt("lnmw", [128, 4], F32)
        ps = es.enter_context(nc.psum_tensor("ps", [128, 4096], F32))

        AR = arn.ap()
        AR16 = arnb.ap()
        ARI = arn.bitcast(I32).ap()
        PIT = ARNF_COLS
        PIT16 = ARNB_COLS

        def f32v(off, cols, p0=0, pn=128):
            return bass.AP(AR.tensor, p0 * PIT + off, [[PIT, pn], [1, cols]])

        def f32d(off, dims, p0=0, pn=128):
            return bass.AP(AR.tensor, p0 * PIT + off, [[PIT, pn]] + dims)

        def bfv(off, cols, p0=0, pn=128):
            return bass.AP(AR16.tensor, p0 * PIT16 + 2 * off, [[PIT16, pn], [1, cols]])

        def bfd(off, dims, p0=0, pn=128):
            return bass.AP(AR16.tensor, p0 * PIT16 + 2 * off, [[PIT16, pn]] + dims)

        def i32v(off, cols, p0=0, pn=128):
            return bass.AP(ARI.tensor, p0 * PIT + off, [[PIT, pn], [1, cols]])

        def smv(col, n, p0=0, pn=128):
            return f32v(O_SMALL + col, n, p0, pn)

        def xsv_ap(n=8):
            return bass.AP(AR16.tensor, 2 * O_XSV, [[PIT16, 128], [1, n]])

        for i, v in enumerate([0.5, -0.5, 1e-5, -LN127, 1.0, 0.0, LN127, -1.0]):
            nc.const_aps.register(F32, float(v), cpool[:, i:i + 1])

        blk = es.enter_context(nc.Block())
        P = Prog(nc, es)
        for s in ("q1", "q2", "pe", "v", "s", "gp", "st", "qx", "qxl", "qwv",
                  "qww", "qwo", "qkht", "qhqt", "qwinA", "qwinB", "qwd", "qgqt",
                  "qldA", "qldB", "qc", "qat", "qwk", "qwr",
                  "qht0", "qht1", "qht2", "qht3"):
            P.new(s)

        eps_c = cpool[:, 2:3]
        one_c = cpool[:, 4:5]
        ln127_c = cpool[:, 6:7]
        neg1_c = cpool[:, 7:8]

        def wload(sy, dst_off, dst_bs, src_h, src_off, pstride, nblk, inner,
                  sem, pieces=1):
            per = max(1, nblk // pieces)
            for s in range(0, nblk, per):
                n = min(per, nblk - s)
                P.inc(sy.dma_start(
                    out=bass.AP(AR16.tensor, dst_off + s * dst_bs,
                                [[PIT16, 128], [dst_bs, n], [1, inner]]),
                    in_=bass.AP(src_h.tensor, src_off + s * 128 * pstride,
                                [[pstride, 128], [128 * pstride, n],
                                 [1, inner]])), sem, 16)

        @blk.sync
        def _(sy):
            for dst, src in ((u_sb, u_h), (tm_sb, tm_h), (lnwb, lnwb_h),
                             (mstrf, mstrf_h), (mdiaf, mdiaf_h), (bdA, bdA_h),
                             (bdB, bdB_h), (sel2f, sel2f_h), (hsel, hsel_h),
                             (onesf, onesf_h), (cpool, cpool_h)):
                P.inc(sy.dma_start(out=dst[:], in_=src), "qc", 16)
        consts_q1 = P.cnt["qc"]
        for eng, key in ((blk.vector, "v"), (blk.scalar, "sc"),
                         (blk.gpsimd, "gp"), (blk.tensor, "pe")):
            @eng
            def _(e, key=key):
                P.wait(e, key, "qc", consts_q1)

        @blk.vector
        def _(v):
            v.tensor_scalar(smv(S_RCC + 1, 1), cpool[:, 4:5], 127.0, None,
                            ALU.mult)
            v.drain()
            v.reciprocal(smv(S_RCC, 1), smv(S_RCC + 1, 1))
            v.drain()
            P.inc(v.engine_nop(), "v", 1)

        att_list = [(wr_h, None, wtr_h), (wk_h, None, wtk_h),
                    (wv_h, None, wtv_h), (ww_h, None, wtw_h), (wo_h, None, wto_h)]
        for i, (src_h, res_t, hbm_t) in enumerate(att_list):
            @blk.gpsimd
            def _(gp, i=i, src_h=src_h):
                if i > 0:
                    P.wait(gp, "gp", "qat")
                for oc in range(8):
                    P.inc(gp.dma_start(
                        out=bass.AP(AR16.tensor, 2 * O_B1 + oc * 1024,
                                    [[PIT16, 128], [1, 1024]]),
                        in_=src_h[oc * 128:(oc + 1) * 128, :]), "q2", 16)

            @blk.sync
            def _(sy, i=i, res_t=res_t, hbm_t=hbm_t):
                P.wait(sy, "sy", "q2")
                if True:
                    for oc in range(8):
                        P.wait(sy, "sy", "qat")
                        for dc in range(8):
                            P.inc(sy.dma_start_transpose(
                                out=bass.AP(AR16.tensor, 2 * O_B7 + dc * 128,
                                            [[PIT16, 128], [1, 128]]),
                                in_=bass.AP(AR16.tensor, 2 * O_B1 + oc * 1024 + dc * 128,
                                            [[PIT16, 128], [1, 128]])), "qat", 16)
                        P.wait(sy, "sy", "qat")
                        src = bass.AP(AR16.tensor, 2 * O_B7,
                                      [[PIT16, 128], [128, 8], [1, 128]])
                        dst = bass.AP(hbm_t.tensor, oc * 128,
                                      [[D, 128], [128 * D, 8], [1, 128]])
                        P.inc(sy.dma_start(out=dst, in_=src), "qat", 16)
        attw_q1 = P.cnt["qat"]

        wstg = [f32v(O_A4, 4096), f32v(O_A5, 4096)]
        absout = f32v(O_X, 4096)
        sqout = f32v(O_RF, 4096)
        sgout = bfv(O_B5, 4096)
        wqnat = bfv(O_B6, 4096)
        accw = smv(S_ACC, 32)
        ffn_specs = [(wg_h, wqtg_h, 32, 1024, FF), (wu_h, wqtu_h, 32, 1024, FF),
                     (wd_h, wqtd_h, 8, 4096, D)]
        for j, (src_h, dst_h, nchunk, ccols, wid) in enumerate(ffn_specs):
            for ch in range(nchunk):
                stg = wstg[ch % 2]

                qld = "qldA" if ch % 2 == 0 else "qldB"

                @blk.sync
                def _(sy, ch=ch, stg=stg, src_h=src_h, ccols=ccols, qld=qld):
                    if ch >= 2 or j > 0:
                        P.wait(sy, "sy", "v")
                        P.wait(sy, "sy", "s")
                    qc_ = ccols // 4
                    for q in range(4):
                        P.inc(sy.dma_start(
                            out=bass.AP(stg.tensor, stg.offset + q * qc_,
                                        [[PIT, 128], [1, qc_]]),
                            in_=src_h[ch * 128:(ch + 1) * 128,
                                      q * qc_:(q + 1) * qc_]), qld, 16)
                ld = P.cnt[qld]

                @blk.vector
                def _(v, ch=ch, stg=stg, ccols=ccols, ld=ld, qld=qld):
                    P.wait(v, "v", qld, ld)
                    stgv = bass.AP(stg.tensor, stg.offset, [[PIT, 128], [1, ccols]])
                    P.inc(v.scalar_tensor_tensor(stgv, stgv, -1.0, stgv,
                                                 ALU.mult, ALU.max,
                                                 accum_out=smv(S_ACC + ch, 1)),
                          "v", 1)

            @blk.vector
            def _(v, j=j, nchunk=nchunk):
                v.drain()
                P.inc(v.reduce_sum(smv(S_TINY, 1), smv(S_ACC, nchunk), axis=AX.X), "v", 1)

            @blk.tensor
            def _(t, j=j):
                P.wait(t, "pe", "v")
                P.wait(t, "pe", "qc", consts_q1)
                P.inc(t.matmul(ps[0:1, 0:1], smv(S_TINY, 1), cpool[:, 4:5],
                               start=True, stop=True), "pe", 1)

            @blk.vector
            def _(v, j=j):
                P.wait(v, "v", "pe")
                v.tensor_scalar(smv(S_TINY, 1, 0, 1), ps[0:1, 0:1],
                                1.0 / (FF * D), 1e-5, ALU.mult, ALU.max)
                v.drain()
                v.tensor_scalar(smv(S_TINY + 1, 1, 0, 1), smv(S_TINY, 1, 0, 1),
                                0.5, None, ALU.mult)
                v.drain()
                v.tensor_mul(smv(S_TINY + 2, 1, 0, 1), smv(S_TINY + 1, 1, 0, 1),
                             smv(S_TINY + 1, 1, 0, 1))
                P.inc(v.tensor_mul(smv(S_TINY + 3, 1, 0, 1), smv(S_TINY, 1, 0, 1),
                                   smv(S_RCC, 1, 0, 1)), "v", 1)

            @blk.tensor
            def _(t, j=j):
                P.wait(t, "pe", "s")
                P.wait(t, "pe", "v")
                P.inc(t.matmul(ps[:, 512:514], onesf[0:1, :], smv(S_TINY + 2, 2, 0, 1),
                               start=True, stop=True), "pe", 1)

            @blk.vector
            def _(v, j=j):
                P.wait(v, "v", "pe")
                v.tensor_copy(thr2[:, j:j + 1], ps[:, 512:513])
                v.tensor_copy(lnmw[:, j:j + 1], ps[:, 513:514])
                v.drain()
                P.inc(v.engine_nop(), "v", 1)
            thr_v = P.cnt["v"]

            npiece = ccols // 128
            for ch in range(nchunk):
                stg = wstg[ch % 2]

                qld = "qldA" if ch % 2 == 0 else "qldB"

                @blk.sync
                def _(sy, ch=ch, stg=stg, src_h=src_h, ccols=ccols, qld=qld):
                    P.wait(sy, "sy", "s")
                    qc_ = ccols // 4
                    for q in range(4):
                        P.inc(sy.dma_start(
                            out=bass.AP(stg.tensor, stg.offset + q * qc_,
                                        [[PIT, 128], [1, qc_]]),
                            in_=src_h[ch * 128:(ch + 1) * 128,
                                      q * qc_:(q + 1) * qc_]), qld, 16)
                ld = P.cnt[qld]

                @blk.scalar
                def _(sc, ch=ch, stg=stg, ccols=ccols, ld=ld, thr_v=thr_v, qld=qld):
                    P.wait(sc, "sc", qld, ld)
                    P.wait(sc, "sc", "v", thr_v)
                    if ch >= 1:
                        P.wait(sc, "sc", "v")
                    sc.activation(bass.AP(sqout.tensor, sqout.offset,
                                          [[PIT, 128], [1, ccols]]),
                                  bass.AP(stg.tensor, stg.offset,
                                          [[PIT, 128], [1, ccols]]), AF.Square)
                    P.inc(sc.activation(bass.AP(sgout.tensor, sgout.offset,
                                                [[PIT16, 128], [1, ccols]]),
                                        bass.AP(stg.tensor, stg.offset,
                                                [[PIT, 128], [1, ccols]]), AF.Sign),
                          "s", 1)
                sq_s = P.cnt["s"]

                @blk.vector
                def _(v, ch=ch, j=j, ccols=ccols, sq_s=sq_s, thr_v=thr_v):
                    P.wait(v, "v", "s", sq_s)
                    P.wait(v, "v", "v", thr_v)
                    P.wait(v, "v", "q1")
                    P.inc(v.scalar_tensor_tensor(
                        bass.AP(wqnat.tensor, wqnat.offset, [[PIT16, 128], [1, ccols]]),
                        bass.AP(sqout.tensor, sqout.offset, [[PIT, 128], [1, ccols]]),
                        thr2[:, j:j + 1],
                        bass.AP(sgout.tensor, sgout.offset, [[PIT16, 128], [1, ccols]]),
                        ALU.is_gt, ALU.mult), "v", 1)
                wq_v = P.cnt["v"]

                @blk.sync
                def _(sy, ch=ch, j=j, npiece=npiece, wid=wid, dst_h=dst_h, wq_v=wq_v):
                    P.wait(sy, "sy", "v", wq_v)
                    P.wait(sy, "sy", "qat")
                    P.wait(sy, "sy", "q1")
                    for pc in range(npiece):
                        P.inc(sy.dma_start_transpose(
                            out=bass.AP(AR16.tensor, 2 * O_B7 + pc * 128,
                                        [[PIT16, 128], [1, 128]]),
                            in_=bass.AP(AR16.tensor, 2 * O_B6 + pc * 128,
                                        [[PIT16, 128], [1, 128]])), "q1", 16)
                    P.wait(sy, "sy", "q1")
                    src = bass.AP(AR16.tensor, 2 * O_B7,
                                  [[PIT16, 128], [128, npiece], [1, 128]])
                    dst = bass.AP(dst_h.tensor, ch * 128,
                                  [[wid, 128], [128 * wid, npiece], [1, 128]])
                    P.inc(sy.dma_start(out=dst, in_=src), "q1", 16)
        ffnw_q1 = P.cnt["q1"]

        @blk.vector
        def _(v):
            P.wait(v, "v", "pe")
            P.inc(v.tensor_copy(smv(S_TINY2, 3), lnmw[:, 0:3]), "v", 1)
        tiny2_v = P.cnt["v"]

        PH = ["A", "B", "C", "D", "E", "F", "G"]
        phlim = PH.index(last_phase)

        def phase_on(p):
            return PH.index(p) <= phlim


        for s_i in range(n_seq):
            @blk.vector
            def _(v, s_i=s_i):
                if s_i > 0:
                    P.wait(v, "v", "pe")
                P.inc(v.memset(state[:], 0.0), "v", 1)

            for t_i in range(n_tiles):
                tok0 = t_i * TILE
                @blk.sync
                def _(sy, s_i=s_i, t_i=t_i, tok0=tok0):
                    P.wait(sy, "sy", "v")
                    P.wait(sy, "sy", "st")
                    src = bass.AP(x_h.tensor, s_i * T * D + tok0 * D,
                                  [[D, 128], [128 * D, NG], [1, D]])
                    P.inc(sy.dma_start(out=f32d(O_X, [[1024, NG], [1, 1024]]), in_=src),
                          "qx", 16)
                xload = P.cnt["qx"]

                @blk.vector
                def _(v, xload=xload):
                    P.wait(v, "v", "qx", xload)
                    P.wait(v, "v", "v", tiny2_v)
                    for g in range(NG):
                        xg = f32v(O_X + g * 1024, 1024)
                        v.reduce_sum(smv(S_ST1 + g, 1), xg, axis=AX.X)
                        v.scalar_tensor_tensor(f32v(O_A5 + g * 1024, 1024), xg, 1.0, xg,
                                               ALU.mult, ALU.mult,
                                               accum_out=smv(S_ST2 + g, 1))
                    v.drain()
                    v.tensor_scalar(smv(S_LNB, NG), smv(S_ST1, NG), 1.0 / D, None,
                                    ALU.mult)
                    v.drain()
                    v.tensor_mul(smv(S_TINY, NG), smv(S_LNB, NG), smv(S_LNB, NG))
                    v.drain()
                    P.inc(v.scalar_tensor_tensor(smv(S_LNS, NG), smv(S_ST2, NG),
                                                 1.0 / D, smv(S_TINY, NG),
                                                 ALU.mult, ALU.subtract), "v", 1)
                lnst_v = P.cnt["v"]

                @blk.scalar
                def _(sc, lnst_v=lnst_v):
                    P.wait(sc, "sc", "v", lnst_v)
                    sc.activation(smv(S_TINY, NG), smv(S_LNS, NG), AF.Ln, bias=eps_c)
                    sc.drain()
                    P.inc(sc.activation(smv(S_LNS, NG), smv(S_TINY, NG), AF.Exp,
                                        scale=-0.5), "s", 1)

                @blk.vector
                def _(v):
                    P.wait(v, "v", "s")
                    P.inc(v.scalar_tensor_tensor(smv(S_LNB, NG), smv(S_LNB, NG), -1.0,
                                                 smv(S_LNS, NG), ALU.mult, ALU.mult),
                          "v", 1)
                lnab_v = P.cnt["v"]

                @blk.scalar
                def _(sc, lnab_v=lnab_v):
                    P.wait(sc, "sc", "v", lnab_v)
                    P.wait(sc, "sc", "pe")
                    for g in range(NG):
                        P.inc(sc.activation(bfv(O_B1 + g * 512, 1024),
                                            f32v(O_X + g * 1024, 1024), AF.Identity,
                                            bias=smv(S_LNB + g, 1),
                                            scale=smv(S_LNS + g, 1)), "s", 1)
                lnap_s = P.cnt["s"]

                @blk.vector
                def _(v, t_i=t_i):
                    bnd = bass.AP(AR16.tensor, 2 * O_XLNF + 15,
                                  [[PIT16, 128], [528, 8], [1, 1]])
                    if t_i == 0:
                        P.inc(v.memset(bnd, 0.0), "v", 1)
                    else:
                        P.inc(v.tensor_copy(bnd, xsv_ap()), "v", 1)
                bnd_v = P.cnt["v"]

                @blk.sync
                def _(sy, lnap_s=lnap_s, bnd_v=bnd_v):
                    P.wait(sy, "sy", "s", lnap_s)
                    P.wait(sy, "sy", "v", bnd_v)
                    P.wait(sy, "sy", "gp")
                    for g in range(NG):
                        for dc in range(8):
                            P.inc(sy.dma_start_transpose(
                                out=bass.AP(AR16.tensor,
                                            2 * O_XLNF + dc * 528 + 16 + g * 128,
                                            [[PIT16, 128], [1, 128]]),
                                in_=bass.AP(AR16.tensor,
                                            2 * O_B1 + g * 1024 + dc * 128,
                                            [[PIT16, 128], [1, 128]])), "qxl", 16)
                    P.wait(sy, "sy", "pe")
                    P.wait(sy, "sy", "q1", ffnw_q1)
                    wload(sy, 2 * O_WWIN, 1024, wtr_h, 0, D, 8, 1024, "qwr")
                    P.wait(sy, "sy", "qxl")
                    P.wait(sy, "sy", "qgqt")
                    wload(sy, 2 * O_B1, 1024, wtk_h, 0, D, 4, 1024, "qwk")
                    wload(sy, 2 * O_B8, 1024, wtk_h, 512 * D, D, 4, 1024, "qwk")
                xlnf_q1 = P.cnt["qxl"]
                wr_q1 = P.cnt["qwr"]
                wk_q1 = P.cnt["qwk"]
                if not phase_on("B"):
                    continue

                @blk.vector
                def _(v, xlnf_q1=xlnf_q1):
                    P.wait(v, "v", "qxl", xlnf_q1)
                    last = bass.AP(AR16.tensor, 2 * O_XLNF + 527,
                                   [[PIT16, 128], [528, 8], [1, 1]])
                    v.tensor_copy(xsv_ap(), last)
                    cur = bass.AP(AR16.tensor, 2 * O_XLNF + 16,
                                  [[PIT16, 128], [528, 8], [1, 512]])
                    prv = bass.AP(AR16.tensor, 2 * O_XLNF + 15,
                                  [[PIT16, 128], [528, 8], [1, 512]])
                    P.inc(v.tensor_sub(bfd(O_B4, [[512, 8], [1, 512]]), cur, prv),
                          "v", 1)
                dlt_v = P.cnt["v"]

                def emit_mix(eng, dst_off, mcol, is_gp=False):
                    for dc in range(8):
                        xprev = bass.AP(AR16.tensor, 2 * O_XLNF + dc * 528 + 15,
                                        [[PIT16, 128], [1, 512]])
                        if is_gp:
                            tmp = f32v(O_ESC, 512)
                            eng.tensor_scalar(tmp, bfv(O_B4 + dc * 256, 512),
                                              tm_sb[:, mcol * 8 + dc: mcol * 8 + dc + 1],
                                              None, ALU.mult)
                            ins = eng.tensor_add(bfv(dst_off + dc * 256, 512),
                                                 tmp, xprev)
                        else:
                            ins = eng.scalar_tensor_tensor(
                                bfv(dst_off + dc * 256, 512),
                                bfv(O_B4 + dc * 256, 512),
                                tm_sb[:, mcol * 8 + dc: mcol * 8 + dc + 1],
                                xprev, ALU.mult, ALU.add)
                    return ins

                @blk.vector
                def _(v, dlt_v=dlt_v):
                    v.drain()
                    P.inc(emit_mix(v, O_B2, 0), "v", 1)
                mixr_v = P.cnt["v"]

                @blk.vector
                def _(v):
                    P.inc(emit_mix(v, O_B3, 1), "v", 1)
                mixk_gp = P.cnt["v"]

                def emit_proj(wsrc, mix_off, msem, mval, evac_fn, extra_pe_wait=None):
                    s_entry = P.cnt["s"]
                    for oc in range(8):
                        b = oc % 2

                        @blk.tensor
                        def _(t, oc=oc, b=b, wsrc=wsrc, mix_off=mix_off,
                              msem=msem, mval=mval, extra_pe_wait=extra_pe_wait,
                              s_entry=s_entry):
                            P.wait(t, "pe", msem, mval)
                            P.wait(t, "pe", "qat", attw_q1)
                            if extra_pe_wait is not None:
                                P.wait(t, "pe", *extra_pe_wait)
                            if oc < 2:
                                P.wait(t, "pe", "s", s_entry)
                            else:
                                P.wait(t, "pe", "s", P.cnt["s"] - 1)
                            for kc in range(8):
                                m = t.matmul(
                                    ps[:, b * 512:(b + 1) * 512],
                                    wsrc(kc, oc),
                                    bfv(mix_off + kc * 256, 512),
                                    start=(kc == 0), stop=(kc == 7))
                            P.inc(m, "pe", 1)
                        pm = P.cnt["pe"]

                        @blk.scalar
                        def _(sc, oc=oc, b=b, pm=pm):
                            P.wait(sc, "sc", "pe", pm)
                            P.inc(evac_fn(sc, oc, ps[:, b * 512:(b + 1) * 512]), "s", 1)
                    return P.cnt["s"]

                wsrc_win = lambda kc, oc: bass.AP(AR16.tensor,
                                                  2 * O_WWIN + kc * 1024 + oc * 128,
                                                  [[PIT16, 128], [1, 128]])
                wsrc_win2 = lambda kc, oc: bass.AP(
                    AR16.tensor,
                    2 * (O_B1 if kc < 4 else O_B8) + (kc % 4) * 1024 + oc * 128,
                    [[PIT16, 128], [1, 128]])

                r_done = emit_proj(wsrc_win, O_B2, "v", mixr_v,
                                   lambda sc, oc, pb: sc.activation(
                                       f32v(O_RF + oc * 512, 512), pb, AF.Sigmoid),
                                   extra_pe_wait=("qwr", wr_q1))

                @blk.sync
                def _(sy):
                    P.wait(sy, "sy", "pe")
                    wload(sy, 2 * O_WWIN, 1024, wtv_h, 0, D, 8, 1024, "qwv")
                wv_q1 = P.cnt["qwv"]
                k_done = emit_proj(wsrc_win2, O_B3, "v", mixk_gp,
                                   lambda sc, oc, pb: sc.activation(
                                       f32v(O_KF + oc * 512, 512), pb, AF.Identity),
                                   extra_pe_wait=("qwk", wk_q1))

                @blk.sync
                def _(sy):
                    P.wait(sy, "sy", "pe")
                    wload(sy, 2 * O_B1, 1024, wtw_h, 0, D, 4, 1024, "qww")
                    wload(sy, 2 * O_B8, 1024, wtw_h, 512 * D, D, 4, 1024, "qww")
                ww_q1 = P.cnt["qww"]

                @blk.vector
                def _(v):
                    P.wait(v, "v", "pe")
                    v.drain()
                    P.inc(emit_mix(v, O_B3, 2), "v", 1)
                mixv_gp = P.cnt["v"]


                for g in range(NG):
                    @blk.tensor
                    def _(t, g=g, mixv_gp=mixv_gp, wv_q1=wv_q1):
                        P.wait(t, "pe", "v", mixv_gp)
                        P.wait(t, "pe", "qwv", wv_q1)
                        if g >= 1:
                            P.wait(t, "pe", "v", P.cnt["v"])
                        for half in range(2):
                            b = 2 + half
                            for kc in range(8):
                                m = t.matmul(
                                    ps[:, b * 512:(b + 1) * 512],
                                    bass.AP(AR16.tensor,
                                            2 * O_B3 + kc * 512 + g * 128,
                                            [[PIT16, 128], [1, 128]]),
                                    bass.AP(AR16.tensor,
                                            2 * O_WWIN + kc * 1024 + half * 512,
                                            [[PIT16, 128], [1, 512]]),
                                    start=(kc == 0), stop=(kc == 7))
                        P.inc(m, "pe", 1)
                    vg_pe = P.cnt["pe"]

                    @blk.vector
                    def _(v, g=g, vg_pe=vg_pe):
                        P.wait(v, "v", "pe", vg_pe)
                        P.wait(v, "v", "q1", ffnw_q1)
                        P.inc(v.tensor_copy(bfv(O_B7 + g * 512, 1024),
                                            ps[:, 1024:2048]), "v", 1)
                vtok_v = P.cnt["v"]

                @blk.vector
                def _(v):
                    P.wait(v, "v", "pe")
                    v.drain()
                    P.inc(emit_mix(v, O_B2, 3), "v", 1)
                mixw_v = P.cnt["v"]

                w_done = emit_proj(wsrc_win2, O_B2, "v", mixw_v,
                                   lambda sc, oc, pb: sc.activation(
                                       f32v(O_A5 + oc * 512, 512), pb, AF.Exp,
                                       scale=-1.0),
                                   extra_pe_wait=("qww", ww_q1))
                if not phase_on("C"):
                    continue

                @blk.scalar
                def _(sc, w_done=w_done):
                    P.wait(sc, "sc", "s", w_done)
                    sc.drain()
                    P.inc(sc.activation(f32v(O_A5, 4096), f32v(O_A5, 4096), AF.Ln,
                                        bias=one_c), "s", 1)
                lp_s = P.cnt["s"]

                @blk.vector
                def _(v, lp_s=lp_s):
                    P.wait(v, "v", "s", lp_s)
                    P.wait(v, "v", "pe")
                    v.tensor_scalar(f32v(O_A4 + 1, 4096), f32v(O_A5, 4096), -1.0,
                                    -1e-4, ALU.mult, ALU.add)
                    v.memset(f32v(O_A4, 1), 0.0)
                    v.drain()
                    P.inc(v.tensor_tensor_scan(f32v(O_A4 + 1, 4096), f32v(O_A4 + 1, 4096),
                                               cpool[:, 5:6].broadcast_to((128, 4096)),
                                               0.0, ALU.add, ALU.add), "v", 1)
                    a_end = f32d(O_A4 + 64, [[64, 64]])
                    a_prev = f32d(O_A4, [[64, 64]])
                    v.drain()
                    v.tensor_sub(smv(S_D1, 64), a_end, a_prev)
                    v.drain()
                    P.inc(v.scalar_tensor_tensor(smv(S_GAM, 64), smv(S_D1, 64), 0.5,
                                                 a_prev, ALU.mult, ALU.add), "v", 1)
                scan_v = P.cnt["v"]

                @blk.scalar
                def _(sc, scan_v=scan_v):
                    P.wait(sc, "sc", "v", scan_v)
                    sc.activation(smv(S_EGP, 64), smv(S_D1, 64), AF.Exp, scale=0.5)
                    P.inc(sc.activation(smv(S_E2C, 64), smv(S_D1, 64), AF.Exp), "s", 1)
                egp_s = P.cnt["s"]

                gam_rep = bass.AP(AR.tensor, O_SMALL + S_GAM,
                                  [[PIT, 128], [1, 64], [0, 64]])
                egp_rep = bass.AP(AR.tensor, O_SMALL + S_EGP,
                                  [[PIT, 128], [1, 64], [0, 64]])

                @blk.vector
                def _(v, egp_s=egp_s):
                    P.wait(v, "v", "s", egp_s)
                    P.inc(v.tensor_sub(f32d(O_A5, [[64, 64], [1, 64]]),
                                       f32d(O_A4, [[64, 64], [1, 64]]), gam_rep), "v", 1)
                arg1_v = P.cnt["v"]

                @blk.scalar
                def _(sc, arg1_v=arg1_v):
                    P.wait(sc, "sc", "v", arg1_v)
                    P.inc(sc.activation(f32v(O_A5, 4096), f32v(O_A5, 4096), AF.Exp),
                          "s", 1)
                ea_s = P.cnt["s"]

                @blk.vector
                def _(v, ea_s=ea_s, r_done=r_done):
                    P.wait(v, "v", "s", ea_s)
                    P.wait(v, "v", "s", r_done)
                    P.inc(v.tensor_mul(bfv(O_B4, 4096), f32v(O_RF, 4096),
                                       f32v(O_A5, 4096)), "v", 1)
                rt_v = P.cnt["v"]

                @blk.gpsimd
                def _(gp, rt_v=rt_v):
                    P.wait(gp, "gp", "v", rt_v)
                    P.wait(gp, "gp", "q1", ffnw_q1)
                    P.inc(gp.tensor_tensor(bfd(O_B6, [[64, 64], [1, 64]]),
                                           bfd(O_B4, [[64, 64], [1, 64]]),
                                           egp_rep, ALU.mult), "gp", 1)
                rh_gp = P.cnt["gp"]

                @blk.vector
                def _(v):
                    v.drain()
                    P.inc(v.tensor_sub(f32d(O_A5, [[64, 64], [1, 64]]), gam_rep,
                                       f32d(O_A4 + 1, [[64, 64], [1, 64]])), "v", 1)
                arg2_v = P.cnt["v"]

                @blk.scalar
                def _(sc, arg2_v=arg2_v):
                    P.wait(sc, "sc", "v", arg2_v)
                    P.inc(sc.activation(f32v(O_A5, 4096), f32v(O_A5, 4096), AF.Exp),
                          "s", 1)
                ek_s = P.cnt["s"]

                @blk.vector
                def _(v, ek_s=ek_s, k_done=k_done):
                    P.wait(v, "v", "s", ek_s)
                    P.wait(v, "v", "s", k_done)
                    P.inc(v.tensor_mul(bfv(O_B5, 4096), f32v(O_KF, 4096),
                                       f32v(O_A5, 4096)), "v", 1)
                kt_v = P.cnt["v"]

                @blk.gpsimd
                def _(gp, kt_v=kt_v):
                    P.wait(gp, "gp", "v", kt_v)
                    P.inc(gp.tensor_tensor(bfd(O_B2, [[64, 64], [1, 64]]),
                                           bfd(O_B5, [[64, 64], [1, 64]]),
                                           egp_rep, ALU.mult), "gp", 1)
                kh_gp = P.cnt["gp"]

                for rc in range(NRC):
                    @blk.scalar
                    def _(sc, rc=rc, k_done=k_done):
                        P.wait(sc, "sc", "s", k_done)
                        if rc > 0:
                            P.wait(sc, "sc", "v", P.cnt["v"])
                        P.inc(sc.activation(f32v(O_ESC, 512),
                                            f32v(O_KF + rc * 512, 512), AF.Exp,
                                            bias=u_sb[:, rc:rc + 1]), "s", 1)
                    e_s = P.cnt["s"]

                    @blk.vector
                    def _(v, rc=rc, e_s=e_s, r_done=r_done):
                        P.wait(v, "v", "s", e_s)
                        P.wait(v, "v", "s", r_done)
                        P.inc(v.tensor_mul(bfv(O_B3 + rc * 256, 512),
                                           f32v(O_RF + rc * 512, 512),
                                           f32v(O_ESC, 512)), "v", 1)
                re_v = P.cnt["v"]

                @blk.sync
                def _(sy, kh_gp=kh_gp):
                    P.wait(sy, "sy", "gp", kh_gp)
                    P.wait(sy, "sy", "pe")
                    for rc in range(NRC):
                        for cp in range(4):
                            o = rc * 512 + cp * 128
                            P.inc(sy.dma_start_transpose(
                                out=bass.AP(AR16.tensor, 2 * O_B1 + o,
                                            [[PIT16, 128], [1, 128]]),
                                in_=bass.AP(AR16.tensor, 2 * O_B2 + o,
                                            [[PIT16, 128], [1, 128]])), "qkht", 16)
                kht_q1 = P.cnt["qkht"]
                if not phase_on("D"):
                    continue

                import os
                nch_lim = int(os.environ.get("KCH", NCH))
                dstop = os.environ.get("KDSTOP", "")
                sel_m, oev_m, stup_m = {}, {}, {}
                for c in range(nch_lim):
                    q = c % 2
                    rq = 64 * q
                    scb = 0 if q == 0 else 2

                    @blk.tensor
                    def _(t, c=c, q=q, rq=rq, scb=scb):
                        P.wait(t, "pe", "v", rt_v)
                        P.wait(t, "pe", "v", kt_v)
                        P.wait(t, "pe", "v", re_v)
                        if c >= 2:
                            P.wait(t, "pe", "v", sel_m[c - 2])
                            P.wait(t, "pe", "v", stup_m[c - 2])
                        for rc in range(NRC):
                            co = rc * 512 + c * 64
                            t.matmul(ps[rq:rq + 64, scb * 512 + rc * 64: scb * 512 + rc * 64 + 64],
                                     bass.AP(AR16.tensor, 2 * O_B5 + co,
                                             [[PIT16, 64], [1, 64]]),
                                     bass.AP(AR16.tensor, 2 * O_B4 + co,
                                             [[PIT16, 64], [1, 64]]),
                                     start=True, stop=True, tile_position=(0, rq))
                            t.matmul(ps[rq:rq + 64, (scb + 1) * 512 + rc * 64: (scb + 1) * 512 + rc * 64 + 64],
                                     bass.AP(AR16.tensor, 64 * PIT16 + 2 * O_B5 + co,
                                             [[PIT16, 64], [1, 64]]),
                                     bass.AP(AR16.tensor, 64 * PIT16 + 2 * O_B4 + co,
                                             [[PIT16, 64], [1, 64]]),
                                     start=True, stop=True, tile_position=(64, rq))
                        if c >= 1:
                            P.wait(t, "pe", "v", sel_m[c - 1])
                        for rc in range(NRC):
                            co = rc * 512 + c * 64
                            t.matmul(ps[rq:rq + 64, 4 * 512 + rc * 64: 4 * 512 + rc * 64 + 64],
                                     bdA[:, 0:64],
                                     bass.AP(AR16.tensor, 2 * O_B3 + co,
                                             [[PIT16, 128], [1, 64]]),
                                     start=True, stop=True, tile_position=(0, rq))
                            m = t.matmul(ps[rq:rq + 64, 5 * 512 + rc * 64: 5 * 512 + rc * 64 + 64],
                                         bdB[:, 0:64],
                                         bass.AP(AR16.tensor, 2 * O_B3 + co,
                                                 [[PIT16, 128], [1, 64]]),
                                         start=True, stop=True, tile_position=(0, rq))
                        P.inc(m, "pe", 1)
                    sc_pe = P.cnt["pe"]
                    if dstop == "scores":
                        sel_m[c] = P.cnt["v"]; oev_m[c] = P.cnt["s"]; stup_m[c] = P.cnt["v"]
                        continue

                    @blk.vector
                    def _(v, c=c, rq=rq, scb=scb, sc_pe=sc_pe):
                        P.wait(v, "v", "pe", sc_pe)
                        sfv = bass.AP(AR16.tensor, rq * PIT16 + 2 * O_SF,
                                      [[PIT16, 64], [1, 1024]])
                        psc = bass.AP(ps.ap().tensor, rq * 4096 + scb * 512,
                                      [[4096, 64], [1, 1024]])
                        pdd = bass.AP(ps.ap().tensor, rq * 4096 + 4 * 512,
                                      [[4096, 64], [1, 1024]])
                        v.tensor_tensor(sfv, psc, mstrf[rq:rq + 64, :], ALU.mult)
                        v.drain()
                        P.inc(v.select(sfv, mdiaf[rq:rq + 64, :], pdd, sfv,
                                       add_drain=True), "v", 1)
                    sel_m[c] = P.cnt["v"]
                    if dstop == "sel":
                        oev_m[c] = P.cnt["s"]; stup_m[c] = P.cnt["v"]
                        continue

                    @blk.tensor
                    def _(t, c=c, q=q, rq=rq, scb=scb):
                        P.wait(t, "pe", "v", sel_m[c])
                        P.wait(t, "pe", "v", vtok_v)
                        P.wait(t, "pe", "qkht", kht_q1)
                        if c >= 1:
                            P.wait(t, "pe", "v", oev_m[c - 1])
                            P.wait(t, "pe", "v", stup_m[c - 1])
                        g = c // 2
                        for rc in range(NRC):
                            vA = bass.AP(AR16.tensor,
                                         rq * PIT16 + 2 * O_B7 + g * 1024 + rc * 128,
                                         [[PIT16, 64], [1, 64]])
                            vB = bass.AP(AR16.tensor,
                                         rq * PIT16 + 2 * O_B7 + g * 1024 + rc * 128 + 64,
                                         [[PIT16, 64], [1, 64]])
                            sfA = bass.AP(AR16.tensor, rq * PIT16 + 2 * O_SF + rc * 64,
                                          [[PIT16, 64], [1, 64]])
                            sfB = bass.AP(AR16.tensor,
                                          rq * PIT16 + 2 * O_SF + 512 + rc * 64,
                                          [[PIT16, 64], [1, 64]])
                            t.matmul(ps[0:64, 6 * 512 + rc * 64: 6 * 512 + rc * 64 + 64],
                                     state[0:64, rc * 64: rc * 64 + 64],
                                     bass.AP(AR16.tensor, 2 * O_B6 + rc * 512 + c * 64,
                                             [[PIT16, 64], [1, 64]]),
                                     start=True, stop=True, tile_position=(0, 0))
                            t.matmul(ps[0:64, 7 * 512 + rc * 64: 7 * 512 + rc * 64 + 64],
                                     vA, sfA, start=True, stop=True,
                                     tile_position=(rq, 0))
                            t.matmul(ps[64:128, 6 * 512 + rc * 64: 6 * 512 + rc * 64 + 64],
                                     state[64:128, rc * 64: rc * 64 + 64],
                                     bass.AP(AR16.tensor,
                                             64 * PIT16 + 2 * O_B6 + rc * 512 + c * 64,
                                             [[PIT16, 64], [1, 64]]),
                                     start=True, stop=True, tile_position=(64, 64))
                            t.matmul(ps[64:128, 7 * 512 + rc * 64: 7 * 512 + rc * 64 + 64],
                                     vB, sfB, start=True, stop=True,
                                     tile_position=(rq, 64))
                            kA = bass.AP(AR16.tensor,
                                         rq * PIT16 + 2 * O_B1 + rc * 512 + (c // 2) * 128,
                                         [[PIT16, 64], [1, 64]])
                            kB = bass.AP(AR16.tensor,
                                         rq * PIT16 + 2 * O_B1 + rc * 512 + (c // 2) * 128 + 64,
                                         [[PIT16, 64], [1, 64]])
                            t.matmul(ps[0:64, scb * 512 + rc * 64: scb * 512 + rc * 64 + 64],
                                     kA, vA, start=True, stop=True,
                                     tile_position=(rq, 0))
                            m = t.matmul(ps[64:128, scb * 512 + rc * 64: scb * 512 + rc * 64 + 64],
                                         kB, vB, start=True, stop=True,
                                         tile_position=(rq, 64))
                        P.inc(m, "pe", 1)
                    av_pe = P.cnt["pe"]
                    if dstop == "av":
                        oev_m[c] = P.cnt["s"]; stup_m[c] = P.cnt["v"]
                        continue

                    @blk.scalar
                    def _(sc, c=c, av_pe=av_pe):
                        P.wait(sc, "sc", "pe", av_pe)
                        srcC = bass.AP(ps.ap().tensor, 6 * 512,
                                       [[4096, 128], [64, 8], [1, 64]])
                        P.inc(sc.activation(f32v(O_TS, 512), srcC, AF.Identity),
                              "s", 1)
                    oevs = P.cnt["s"]

                    @blk.vector
                    def _(v, c=c, av_pe=av_pe, oevs=oevs):
                        P.wait(v, "v", "pe", av_pe)
                        P.wait(v, "v", "s", oevs)
                        dst = bass.AP(AR16.tensor, 2 * (O_B8) + c * 64,
                                      [[PIT16, 128], [512, 8], [1, 64]])
                        srcI = bass.AP(ps.ap().tensor, 7 * 512,
                                       [[4096, 128], [64, 8], [1, 64]])
                        P.inc(v.tensor_add(dst, f32v(O_TS, 512), srcI), "v", 1)
                    oev_m[c] = P.cnt["v"]

                    @blk.vector
                    def _(v, c=c, av_pe=av_pe, scb=scb):
                        P.wait(v, "v", "pe", av_pe)
                        e2c_rep = bass.AP(AR.tensor, O_SMALL + S_E2C + c,
                                          [[PIT, 128], [8, 8], [0, 64]])
                        v.drain()
                        v.tensor_tensor(f32d(O_A5, [[64, 8], [1, 64]]),
                                        state[:].rearrange("p (a b) -> p a b", a=8),
                                        e2c_rep, ALU.mult)
                        v.drain()
                        P.inc(v.tensor_add(state[:], f32v(O_A5, 512),
                                           ps[:, scb * 512: (scb + 1) * 512]), "v", 1)
                    stup_m[c] = P.cnt["v"]

                @blk.sync
                def _(sy):
                    P.wait(sy, "sy", "pe")
                    wload(sy, 2 * O_B1, 1024, wto_h, 0, D, 8, 1024, "qwo")
                wo_q1 = P.cnt["qwo"]
                if not phase_on("E"):
                    continue

                gst_done = None
                for rc in range(NRC):
                    @blk.scalar
                    def _(sc, rc=rc):
                        P.wait(sc, "sc", "v", oev_m[nch_lim - 1])
                        if rc > 0:
                            P.wait(sc, "sc", "pe", P.cnt["pe"])
                        P.inc(sc.activation(bass.AP(AR16.tensor, 2 * O_SF,
                                                    [[PIT16, 128], [1, 512]]),
                                            bfv(O_B8 + rc * 256, 512), AF.Square),
                              "s", 1)
                    o2_s = P.cnt["s"]

                    @blk.tensor
                    def _(t, rc=rc, o2_s=o2_s):
                        P.wait(t, "pe", "s", o2_s)
                        P.wait(t, "pe", "v", sel_m[nch_lim - 1])
                        P.wait(t, "pe", "v", stup_m[nch_lim - 1])
                        b1 = rc // 4
                        pb = 32 * (rc % 4)
                        t.matmul(ps[pb:pb + 2, b1 * 512:(b1 + 1) * 512],
                                 hsel[:, 0:2], bfv(O_B8 + rc * 256, 512),
                                 start=True, stop=True, tile_position=(0, pb))
                        m = t.matmul(ps[pb:pb + 2, (2 + b1) * 512:(3 + b1) * 512],
                                     hsel[:, 0:2],
                                     bass.AP(AR16.tensor, 2 * O_SF,
                                             [[PIT16, 128], [1, 512]]),
                                     start=True, stop=True, tile_position=(0, pb))
                        P.inc(m, "pe", 1)
                gst_pe = P.cnt["pe"]

                og_v = None
                for b1 in range(2):
                    @blk.vector
                    def _(v, b1=b1, gst_pe=gst_pe):
                        P.wait(v, "v", "pe", gst_pe)
                        mu = f32v(O_SM, 512)
                        v.tensor_scalar(mu, ps[:, b1 * 512:(b1 + 1) * 512], 1.0 / DH,
                                        1e6, ALU.mult, ALU.min)
                        v.drain()
                        v.tensor_scalar_max(mu, mu, -1e6)
                        v.drain()
                        v.tensor_mul(f32v(O_A5, 512), mu, mu)
                        v.drain()
                        P.inc(v.scalar_tensor_tensor(f32v(O_SM + 512, 512),
                                                     ps[:, (2 + b1) * 512:(3 + b1) * 512],
                                                     1.0 / DH, f32v(O_A5, 512),
                                                     ALU.mult, ALU.subtract), "v", 1)
                    gv_v = P.cnt["v"]

                    @blk.vector
                    def _(v, b1=b1, gv_v=gv_v):
                        v.drain()
                        P.inc(v.tensor_scalar(f32v(O_SM + 512, 512),
                                              f32v(O_SM + 512, 512), 0.0, 1e18,
                                              ALU.max, ALU.min), "v", 1)
                    gvc_v = P.cnt["v"]

                    @blk.scalar
                    def _(sc, b1=b1, gvc_v=gvc_v):
                        P.wait(sc, "sc", "v", gvc_v)
                        sc.activation(f32v(O_A5, 512), f32v(O_SM + 512, 512), AF.Ln,
                                      bias=eps_c)
                        sc.drain()
                        P.inc(sc.activation(f32v(O_SM + 512, 512), f32v(O_A5, 512),
                                            AF.Exp, scale=-0.5), "s", 1)
                    gs_s = P.cnt["s"]

                    @blk.vector
                    def _(v, b1=b1, gs_s=gs_s):
                        P.wait(v, "v", "s", gs_s)
                        P.inc(v.tensor_mul(f32v(O_SM, 512), f32v(O_SM, 512),
                                           f32v(O_SM + 512, 512)), "v", 1)
                    gm2_v = P.cnt["v"]

                    for rr in range(4):
                        rc = b1 * 4 + rr
                        pb = 32 * rr

                        @blk.tensor
                        def _(t, rc=rc, pb=pb, gm2_v=gm2_v):
                            P.wait(t, "pe", "v", gm2_v)
                            if rc >= 2:
                                P.wait(t, "pe", "v", P.cnt["v"])
                            bs = 4 + (rc % 2)
                            bm = 6 + (rc % 2)
                            t.matmul(ps[:, bs * 512:(bs + 1) * 512], sel2f[pb:pb + 2, :],
                                     f32v(O_SM + 512, 512, pb, 2),
                                     start=True, stop=True, tile_position=(pb, 0))
                            m = t.matmul(ps[:, bm * 512:(bm + 1) * 512],
                                         sel2f[pb:pb + 2, :], f32v(O_SM, 512, pb, 2),
                                         start=True, stop=True, tile_position=(pb, 0))
                            P.inc(m, "pe", 1)
                        gb_pe = P.cnt["pe"]

                        @blk.vector
                        def _(v, rc=rc, gb_pe=gb_pe):
                            P.wait(v, "v", "pe", gb_pe)
                            v.drain()
                            bs = 4 + (rc % 2)
                            bm = 6 + (rc % 2)
                            v.tensor_mul(f32v(O_A5, 512), bfv(O_B8 + rc * 256, 512),
                                         ps[:, bs * 512:(bs + 1) * 512])
                            v.drain()
                            P.inc(v.tensor_sub(f32v(O_A5 + 512, 512), f32v(O_A5, 512),
                                               ps[:, bm * 512:(bm + 1) * 512]), "v", 1)
                        ga_v = P.cnt["v"]

                        @blk.scalar
                        def _(sc, rc=rc, ga_v=ga_v):
                            P.wait(sc, "sc", "v", ga_v)
                            P.inc(sc.activation(f32v(O_A5, 512), f32v(O_A5 + 512, 512),
                                                AF.Identity, bias=lnwb[:, 1:2],
                                                scale=lnwb[:, 0:1]), "s", 1)
                        gf_s = P.cnt["s"]

                        @blk.vector
                        def _(v, rc=rc, gf_s=gf_s):
                            P.wait(v, "v", "s", gf_s)
                            P.inc(v.tensor_mul(bfv(O_B8 + rc * 256, 512),
                                               f32v(O_A5, 512),
                                               f32v(O_RF + rc * 512, 512)), "v", 1)
                og_v = P.cnt["v"]

                for g in range(NG):
                    @blk.tensor
                    def _(t, g=g, og_v=og_v, wo_q1=wo_q1):
                        P.wait(t, "pe", "v", og_v)
                        P.wait(t, "pe", "qwo", wo_q1)
                        if g >= 1:
                            P.wait(t, "pe", "v", P.cnt["v"])
                        for half in range(2):
                            b = half
                            for kc in range(8):
                                m = t.matmul(
                                    ps[:, b * 512:(b + 1) * 512],
                                    bass.AP(AR16.tensor,
                                            2 * O_B8 + kc * 512 + g * 128,
                                            [[PIT16, 128], [1, 128]]),
                                    bass.AP(AR16.tensor,
                                            2 * O_B1 + kc * 1024 + half * 512,
                                            [[PIT16, 128], [1, 512]]),
                                    start=(kc == 0), stop=(kc == 7))
                        P.inc(m, "pe", 1)
                    op_pe = P.cnt["pe"]

                    @blk.vector
                    def _(v, g=g, op_pe=op_pe):
                        P.wait(v, "v", "pe", op_pe)
                        P.inc(v.tensor_add(f32v(O_X + g * 1024, 1024),
                                           f32v(O_X + g * 1024, 1024),
                                           ps[:, 0:1024]), "v", 1)
                x1_v = P.cnt["v"]

                @blk.vector
                def _(v, x1_v=x1_v):
                    P.wait(v, "v", "v", x1_v)
                    for g in range(NG):
                        xg = f32v(O_X + g * 1024, 1024)
                        v.reduce_sum(smv(S_ST1 + g, 1), xg, axis=AX.X)
                        v.scalar_tensor_tensor(f32v(O_A5 + g * 1024, 1024), xg, 1.0, xg,
                                               ALU.mult, ALU.mult,
                                               accum_out=smv(S_ST2 + g, 1))
                    v.drain()
                    v.tensor_scalar(smv(S_LNB, NG), smv(S_ST1, NG), 1.0 / D, None,
                                    ALU.mult)
                    v.drain()
                    v.tensor_mul(smv(S_TINY, NG), smv(S_LNB, NG), smv(S_LNB, NG))
                    v.drain()
                    P.inc(v.scalar_tensor_tensor(smv(S_LNS, NG), smv(S_ST2, NG),
                                                 1.0 / D, smv(S_TINY, NG),
                                                 ALU.mult, ALU.subtract), "v", 1)
                l2st_v = P.cnt["v"]

                @blk.scalar
                def _(sc, l2st_v=l2st_v):
                    P.wait(sc, "sc", "v", l2st_v)
                    sc.activation(smv(S_TINY, NG), smv(S_LNS, NG), AF.Ln, bias=eps_c)
                    sc.drain()
                    P.inc(sc.activation(smv(S_LNS, NG), smv(S_TINY, NG), AF.Exp,
                                        scale=-0.5), "s", 1)

                @blk.vector
                def _(v):
                    P.wait(v, "v", "s")
                    P.inc(v.scalar_tensor_tensor(smv(S_LNB, NG), smv(S_LNB, NG), -1.0,
                                                 smv(S_LNS, NG), ALU.mult, ALU.mult),
                          "v", 1)
                l2ab_v = P.cnt["v"]

                tq_h = []
                for p in range(2):
                    ga, gb = 2 * p, 2 * p + 1

                    @blk.scalar
                    def _(sc, p=p, ga=ga, gb=gb, l2ab_v=l2ab_v):
                        P.wait(sc, "sc", "v", l2ab_v)
                        if p > 0:
                            P.wait(sc, "sc", "v", P.cnt["v"])
                        for i, g in enumerate((ga, gb)):
                            P.inc(sc.activation(f32v(O_A5 + i * 1024, 1024),
                                                f32v(O_X + g * 1024, 1024),
                                                AF.Identity,
                                                bias=smv(S_LNB + g, 1),
                                                scale=smv(S_LNS + g, 1)), "s", 1)
                    h_s = P.cnt["s"]

                    @blk.vector
                    def _(v, p=p, ga=ga, gb=gb, h_s=h_s):
                        P.wait(v, "v", "s", h_s)
                        for i, g in enumerate((ga, gb)):
                            v.reduce_max(smv(S_MAX + g, 1),
                                         f32v(O_A5 + i * 1024, 1024), axis=AX.X,
                                         apply_absolute_value=True)
                        v.drain()
                        v.tensor_scalar_max(smv(S_MAX + ga, 2), smv(S_MAX + ga, 2),
                                            1e-5)
                        v.drain()
                        v.reciprocal(smv(S_TINY + 4, 2), smv(S_MAX + ga, 2))
                        v.drain()
                        v.tensor_scalar(smv(S_TINY + 6, 2), smv(S_TINY + 4, 2),
                                        127.0, None, ALU.mult)
                        for g in (ga, gb):
                            v.tensor_mul(smv(S_DQ + g, 1), smv(S_MAX + g, 1),
                                         smv(S_TINY2, 1))
                            v.tensor_mul(smv(S_DQ2 + g, 1), smv(S_MAX + g, 1),
                                         smv(S_TINY2 + 1, 1))
                        v.drain()
                        P.inc(v.engine_nop(), "v", 1)
                    hm_v = P.cnt["v"]

                    @blk.scalar
                    def _(sc, p=p, ga=ga, gb=gb, hm_v=hm_v):
                        P.wait(sc, "sc", "v", hm_v)
                        P.wait(sc, "sc", "gp", P.cnt["gp"])
                        P.wait(sc, "sc", "pe")
                        for i in range(2):
                            P.inc(sc.activation(f32v(O_A4 + i * 1024, 1024),
                                                f32v(O_A5 + i * 1024, 1024),
                                                AF.Identity,
                                                scale=smv(S_TINY + 6 + i, 1),
                                                bias=cpool[:, 8:9]), "s", 1)
                    hv_s = P.cnt["s"]

                    @blk.vector
                    def _(v, p=p, ga=ga, gb=gb, hv_s=hv_s):
                        P.wait(v, "v", "s", hv_s)
                        Y0 = [f32v(O_A4, 1024), f32v(O_A4 + 1024, 1024)]
                        YF = [f32v(O_A5 + 2048, 1024), f32v(O_A5 + 3072, 1024)]
                        BF = [f32v(O_A4 + 2048, 1024), f32v(O_A4 + 3072, 1024)]
                        for i in range(2):
                            v.tensor_scalar(BF[i], Y0[i], 128.0, -128.0,
                                            ALU.is_ge, ALU.mult)
                        v.drain()
                        for i in range(2):
                            v.tensor_add(YF[i], Y0[i], BF[i])
                        for m in (64.0, 32.0, 16.0, 8.0, 4.0, 2.0):
                            v.drain()
                            for i in range(2):
                                v.tensor_scalar(BF[i], YF[i], m, -m, ALU.is_ge,
                                                ALU.mult)
                            v.drain()
                            for i in range(2):
                                v.tensor_add(YF[i], YF[i], BF[i])
                        v.drain()
                        for i in range(2):
                            v.scalar_tensor_tensor(BF[i], YF[i], 1.0, YF[i],
                                                   ALU.is_ge, ALU.subtract)
                        v.drain()
                        for i, g in enumerate((ga, gb)):
                            P.inc(v.scalar_tensor_tensor(
                                bfv(O_B4 + g * 512, 1024), Y0[i], -128.0, BF[i],
                                ALU.add, ALU.add), "v", 1)
                    lad_h = P.cnt["v"]

                    for g in (ga, gb):
                        @blk.sync
                        def _(sy, g=g, lad_h=lad_h):
                            P.wait(sy, "sy", "v", lad_h)
                            if g == 0:
                                P.wait(sy, "sy", "pe")
                            for dc in range(8):
                                P.inc(sy.dma_start_transpose(
                                    out=bass.AP(AR16.tensor,
                                                2 * O_B6 + dc * 512 + g * 128,
                                                [[PIT16, 128], [1, 128]]),
                                    in_=bass.AP(AR16.tensor,
                                                2 * O_B4 + g * 1024 + dc * 128,
                                                [[PIT16, 128], [1, 128]])),
                                    "qht%d" % g, 16)
                        tq_h.append(P.cnt["qht%d" % g])
                hq_v = P.cnt["v"]
                if not phase_on("F"):
                    continue

                win_of = [(O_B7, O_B8), (O_B3, O_B4)]
                for fc in range(8):
                    wgo, wuo = win_of[fc % 2]
                    qwin = "qwinA" if fc % 2 == 0 else "qwinB"

                    @blk.sync
                    def _(sy, fc=fc, wgo=wgo, wuo=wuo, qwin=qwin):
                        P.wait(sy, "sy", "q1", ffnw_q1)
                        P.wait(sy, "sy", "pe")
                        wload(sy, 2 * wgo, 512, wqtg_h, fc * 512, FF, 8, 512,
                              qwin)
                        wload(sy, 2 * wuo, 512, wqtu_h, fc * 512, FF, 8, 512,
                              qwin)
                    win_q1 = P.cnt[qwin]

                    for g in range(NG):
                        @blk.tensor
                        def _(t, fc=fc, g=g, wgo=wgo, wuo=wuo, win_q1=win_q1,
                              tqh=tq_h[g], qwin=qwin):
                            P.wait(t, "pe", qwin, win_q1)
                            P.wait(t, "pe", "qht%d" % g, tqh)
                            bg = g % 2
                            if fc + g >= 1:
                                P.wait(t, "pe", "v", P.cnt["v"])
                            for kc in range(8):
                                t.matmul(ps[:, bg * 512:(bg + 1) * 512],
                                         bass.AP(AR16.tensor,
                                                 2 * O_B6 + kc * 512 + g * 128,
                                                 [[PIT16, 128], [1, 128]]),
                                         bass.AP(AR16.tensor, 2 * wgo + kc * 512, [[PIT16, 128], [1, 512]]),
                                         start=(kc == 0), stop=(kc == 7))
                            for kc in range(8):
                                m = t.matmul(ps[:, (2 + bg) * 512:(3 + bg) * 512],
                                             bass.AP(AR16.tensor,
                                                     2 * O_B6 + kc * 512 + g * 128,
                                                     [[PIT16, 128], [1, 128]]),
                                             bass.AP(AR16.tensor, 2 * wuo + kc * 512, [[PIT16, 128], [1, 512]]),
                                             start=(kc == 0), stop=(kc == 7))
                            P.inc(m, "pe", 1)
                        gu_pe = P.cnt["pe"]

                        @blk.scalar
                        def _(sc, g=g, gu_pe=gu_pe):
                            P.wait(sc, "sc", "pe", gu_pe)
                            bg = g % 2
                            P.inc(sc.activation(f32v(O_TS, 512),
                                                ps[:, bg * 512:(bg + 1) * 512],
                                                AF.Silu, scale=smv(S_DQ + g, 1)),
                                  "s", 1)
                        si_s = P.cnt["s"]

                        @blk.vector
                        def _(v, fc=fc, g=g, si_s=si_s):
                            P.wait(v, "v", "s", si_s)
                            bg = g % 2
                            P.inc(v.scalar_tensor_tensor(
                                f32v(GBIG[g] + fc * 512, 512),
                                ps[:, (2 + bg) * 512:(3 + bg) * 512],
                                smv(S_DQ2 + g, 1), f32v(O_TS, 512),
                                ALU.mult, ALU.mult), "v", 1)
                gu_done_v = P.cnt["v"]

                @blk.sync
                def _(sy):
                    P.wait(sy, "sy", "pe")
                    for qr in range(NG):
                        wload(sy, 2 * WDQ[qr], 512, wqtd_h, qr * 1024 * D,
                              D, 8, 512, "qwd", pieces=1)
                wdw_h0 = P.cnt["qwd"]

                GT = [O_B6, O_B1, O_B2, O_B3]
                tq_g = []

                for g in range(NG):
                    @blk.vector
                    def _(v, g=g, gu_done_v=gu_done_v):
                        P.wait(v, "v", "v", gu_done_v)
                        v.reduce_max(smv(S_MAX + g, 1), f32v(GBIG[g], 4096), axis=AX.X,
                                     apply_absolute_value=True)
                        v.drain()
                        v.tensor_scalar_max(smv(S_MAX + g, 1), smv(S_MAX + g, 1),
                                            1e-5)
                        v.drain()
                        v.reciprocal(smv(S_TINY + 4, 1), smv(S_MAX + g, 1))
                        v.drain()
                        v.tensor_scalar(smv(S_TINY + 5, 1), smv(S_TINY + 4, 1),
                                        127.0, None, ALU.mult)
                        v.tensor_mul(smv(S_DQ + g, 1), smv(S_MAX + g, 1),
                                     smv(S_TINY2 + 2, 1))
                        v.drain()
                        P.inc(v.engine_nop(), "v", 1)
                    gm_v = P.cnt["v"]
                    gsc_s = P.cnt["s"]

                    for hf in range(2):
                        @blk.scalar
                        def _(sc, g=g, hf=hf, gm_v=gm_v):
                            if hf == 0:
                                P.wait(sc, "sc", "v", gm_v)
                                P.wait(sc, "sc", "gp", P.cnt["gp"])
                            sc.drain()
                            P.inc(sc.activation(f32v(GBIG[g] + hf * 2048, 2048),
                                                f32v(GBIG[g] + hf * 2048, 2048),
                                                AF.Identity, scale=smv(S_TINY + 5, 1),
                                                bias=cpool[:, 8:9]), "s", 1)
                    gv_s = P.cnt["s"]

                    @blk.vector
                    def _(v, g=g, gv_s=gv_s):
                        P.wait(v, "v", "s", gv_s)
                        Y0 = [f32v(GBIG[g], 2048), f32v(GBIG[g] + 2048, 2048)]
                        YF = [f32v(O_A5, 2048), f32v(O_A5 + 2048, 2048)]
                        BF = [f32v(O_A4, 2048), f32v(O_A4 + 2048, 2048)]
                        for h in range(2):
                            v.tensor_scalar(BF[h], Y0[h], 128.0, -128.0,
                                            ALU.is_ge, ALU.mult)
                        v.drain()
                        for h in range(2):
                            v.tensor_add(YF[h], Y0[h], BF[h])
                        for m in (64.0, 32.0, 16.0, 8.0, 4.0, 2.0):
                            v.drain()
                            for h in range(2):
                                v.tensor_scalar(BF[h], YF[h], m, -m, ALU.is_ge,
                                                ALU.mult)
                            v.drain()
                            for h in range(2):
                                v.tensor_add(YF[h], YF[h], BF[h])
                        v.drain()
                        for h in range(2):
                            v.scalar_tensor_tensor(BF[h], YF[h], 1.0, YF[h],
                                                   ALU.is_ge, ALU.subtract)
                        v.drain()
                        for h in range(2):
                            P.inc(v.scalar_tensor_tensor(
                                bass.AP(AR16.tensor, 2 * GQB[g] + h * 2048,
                                        [[PIT16, 128], [1, 2048]]), Y0[h],
                                -128.0, BF[h], ALU.add, ALU.add),
                                "v", 1)
                    lad_v = P.cnt["v"]

                    @blk.sync
                    def _(sy, g=g, lad_v=lad_v):
                        P.wait(sy, "sy", "v", lad_v)
                        if g == 0:
                            P.wait(sy, "sy", "pe")
                        else:
                            P.wait(sy, "sy", "qgqt", P.cnt["qgqt"])
                        for fcq in range(32):
                            P.inc(sy.dma_start_transpose(
                                out=bass.AP(AR16.tensor, 2 * GT[g] + fcq * 128,
                                            [[PIT16, 128], [1, 128]]),
                                in_=bass.AP(AR16.tensor, 2 * GQB[g] + fcq * 128,
                                            [[PIT16, 128], [1, 128]])), "qgqt", 16)
                    tq_g.append(P.cnt["qgqt"])

                for half in range(2):
                    if half == 0:
                        wdw_q1 = wdw_h0
                    else:
                        @blk.sync
                        def _(sy):
                            P.wait(sy, "sy", "pe")
                            for qr in range(NG):
                                wload(sy, 2 * WDQ[qr], 512, wqtd_h,
                                      qr * 1024 * D + 512, D, 8, 512, "qwd")
                        wdw_q1 = P.cnt["qwd"]

                    for g in range(NG):
                        @blk.tensor
                        def _(t, g=g, half=half, tq=tq_g[g], wdw_q1=wdw_q1):
                            P.wait(t, "pe", "qgqt", tq)
                            P.wait(t, "pe", "qwd", wdw_q1)
                            if g >= 2:
                                P.wait(t, "pe", "v", P.cnt["v"])
                            b = 4 + (g % 2) * 2 + half
                            for kc in range(32):
                                m = t.matmul(
                                    ps[:, b * 512:(b + 1) * 512],
                                    bass.AP(AR16.tensor, 2 * GT[g] + kc * 128,
                                            [[PIT16, 128], [1, 128]]),
                                    bass.AP(AR16.tensor,
                                            2 * WDQ[kc // 8] + (kc % 8) * 512,
                                            [[PIT16, 128], [1, 512]]),
                                    start=(kc == 0), stop=(kc == 31))
                            P.inc(m, "pe", 1)
                        dn_pe = P.cnt["pe"]

                        @blk.vector
                        def _(v, g=g, half=half, dn_pe=dn_pe):
                            P.wait(v, "v", "pe", dn_pe)
                            b = 4 + (g % 2) * 2 + half
                            P.inc(v.scalar_tensor_tensor(
                                f32v(O_X + g * 1024 + half * 512, 512),
                                ps[:, b * 512:(b + 1) * 512],
                                smv(S_DQ + g, 1),
                                f32v(O_X + g * 1024 + half * 512, 512),
                                ALU.mult, ALU.add), "v", 1)
                out_v = P.cnt["v"]
                if not phase_on("G"):
                    continue

                @blk.sync
                def _(sy, s_i=s_i, t_i=t_i, tok0=tok0, out_v=out_v):
                    P.wait(sy, "sy", "v", out_v)
                    dst = bass.AP(out_h.tensor, s_i * T * D + tok0 * D,
                                  [[D, 128], [128 * D, NG], [1, D]])
                    P.inc(sy.dma_start(out=dst, in_=f32d(O_X, [[1024, NG], [1, 1024]])),
                          "st", 16)

        if dbg_h:
            @blk.sync
            def _(sy):
                P.wait(sy, "sy", "v")
                P.wait(sy, "sy", "s")
                P.wait(sy, "sy", "gp")
                P.wait(sy, "sy", "pe")
                for n, ap in dbg_h.items():
                    kind, off = n.split("_", 1)
                    off = int(off)
                    cols = ap.ap[-1][1] if len(ap.ap) >= 2 else 1
                    nrow = ap.ap[0][1]
                    if kind == "f":
                        src = f32v(off, cols, 0, nrow)
                    elif kind == "b":
                        src = bfv(off, cols, 0, nrow)
                    else:
                        src = i32v(off, cols, 0, nrow)
                    P.inc(sy.dma_start(out=ap, in_=src), "st", 16)

        @blk.sync
        def _(sy):
            P.wait(sy, "sy", "st")

    return nc


def _prep_inputs(ip):
    x = np.asarray(ip["L0_attn_x"], np.float32)
    u = np.asarray(ip["L0_attn_u"], np.float32)
    u_p = np.zeros((128, 8), np.float32)
    for rc in range(8):
        u_p[0:64, rc] = u[2 * rc]
        u_p[64:128, rc] = u[2 * rc + 1]
    tm = np.zeros((128, 32), np.float32)
    for m, nm in enumerate(["tm_r", "tm_k", "tm_v", "tm_w"]):
        t = np.asarray(ip["L0_attn_" + nm], np.float32)
        for dc in range(8):
            tm[:, m * 8 + dc] = t[dc * 128:(dc + 1) * 128]
    lnwb = np.zeros((128, 2), np.float32)
    lnwb[:, 0] = np.tile(np.asarray(ip["L0_attn_ln_w"], np.float32), 2)
    lnwb[:, 1] = np.tile(np.asarray(ip["L0_attn_ln_b"], np.float32), 2)
    pp = np.arange(128) % 64
    cc = np.arange(64)
    mstrict = (cc[None, :] > pp[:, None]).astype(np.float32)
    mdiag = (cc[None, :] == pp[:, None]).astype(np.uint8)
    mstrf = np.tile(mstrict, (1, 16)).astype(ml_dtypes.bfloat16)
    mdiaf = np.tile(mdiag, (1, 16)).astype(np.uint8)
    bdA = np.zeros((128, 64), ml_dtypes.bfloat16)
    bdA[0:64, :] = 1
    bdB = np.zeros((128, 64), ml_dtypes.bfloat16)
    bdB[64:128, :] = 1
    sel2 = np.zeros((128, 128), ml_dtypes.bfloat16)
    for qq in range(4):
        sel2[32 * qq, 0:64] = 1
        sel2[32 * qq + 1, 64:128] = 1
    sel2f = sel2.astype(np.float32)
    hsel = np.zeros((128, 2), ml_dtypes.bfloat16)
    hsel[0:64, 0] = 1
    hsel[64:128, 1] = 1
    onesf = np.ones((128, 128), np.float32)
    cpool = np.zeros((128, 10), np.float32)
    cpool[:, 8] = 128.5
    cpool[:, 0] = 0.5
    cpool[:, 1] = -0.5
    cpool[:, 2] = 1e-5
    cpool[:, 3] = -LN127
    cpool[:, 4] = 1.0
    cpool[:, 5] = 0.0
    cpool[:, 6] = LN127
    cpool[:, 7] = -1.0
    base = {
        "wr": np.asarray(ip["L0_attn_W_r"], np.float32),
        "wk": np.asarray(ip["L0_attn_W_k"], np.float32),
        "wv": np.asarray(ip["L0_attn_W_v"], np.float32),
        "ww": np.asarray(ip["L0_attn_W_w"], np.float32),
        "wo": np.asarray(ip["L0_attn_W_o"], np.float32),
        "wg": np.asarray(ip["L0_ffn_w_gate"], np.float32),
        "wu": np.asarray(ip["L0_ffn_w_up"], np.float32),
        "wd": np.asarray(ip["L0_ffn_w_down"], np.float32),
        "u_p": u_p, "tm": tm, "lnwb": lnwb, "mstrf": mstrf, "mdiaf": mdiaf,
        "bdA": bdA, "bdB": bdB, "sel2f": sel2f, "hsel": hsel,
        "onesf": onesf, "cpool": cpool,
    }
    return [dict(base, x=np.ascontiguousarray(x[2 * c:2 * c + 2])) for c in range(8)]


def kernel(**inputs):
    in_maps = _prep_inputs(inputs)
    nc = build()
    res = run_bass_kernel_spmd(nc, in_maps, core_ids=list(range(8)))
    return np.concatenate([r["out"] for r in res.results], axis=0).astype(np.float32)


# version 18
# speedup vs baseline: 1.0358x; 1.0214x over previous
import contextlib
import numpy as np
import ml_dtypes

import concourse.bass as bass
import concourse.mybir as mybir
from concourse.bass_utils import run_bass_kernel_spmd

F32 = mybir.dt.float32
BF16 = mybir.dt.bfloat16
I32 = mybir.dt.int32
U8 = mybir.dt.uint8
AF = mybir.ActivationFunctionType
ALU = mybir.AluOpType
AX = mybir.AxisListType

BQ, T, D, H, DH = 2, 2048, 1024, 16, 64
NRC = 8
TILE, NTILE = 512, 4
NG, NCH = 4, 8
FF = 4096
LN127 = float(np.log(np.float64(127.0)))

O_X = 0
O_RF = 4096
O_KF = 8192
O_A4 = 12288
O_A5 = 16448
O_TS = 20544
O_QI = 20544
O_SM = 14336
O_ESC = 21056
O_SMALL = 21568
O_GB2 = 21952
ARNF_COLS = 30144
O_XLNF = 0
O_SF = 0
O_B1 = 2112
O_B2 = 4160
O_B3 = 6208
O_B4 = 8256
O_B5 = 10304
O_B6 = 12352
O_B7 = 14400
O_B8 = 16448
O_XSV = 18496
O_WWIN = O_B5
ARNB_COLS = 37008
S_TINY, S_TINY2, S_ST1, S_ST2, S_LNS, S_LNB = 0, 8, 16, 24, 32, 40
S_GAM, S_EGP, S_E2C, S_D1 = 48, 112, 176, 240
S_MAX, S_DQ, S_DQ2, S_XSV, S_ACC = 304, 312, 320, 328, 336
S_RCC = 370

GBIG = [O_RF, O_KF, O_GB2, O_GB2 + 4096]
GQB = [O_B1, O_B2, O_B3, O_B4]
WDQ = [O_B5, O_B7, O_B8, O_XLNF]


class Prog:
    def __init__(self, nc, es):
        self.nc, self.es = nc, es
        self.sem, self.cnt, self.hwm = {}, {}, {}

    def new(self, name):
        self.sem[name] = self.es.enter_context(self.nc.semaphore(name))
        self.cnt[name] = 0

    def inc(self, instr, name, k):
        self.cnt[name] += k
        instr.then_inc(self.sem[name], k)
        return self.cnt[name]

    def wait(self, eng, ekey, name, val=None):
        if val is None:
            val = self.cnt[name]
        assert val <= self.cnt[name], f"forward wait {name}:{val}>{self.cnt[name]}"
        if val <= 0:
            return
        if self.hwm.get((ekey, name), -1) >= val:
            return
        eng.wait_ge(self.sem[name], val)
        self.hwm[(ekey, name)] = val


def build(n_seq=BQ, n_tiles=NTILE, last_phase="G", dbg=()):
    nc = bass.Bass("TRN2", target_bir_lowering=False, debug=False)
    dt = nc.dram_tensor
    x_h = dt("x", [BQ, T, D], F32, kind="ExternalInput").ap()
    wr_h = dt("wr", [D, D], F32, kind="ExternalInput").ap()
    wk_h = dt("wk", [D, D], F32, kind="ExternalInput").ap()
    wv_h = dt("wv", [D, D], F32, kind="ExternalInput").ap()
    ww_h = dt("ww", [D, D], F32, kind="ExternalInput").ap()
    wo_h = dt("wo", [D, D], F32, kind="ExternalInput").ap()
    wg_h = dt("wg", [FF, D], F32, kind="ExternalInput").ap()
    wu_h = dt("wu", [FF, D], F32, kind="ExternalInput").ap()
    wd_h = dt("wd", [D, FF], F32, kind="ExternalInput").ap()
    u_h = dt("u_p", [128, 8], F32, kind="ExternalInput").ap()
    tm_h = dt("tm", [128, 32], F32, kind="ExternalInput").ap()
    lnwb_h = dt("lnwb", [128, 2], F32, kind="ExternalInput").ap()
    mstrf_h = dt("mstrf", [128, 1024], BF16, kind="ExternalInput").ap()
    mdiaf_h = dt("mdiaf", [128, 1024], U8, kind="ExternalInput").ap()
    bdA_h = dt("bdA", [128, 64], BF16, kind="ExternalInput").ap()
    bdB_h = dt("bdB", [128, 64], BF16, kind="ExternalInput").ap()
    sel2f_h = dt("sel2f", [128, 128], F32, kind="ExternalInput").ap()
    hsel_h = dt("hsel", [128, 2], BF16, kind="ExternalInput").ap()
    onesf_h = dt("onesf", [128, 128], F32, kind="ExternalInput").ap()
    cpool_h = dt("cpool", [128, 10], F32, kind="ExternalInput").ap()
    out_h = dt("out", [BQ, T, D], F32, kind="ExternalOutput").ap()
    wtr_h = dt("wtr", [D, D], BF16, kind="Internal").ap()
    wtk_h = dt("wtk", [D, D], BF16, kind="Internal").ap()
    wtv_h = dt("wtv", [D, D], BF16, kind="Internal").ap()
    wtw_h = dt("wtw", [D, D], BF16, kind="Internal").ap()
    wto_h = dt("wto", [D, D], BF16, kind="Internal").ap()
    wqtg_h = dt("wqtg", [D, FF], BF16, kind="Internal").ap()
    wqtu_h = dt("wqtu", [D, FF], BF16, kind="Internal").ap()
    wqtd_h = dt("wqtd", [FF, D], BF16, kind="Internal").ap()
    dbg_h = {n: dt("dbg_" + n, sh, dy, kind="ExternalOutput").ap() for n, sh, dy in dbg}

    es = contextlib.ExitStack()
    with es:
        sbt = lambda n, s, d: es.enter_context(nc.sbuf_tensor(n, s, d))
        arn = sbt("arn", [128, ARNF_COLS], F32)
        arnb = sbt("arnb", [128, ARNB_COLS], BF16)
        u_sb = sbt("u_sb", [128, 8], F32)
        tm_sb = sbt("tm_sb", [128, 32], F32)
        lnwb = sbt("lnwb_s", [128, 2], F32)
        mstrf = sbt("mstrf_s", [128, 1024], BF16)
        mdiaf = sbt("mdiaf_s", [128, 1024], U8)
        bdA = sbt("bdA_s", [128, 64], BF16)
        bdB = sbt("bdB_s", [128, 64], BF16)
        sel2f = sbt("sel2f_s", [128, 128], F32)
        hsel = sbt("hsel_s", [128, 2], BF16)
        onesf = sbt("onesf_s", [128, 128], F32)
        cpool = sbt("cpool_s", [128, 10], F32)
        state = sbt("state", [128, 512], BF16)
        thr2 = sbt("thr2", [128, 4], F32)
        lnmw = sbt("lnmw", [128, 4], F32)
        ps = es.enter_context(nc.psum_tensor("ps", [128, 4096], F32))

        AR = arn.ap()
        AR16 = arnb.ap()
        ARI = arn.bitcast(I32).ap()
        PIT = ARNF_COLS
        PIT16 = ARNB_COLS

        def f32v(off, cols, p0=0, pn=128):
            return bass.AP(AR.tensor, p0 * PIT + off, [[PIT, pn], [1, cols]])

        def f32d(off, dims, p0=0, pn=128):
            return bass.AP(AR.tensor, p0 * PIT + off, [[PIT, pn]] + dims)

        def bfv(off, cols, p0=0, pn=128):
            return bass.AP(AR16.tensor, p0 * PIT16 + 2 * off, [[PIT16, pn], [1, cols]])

        def bfd(off, dims, p0=0, pn=128):
            return bass.AP(AR16.tensor, p0 * PIT16 + 2 * off, [[PIT16, pn]] + dims)

        def i32v(off, cols, p0=0, pn=128):
            return bass.AP(ARI.tensor, p0 * PIT + off, [[PIT, pn], [1, cols]])

        def smv(col, n, p0=0, pn=128):
            return f32v(O_SMALL + col, n, p0, pn)

        def xsv_ap(n=8):
            return bass.AP(AR16.tensor, 2 * O_XSV, [[PIT16, 128], [1, n]])

        for i, v in enumerate([0.5, -0.5, 1e-5, -LN127, 1.0, 0.0, LN127, -1.0]):
            nc.const_aps.register(F32, float(v), cpool[:, i:i + 1])

        blk = es.enter_context(nc.Block())
        P = Prog(nc, es)
        for s in ("q1", "q2", "pe", "v", "s", "gp", "st", "qx", "qxl", "qwv",
                  "qww", "qwo", "qkht", "qhqt", "qwinA", "qwinB", "qwd", "qgqt",
                  "qldA", "qldB", "qc", "qat", "qwk", "qwr",
                  "qht0", "qht1", "qht2", "qht3"):
            P.new(s)

        eps_c = cpool[:, 2:3]
        one_c = cpool[:, 4:5]
        ln127_c = cpool[:, 6:7]
        neg1_c = cpool[:, 7:8]

        def wload(sy, dst_off, dst_bs, src_h, src_off, pstride, nblk, inner,
                  sem, pieces=1):
            per = max(1, nblk // pieces)
            for s in range(0, nblk, per):
                n = min(per, nblk - s)
                P.inc(sy.dma_start(
                    out=bass.AP(AR16.tensor, dst_off + s * dst_bs,
                                [[PIT16, 128], [dst_bs, n], [1, inner]]),
                    in_=bass.AP(src_h.tensor, src_off + s * 128 * pstride,
                                [[pstride, 128], [128 * pstride, n],
                                 [1, inner]])), sem, 16)

        @blk.sync
        def _(sy):
            for dst, src in ((u_sb, u_h), (tm_sb, tm_h), (lnwb, lnwb_h),
                             (mstrf, mstrf_h), (mdiaf, mdiaf_h), (bdA, bdA_h),
                             (bdB, bdB_h), (sel2f, sel2f_h), (hsel, hsel_h),
                             (onesf, onesf_h), (cpool, cpool_h)):
                P.inc(sy.dma_start(out=dst[:], in_=src), "qc", 16)
        consts_q1 = P.cnt["qc"]
        for eng, key in ((blk.vector, "v"), (blk.scalar, "sc"),
                         (blk.gpsimd, "gp"), (blk.tensor, "pe")):
            @eng
            def _(e, key=key):
                P.wait(e, key, "qc", consts_q1)

        @blk.vector
        def _(v):
            v.tensor_scalar(smv(S_RCC + 1, 1), cpool[:, 4:5], 127.0, None,
                            ALU.mult)
            v.drain()
            v.reciprocal(smv(S_RCC, 1), smv(S_RCC + 1, 1))
            v.drain()
            P.inc(v.engine_nop(), "v", 1)

        att_list = [(wr_h, None, wtr_h), (wk_h, None, wtk_h),
                    (wv_h, None, wtv_h), (ww_h, None, wtw_h), (wo_h, None, wto_h)]
        for i, (src_h, res_t, hbm_t) in enumerate(att_list):
            @blk.gpsimd
            def _(gp, i=i, src_h=src_h):
                if i > 0:
                    P.wait(gp, "gp", "qat")
                for oc in range(8):
                    P.inc(gp.dma_start(
                        out=bass.AP(AR16.tensor, 2 * O_B1 + oc * 1024,
                                    [[PIT16, 128], [1, 1024]]),
                        in_=src_h[oc * 128:(oc + 1) * 128, :]), "q2", 16)

            @blk.sync
            def _(sy, i=i, res_t=res_t, hbm_t=hbm_t):
                P.wait(sy, "sy", "q2")
                if True:
                    for oc in range(8):
                        P.wait(sy, "sy", "qat")
                        for dc in range(8):
                            P.inc(sy.dma_start_transpose(
                                out=bass.AP(AR16.tensor, 2 * O_B7 + dc * 128,
                                            [[PIT16, 128], [1, 128]]),
                                in_=bass.AP(AR16.tensor, 2 * O_B1 + oc * 1024 + dc * 128,
                                            [[PIT16, 128], [1, 128]])), "qat", 16)
                        P.wait(sy, "sy", "qat")
                        src = bass.AP(AR16.tensor, 2 * O_B7,
                                      [[PIT16, 128], [128, 8], [1, 128]])
                        dst = bass.AP(hbm_t.tensor, oc * 128,
                                      [[D, 128], [128 * D, 8], [1, 128]])
                        P.inc(sy.dma_start(out=dst, in_=src), "qat", 16)
        attw_q1 = P.cnt["qat"]

        wstg = [f32v(O_A4, 4096), f32v(O_A5, 4096)]
        absout = f32v(O_X, 4096)
        sqout = f32v(O_RF, 4096)
        sgout = bfv(O_B5, 4096)
        wqnat = bfv(O_B6, 4096)
        accw = smv(S_ACC, 32)
        ffn_specs = [(wg_h, wqtg_h, 32, 1024, FF), (wu_h, wqtu_h, 32, 1024, FF),
                     (wd_h, wqtd_h, 8, 4096, D)]
        for j, (src_h, dst_h, nchunk, ccols, wid) in enumerate(ffn_specs):
            for ch in range(nchunk):
                stg = wstg[ch % 2]

                qld = "qldA" if ch % 2 == 0 else "qldB"

                @blk.sync
                def _(sy, ch=ch, stg=stg, src_h=src_h, ccols=ccols, qld=qld):
                    if ch >= 2 or j > 0:
                        P.wait(sy, "sy", "v")
                        P.wait(sy, "sy", "s")
                    qc_ = ccols // 4
                    for q in range(4):
                        P.inc(sy.dma_start(
                            out=bass.AP(stg.tensor, stg.offset + q * qc_,
                                        [[PIT, 128], [1, qc_]]),
                            in_=src_h[ch * 128:(ch + 1) * 128,
                                      q * qc_:(q + 1) * qc_]), qld, 16)
                ld = P.cnt[qld]

                @blk.vector
                def _(v, ch=ch, stg=stg, ccols=ccols, ld=ld, qld=qld):
                    P.wait(v, "v", qld, ld)
                    stgv = bass.AP(stg.tensor, stg.offset, [[PIT, 128], [1, ccols]])
                    P.inc(v.scalar_tensor_tensor(stgv, stgv, -1.0, stgv,
                                                 ALU.mult, ALU.max,
                                                 accum_out=smv(S_ACC + ch, 1)),
                          "v", 1)

            @blk.vector
            def _(v, j=j, nchunk=nchunk):
                v.drain()
                P.inc(v.reduce_sum(smv(S_TINY, 1), smv(S_ACC, nchunk), axis=AX.X), "v", 1)

            @blk.tensor
            def _(t, j=j):
                P.wait(t, "pe", "v")
                P.wait(t, "pe", "qc", consts_q1)
                P.inc(t.matmul(ps[0:1, 0:1], smv(S_TINY, 1), cpool[:, 4:5],
                               start=True, stop=True), "pe", 1)

            @blk.vector
            def _(v, j=j):
                P.wait(v, "v", "pe")
                v.tensor_scalar(smv(S_TINY, 1, 0, 1), ps[0:1, 0:1],
                                1.0 / (FF * D), 1e-5, ALU.mult, ALU.max)
                v.drain()
                v.tensor_scalar(smv(S_TINY + 1, 1, 0, 1), smv(S_TINY, 1, 0, 1),
                                0.5, None, ALU.mult)
                v.drain()
                v.tensor_mul(smv(S_TINY + 2, 1, 0, 1), smv(S_TINY + 1, 1, 0, 1),
                             smv(S_TINY + 1, 1, 0, 1))
                P.inc(v.tensor_mul(smv(S_TINY + 3, 1, 0, 1), smv(S_TINY, 1, 0, 1),
                                   smv(S_RCC, 1, 0, 1)), "v", 1)

            @blk.tensor
            def _(t, j=j):
                P.wait(t, "pe", "s")
                P.wait(t, "pe", "v")
                P.inc(t.matmul(ps[:, 512:514], onesf[0:1, :], smv(S_TINY + 2, 2, 0, 1),
                               start=True, stop=True), "pe", 1)

            @blk.vector
            def _(v, j=j):
                P.wait(v, "v", "pe")
                v.tensor_copy(thr2[:, j:j + 1], ps[:, 512:513])
                v.tensor_copy(lnmw[:, j:j + 1], ps[:, 513:514])
                v.drain()
                P.inc(v.engine_nop(), "v", 1)
            thr_v = P.cnt["v"]

            npiece = ccols // 128
            for ch in range(nchunk):
                stg = wstg[ch % 2]

                qld = "qldA" if ch % 2 == 0 else "qldB"

                @blk.sync
                def _(sy, ch=ch, stg=stg, src_h=src_h, ccols=ccols, qld=qld):
                    P.wait(sy, "sy", "s")
                    qc_ = ccols // 4
                    for q in range(4):
                        P.inc(sy.dma_start(
                            out=bass.AP(stg.tensor, stg.offset + q * qc_,
                                        [[PIT, 128], [1, qc_]]),
                            in_=src_h[ch * 128:(ch + 1) * 128,
                                      q * qc_:(q + 1) * qc_]), qld, 16)
                ld = P.cnt[qld]

                @blk.scalar
                def _(sc, ch=ch, stg=stg, ccols=ccols, ld=ld, thr_v=thr_v, qld=qld):
                    P.wait(sc, "sc", qld, ld)
                    P.wait(sc, "sc", "v", thr_v)
                    if ch >= 1:
                        P.wait(sc, "sc", "v")
                    sc.activation(bass.AP(sqout.tensor, sqout.offset,
                                          [[PIT, 128], [1, ccols]]),
                                  bass.AP(stg.tensor, stg.offset,
                                          [[PIT, 128], [1, ccols]]), AF.Square)
                    P.inc(sc.activation(bass.AP(sgout.tensor, sgout.offset,
                                                [[PIT16, 128], [1, ccols]]),
                                        bass.AP(stg.tensor, stg.offset,
                                                [[PIT, 128], [1, ccols]]), AF.Sign),
                          "s", 1)
                sq_s = P.cnt["s"]

                @blk.vector
                def _(v, ch=ch, j=j, ccols=ccols, sq_s=sq_s, thr_v=thr_v):
                    P.wait(v, "v", "s", sq_s)
                    P.wait(v, "v", "v", thr_v)
                    P.wait(v, "v", "q1")
                    P.inc(v.scalar_tensor_tensor(
                        bass.AP(wqnat.tensor, wqnat.offset, [[PIT16, 128], [1, ccols]]),
                        bass.AP(sqout.tensor, sqout.offset, [[PIT, 128], [1, ccols]]),
                        thr2[:, j:j + 1],
                        bass.AP(sgout.tensor, sgout.offset, [[PIT16, 128], [1, ccols]]),
                        ALU.is_gt, ALU.mult), "v", 1)
                wq_v = P.cnt["v"]

                @blk.sync
                def _(sy, ch=ch, j=j, npiece=npiece, wid=wid, dst_h=dst_h, wq_v=wq_v):
                    P.wait(sy, "sy", "v", wq_v)
                    P.wait(sy, "sy", "qat")
                    P.wait(sy, "sy", "q1")
                    for pc in range(npiece):
                        P.inc(sy.dma_start_transpose(
                            out=bass.AP(AR16.tensor, 2 * O_B7 + pc * 128,
                                        [[PIT16, 128], [1, 128]]),
                            in_=bass.AP(AR16.tensor, 2 * O_B6 + pc * 128,
                                        [[PIT16, 128], [1, 128]])), "q1", 16)
                    P.wait(sy, "sy", "q1")
                    src = bass.AP(AR16.tensor, 2 * O_B7,
                                  [[PIT16, 128], [128, npiece], [1, 128]])
                    dst = bass.AP(dst_h.tensor, ch * 128,
                                  [[wid, 128], [128 * wid, npiece], [1, 128]])
                    P.inc(sy.dma_start(out=dst, in_=src), "q1", 16)
        ffnw_q1 = P.cnt["q1"]

        @blk.vector
        def _(v):
            P.wait(v, "v", "pe")
            P.inc(v.tensor_copy(smv(S_TINY2, 3), lnmw[:, 0:3]), "v", 1)
        tiny2_v = P.cnt["v"]

        PH = ["A", "B", "C", "D", "E", "F", "G"]
        phlim = PH.index(last_phase)

        def phase_on(p):
            return PH.index(p) <= phlim


        for s_i in range(n_seq):
            @blk.vector
            def _(v, s_i=s_i):
                if s_i > 0:
                    P.wait(v, "v", "pe")
                P.inc(v.memset(state[:], 0.0), "v", 1)

            for t_i in range(n_tiles):
                tok0 = t_i * TILE
                @blk.sync
                def _(sy, s_i=s_i, t_i=t_i, tok0=tok0):
                    P.wait(sy, "sy", "v")
                    P.wait(sy, "sy", "st")
                    src = bass.AP(x_h.tensor, s_i * T * D + tok0 * D,
                                  [[D, 128], [128 * D, NG], [1, D]])
                    P.inc(sy.dma_start(out=f32d(O_X, [[1024, NG], [1, 1024]]), in_=src),
                          "qx", 16)
                xload = P.cnt["qx"]

                @blk.vector
                def _(v, xload=xload):
                    P.wait(v, "v", "qx", xload)
                    P.wait(v, "v", "v", tiny2_v)
                    for g in range(NG):
                        xg = f32v(O_X + g * 1024, 1024)
                        v.reduce_sum(smv(S_ST1 + g, 1), xg, axis=AX.X)
                        v.scalar_tensor_tensor(f32v(O_A5 + g * 1024, 1024), xg, 1.0, xg,
                                               ALU.mult, ALU.mult,
                                               accum_out=smv(S_ST2 + g, 1))
                    v.drain()
                    v.tensor_scalar(smv(S_LNB, NG), smv(S_ST1, NG), 1.0 / D, None,
                                    ALU.mult)
                    v.drain()
                    v.tensor_mul(smv(S_TINY, NG), smv(S_LNB, NG), smv(S_LNB, NG))
                    v.drain()
                    P.inc(v.scalar_tensor_tensor(smv(S_LNS, NG), smv(S_ST2, NG),
                                                 1.0 / D, smv(S_TINY, NG),
                                                 ALU.mult, ALU.subtract), "v", 1)
                lnst_v = P.cnt["v"]

                @blk.scalar
                def _(sc, lnst_v=lnst_v):
                    P.wait(sc, "sc", "v", lnst_v)
                    sc.activation(smv(S_TINY, NG), smv(S_LNS, NG), AF.Ln, bias=eps_c)
                    sc.drain()
                    P.inc(sc.activation(smv(S_LNS, NG), smv(S_TINY, NG), AF.Exp,
                                        scale=-0.5), "s", 1)

                @blk.vector
                def _(v):
                    P.wait(v, "v", "s")
                    P.inc(v.scalar_tensor_tensor(smv(S_LNB, NG), smv(S_LNB, NG), -1.0,
                                                 smv(S_LNS, NG), ALU.mult, ALU.mult),
                          "v", 1)
                lnab_v = P.cnt["v"]

                @blk.scalar
                def _(sc, lnab_v=lnab_v):
                    P.wait(sc, "sc", "v", lnab_v)
                    P.wait(sc, "sc", "pe")
                    for g in range(NG):
                        P.inc(sc.activation(bfv(O_B1 + g * 512, 1024),
                                            f32v(O_X + g * 1024, 1024), AF.Identity,
                                            bias=smv(S_LNB + g, 1),
                                            scale=smv(S_LNS + g, 1)), "s", 1)
                lnap_s = P.cnt["s"]

                @blk.vector
                def _(v, t_i=t_i):
                    bnd = bass.AP(AR16.tensor, 2 * O_XLNF + 15,
                                  [[PIT16, 128], [528, 8], [1, 1]])
                    if t_i == 0:
                        P.inc(v.memset(bnd, 0.0), "v", 1)
                    else:
                        P.inc(v.tensor_copy(bnd, xsv_ap()), "v", 1)
                bnd_v = P.cnt["v"]

                @blk.sync
                def _(sy, lnap_s=lnap_s, bnd_v=bnd_v):
                    P.wait(sy, "sy", "s", lnap_s)
                    P.wait(sy, "sy", "v", bnd_v)
                    P.wait(sy, "sy", "gp")
                    for g in range(NG):
                        for dc in range(8):
                            P.inc(sy.dma_start_transpose(
                                out=bass.AP(AR16.tensor,
                                            2 * O_XLNF + dc * 528 + 16 + g * 128,
                                            [[PIT16, 128], [1, 128]]),
                                in_=bass.AP(AR16.tensor,
                                            2 * O_B1 + g * 1024 + dc * 128,
                                            [[PIT16, 128], [1, 128]])), "qxl", 16)
                    P.wait(sy, "sy", "pe")
                    P.wait(sy, "sy", "q1", ffnw_q1)
                    wload(sy, 2 * O_WWIN, 1024, wtr_h, 0, D, 8, 1024, "qwr")
                    P.wait(sy, "sy", "qxl")
                    P.wait(sy, "sy", "qgqt")
                    wload(sy, 2 * O_B1, 1024, wtk_h, 0, D, 4, 1024, "qwk")
                    wload(sy, 2 * O_B8, 1024, wtk_h, 512 * D, D, 4, 1024, "qwk")
                xlnf_q1 = P.cnt["qxl"]
                wr_q1 = P.cnt["qwr"]
                wk_q1 = P.cnt["qwk"]
                if not phase_on("B"):
                    continue

                @blk.vector
                def _(v, xlnf_q1=xlnf_q1):
                    P.wait(v, "v", "qxl", xlnf_q1)
                    last = bass.AP(AR16.tensor, 2 * O_XLNF + 527,
                                   [[PIT16, 128], [528, 8], [1, 1]])
                    v.tensor_copy(xsv_ap(), last)
                    cur = bass.AP(AR16.tensor, 2 * O_XLNF + 16,
                                  [[PIT16, 128], [528, 8], [1, 512]])
                    prv = bass.AP(AR16.tensor, 2 * O_XLNF + 15,
                                  [[PIT16, 128], [528, 8], [1, 512]])
                    P.inc(v.tensor_sub(bfd(O_B4, [[512, 8], [1, 512]]), cur, prv),
                          "v", 1)
                dlt_v = P.cnt["v"]

                def emit_mix(eng, dst_off, mcol, is_gp=False):
                    for dc in range(8):
                        xprev = bass.AP(AR16.tensor, 2 * O_XLNF + dc * 528 + 15,
                                        [[PIT16, 128], [1, 512]])
                        if is_gp:
                            tmp = f32v(O_ESC, 512)
                            eng.tensor_scalar(tmp, bfv(O_B4 + dc * 256, 512),
                                              tm_sb[:, mcol * 8 + dc: mcol * 8 + dc + 1],
                                              None, ALU.mult)
                            ins = eng.tensor_add(bfv(dst_off + dc * 256, 512),
                                                 tmp, xprev)
                        else:
                            ins = eng.scalar_tensor_tensor(
                                bfv(dst_off + dc * 256, 512),
                                bfv(O_B4 + dc * 256, 512),
                                tm_sb[:, mcol * 8 + dc: mcol * 8 + dc + 1],
                                xprev, ALU.mult, ALU.add)
                    return ins

                @blk.vector
                def _(v, dlt_v=dlt_v):
                    v.drain()
                    P.inc(emit_mix(v, O_B2, 0), "v", 1)
                mixr_v = P.cnt["v"]

                @blk.vector
                def _(v):
                    P.inc(emit_mix(v, O_B3, 1), "v", 1)
                mixk_gp = P.cnt["v"]

                def emit_proj(wsrc, mix_off, msem, mval, evac_fn, extra_pe_wait=None):
                    s_entry = P.cnt["s"]
                    for oc in range(8):
                        b = oc % 2

                        @blk.tensor
                        def _(t, oc=oc, b=b, wsrc=wsrc, mix_off=mix_off,
                              msem=msem, mval=mval, extra_pe_wait=extra_pe_wait,
                              s_entry=s_entry):
                            P.wait(t, "pe", msem, mval)
                            P.wait(t, "pe", "qat", attw_q1)
                            if extra_pe_wait is not None:
                                P.wait(t, "pe", *extra_pe_wait)
                            if oc < 2:
                                P.wait(t, "pe", "s", s_entry)
                            else:
                                P.wait(t, "pe", "s", P.cnt["s"] - 1)
                            for kc in range(8):
                                m = t.matmul(
                                    ps[:, b * 512:(b + 1) * 512],
                                    wsrc(kc, oc),
                                    bfv(mix_off + kc * 256, 512),
                                    start=(kc == 0), stop=(kc == 7))
                            P.inc(m, "pe", 1)
                        pm = P.cnt["pe"]

                        @blk.scalar
                        def _(sc, oc=oc, b=b, pm=pm):
                            P.wait(sc, "sc", "pe", pm)
                            P.inc(evac_fn(sc, oc, ps[:, b * 512:(b + 1) * 512]), "s", 1)
                    return P.cnt["s"]

                wsrc_win = lambda kc, oc: bass.AP(AR16.tensor,
                                                  2 * O_WWIN + kc * 1024 + oc * 128,
                                                  [[PIT16, 128], [1, 128]])
                wsrc_win2 = lambda kc, oc: bass.AP(
                    AR16.tensor,
                    2 * (O_B1 if kc < 4 else O_B8) + (kc % 4) * 1024 + oc * 128,
                    [[PIT16, 128], [1, 128]])

                r_done = emit_proj(wsrc_win, O_B2, "v", mixr_v,
                                   lambda sc, oc, pb: sc.activation(
                                       f32v(O_RF + oc * 512, 512), pb, AF.Sigmoid),
                                   extra_pe_wait=("qwr", wr_q1))

                @blk.sync
                def _(sy):
                    P.wait(sy, "sy", "pe")
                    wload(sy, 2 * O_WWIN, 1024, wtv_h, 0, D, 8, 1024, "qwv")
                wv_q1 = P.cnt["qwv"]
                k_done = emit_proj(wsrc_win2, O_B3, "v", mixk_gp,
                                   lambda sc, oc, pb: sc.activation(
                                       f32v(O_KF + oc * 512, 512), pb, AF.Identity),
                                   extra_pe_wait=("qwk", wk_q1))

                @blk.sync
                def _(sy):
                    P.wait(sy, "sy", "pe")
                    wload(sy, 2 * O_B1, 1024, wtw_h, 0, D, 4, 1024, "qww")
                    wload(sy, 2 * O_B8, 1024, wtw_h, 512 * D, D, 4, 1024, "qww")
                ww_q1 = P.cnt["qww"]

                @blk.vector
                def _(v):
                    P.wait(v, "v", "pe")
                    v.drain()
                    P.inc(emit_mix(v, O_B3, 2), "v", 1)
                mixv_gp = P.cnt["v"]


                for g in range(NG):
                    @blk.tensor
                    def _(t, g=g, mixv_gp=mixv_gp, wv_q1=wv_q1):
                        P.wait(t, "pe", "v", mixv_gp)
                        P.wait(t, "pe", "qwv", wv_q1)
                        if g >= 1:
                            P.wait(t, "pe", "v", P.cnt["v"])
                        for half in range(2):
                            b = 2 + half
                            for kc in range(8):
                                m = t.matmul(
                                    ps[:, b * 512:(b + 1) * 512],
                                    bass.AP(AR16.tensor,
                                            2 * O_B3 + kc * 512 + g * 128,
                                            [[PIT16, 128], [1, 128]]),
                                    bass.AP(AR16.tensor,
                                            2 * O_WWIN + kc * 1024 + half * 512,
                                            [[PIT16, 128], [1, 512]]),
                                    start=(kc == 0), stop=(kc == 7))
                        P.inc(m, "pe", 1)
                    vg_pe = P.cnt["pe"]

                    @blk.vector
                    def _(v, g=g, vg_pe=vg_pe):
                        P.wait(v, "v", "pe", vg_pe)
                        P.wait(v, "v", "q1", ffnw_q1)
                        P.inc(v.tensor_copy(bfv(O_B7 + g * 512, 1024),
                                            ps[:, 1024:2048]), "v", 1)
                vtok_v = P.cnt["v"]

                @blk.vector
                def _(v):
                    P.wait(v, "v", "pe")
                    v.drain()
                    P.inc(emit_mix(v, O_B2, 3), "v", 1)
                mixw_v = P.cnt["v"]

                w_done = emit_proj(wsrc_win2, O_B2, "v", mixw_v,
                                   lambda sc, oc, pb: sc.activation(
                                       f32v(O_A5 + oc * 512, 512), pb, AF.Exp,
                                       scale=-1.0),
                                   extra_pe_wait=("qww", ww_q1))
                if not phase_on("C"):
                    continue

                @blk.scalar
                def _(sc, w_done=w_done):
                    P.wait(sc, "sc", "s", w_done)
                    sc.drain()
                    P.inc(sc.activation(f32v(O_A5, 4096), f32v(O_A5, 4096), AF.Ln,
                                        bias=one_c), "s", 1)
                lp_s = P.cnt["s"]

                @blk.vector
                def _(v, lp_s=lp_s):
                    P.wait(v, "v", "s", lp_s)
                    P.wait(v, "v", "pe")
                    v.tensor_scalar(f32v(O_A4 + 1, 4096), f32v(O_A5, 4096), -1.0,
                                    -1e-4, ALU.mult, ALU.add)
                    v.memset(f32v(O_A4, 1), 0.0)
                    v.drain()
                    P.inc(v.tensor_tensor_scan(f32v(O_A4 + 1, 4096), f32v(O_A4 + 1, 4096),
                                               cpool[:, 5:6].broadcast_to((128, 4096)),
                                               0.0, ALU.add, ALU.add), "v", 1)
                    a_end = f32d(O_A4 + 64, [[64, 64]])
                    a_prev = f32d(O_A4, [[64, 64]])
                    v.drain()
                    v.tensor_sub(smv(S_D1, 64), a_end, a_prev)
                    v.drain()
                    P.inc(v.scalar_tensor_tensor(smv(S_GAM, 64), smv(S_D1, 64), 0.5,
                                                 a_prev, ALU.mult, ALU.add), "v", 1)
                scan_v = P.cnt["v"]

                @blk.scalar
                def _(sc, scan_v=scan_v):
                    P.wait(sc, "sc", "v", scan_v)
                    sc.activation(smv(S_EGP, 64), smv(S_D1, 64), AF.Exp, scale=0.5)
                    P.inc(sc.activation(smv(S_E2C, 64), smv(S_D1, 64), AF.Exp), "s", 1)
                egp_s = P.cnt["s"]

                gam_rep = bass.AP(AR.tensor, O_SMALL + S_GAM,
                                  [[PIT, 128], [1, 64], [0, 64]])
                egp_rep = bass.AP(AR.tensor, O_SMALL + S_EGP,
                                  [[PIT, 128], [1, 64], [0, 64]])

                @blk.vector
                def _(v, egp_s=egp_s):
                    P.wait(v, "v", "s", egp_s)
                    P.inc(v.tensor_sub(f32d(O_A5, [[64, 64], [1, 64]]),
                                       f32d(O_A4, [[64, 64], [1, 64]]), gam_rep), "v", 1)
                arg1_v = P.cnt["v"]

                @blk.scalar
                def _(sc, arg1_v=arg1_v):
                    P.wait(sc, "sc", "v", arg1_v)
                    P.inc(sc.activation(f32v(O_A5, 4096), f32v(O_A5, 4096), AF.Exp),
                          "s", 1)
                ea_s = P.cnt["s"]

                @blk.vector
                def _(v, ea_s=ea_s, r_done=r_done):
                    P.wait(v, "v", "s", ea_s)
                    P.wait(v, "v", "s", r_done)
                    P.inc(v.tensor_mul(bfv(O_B4, 4096), f32v(O_RF, 4096),
                                       f32v(O_A5, 4096)), "v", 1)
                rt_v = P.cnt["v"]

                @blk.gpsimd
                def _(gp, rt_v=rt_v):
                    P.wait(gp, "gp", "v", rt_v)
                    P.wait(gp, "gp", "q1", ffnw_q1)
                    P.inc(gp.tensor_tensor(bfd(O_B6, [[64, 64], [1, 64]]),
                                           bfd(O_B4, [[64, 64], [1, 64]]),
                                           egp_rep, ALU.mult), "gp", 1)
                rh_gp = P.cnt["gp"]

                @blk.vector
                def _(v):
                    v.drain()
                    P.inc(v.tensor_sub(f32d(O_A5, [[64, 64], [1, 64]]), gam_rep,
                                       f32d(O_A4 + 1, [[64, 64], [1, 64]])), "v", 1)
                arg2_v = P.cnt["v"]

                @blk.scalar
                def _(sc, arg2_v=arg2_v):
                    P.wait(sc, "sc", "v", arg2_v)
                    P.inc(sc.activation(f32v(O_A5, 4096), f32v(O_A5, 4096), AF.Exp),
                          "s", 1)
                ek_s = P.cnt["s"]

                @blk.vector
                def _(v, ek_s=ek_s, k_done=k_done):
                    P.wait(v, "v", "s", ek_s)
                    P.wait(v, "v", "s", k_done)
                    P.inc(v.tensor_mul(bfv(O_B5, 4096), f32v(O_KF, 4096),
                                       f32v(O_A5, 4096)), "v", 1)
                kt_v = P.cnt["v"]

                @blk.gpsimd
                def _(gp, kt_v=kt_v):
                    P.wait(gp, "gp", "v", kt_v)
                    P.inc(gp.tensor_tensor(bfd(O_B2, [[64, 64], [1, 64]]),
                                           bfd(O_B5, [[64, 64], [1, 64]]),
                                           egp_rep, ALU.mult), "gp", 1)
                kh_gp = P.cnt["gp"]

                for rc in range(NRC):
                    @blk.scalar
                    def _(sc, rc=rc, k_done=k_done):
                        P.wait(sc, "sc", "s", k_done)
                        if rc > 0:
                            P.wait(sc, "sc", "v", P.cnt["v"])
                        P.inc(sc.activation(f32v(O_ESC, 512),
                                            f32v(O_KF + rc * 512, 512), AF.Exp,
                                            bias=u_sb[:, rc:rc + 1]), "s", 1)
                    e_s = P.cnt["s"]

                    @blk.vector
                    def _(v, rc=rc, e_s=e_s, r_done=r_done):
                        P.wait(v, "v", "s", e_s)
                        P.wait(v, "v", "s", r_done)
                        P.inc(v.tensor_mul(bfv(O_B3 + rc * 256, 512),
                                           f32v(O_RF + rc * 512, 512),
                                           f32v(O_ESC, 512)), "v", 1)
                re_v = P.cnt["v"]

                @blk.sync
                def _(sy, kh_gp=kh_gp):
                    P.wait(sy, "sy", "gp", kh_gp)
                    P.wait(sy, "sy", "pe")
                    for rc in range(NRC):
                        for cp in range(4):
                            o = rc * 512 + cp * 128
                            P.inc(sy.dma_start_transpose(
                                out=bass.AP(AR16.tensor, 2 * O_B1 + o,
                                            [[PIT16, 128], [1, 128]]),
                                in_=bass.AP(AR16.tensor, 2 * O_B2 + o,
                                            [[PIT16, 128], [1, 128]])), "qkht", 16)
                kht_q1 = P.cnt["qkht"]
                if not phase_on("D"):
                    continue

                import os
                nch_lim = int(os.environ.get("KCH", NCH))
                dstop = os.environ.get("KDSTOP", "")
                sel_m, oev_m, stup_m = {}, {}, {}
                for c in range(nch_lim):
                    q = c % 2
                    rq = 64 * q
                    scb = 0 if q == 0 else 2

                    @blk.tensor
                    def _(t, c=c, q=q, rq=rq, scb=scb):
                        P.wait(t, "pe", "v", rt_v)
                        P.wait(t, "pe", "v", kt_v)
                        P.wait(t, "pe", "v", re_v)
                        if c >= 2:
                            P.wait(t, "pe", "v", sel_m[c - 2])
                            P.wait(t, "pe", "v", stup_m[c - 2])
                        for rc in range(NRC):
                            co = rc * 512 + c * 64
                            t.matmul(ps[rq:rq + 64, scb * 512 + rc * 64: scb * 512 + rc * 64 + 64],
                                     bass.AP(AR16.tensor, 2 * O_B5 + co,
                                             [[PIT16, 64], [1, 64]]),
                                     bass.AP(AR16.tensor, 2 * O_B4 + co,
                                             [[PIT16, 64], [1, 64]]),
                                     start=True, stop=True, tile_position=(0, rq))
                            t.matmul(ps[rq:rq + 64, (scb + 1) * 512 + rc * 64: (scb + 1) * 512 + rc * 64 + 64],
                                     bass.AP(AR16.tensor, 64 * PIT16 + 2 * O_B5 + co,
                                             [[PIT16, 64], [1, 64]]),
                                     bass.AP(AR16.tensor, 64 * PIT16 + 2 * O_B4 + co,
                                             [[PIT16, 64], [1, 64]]),
                                     start=True, stop=True, tile_position=(64, rq))
                        if c >= 1:
                            P.wait(t, "pe", "v", sel_m[c - 1])
                        for rc in range(NRC):
                            co = rc * 512 + c * 64
                            t.matmul(ps[rq:rq + 64, 4 * 512 + rc * 64: 4 * 512 + rc * 64 + 64],
                                     bdA[:, 0:64],
                                     bass.AP(AR16.tensor, 2 * O_B3 + co,
                                             [[PIT16, 128], [1, 64]]),
                                     start=True, stop=True, tile_position=(0, rq))
                            m = t.matmul(ps[rq:rq + 64, 5 * 512 + rc * 64: 5 * 512 + rc * 64 + 64],
                                         bdB[:, 0:64],
                                         bass.AP(AR16.tensor, 2 * O_B3 + co,
                                                 [[PIT16, 128], [1, 64]]),
                                         start=True, stop=True, tile_position=(0, rq))
                        P.inc(m, "pe", 1)
                    sc_pe = P.cnt["pe"]
                    if dstop == "scores":
                        sel_m[c] = P.cnt["v"]; oev_m[c] = P.cnt["s"]; stup_m[c] = P.cnt["v"]
                        continue

                    @blk.vector
                    def _(v, c=c, rq=rq, scb=scb, sc_pe=sc_pe):
                        P.wait(v, "v", "pe", sc_pe)
                        sfv = bass.AP(AR16.tensor, rq * PIT16 + 2 * O_SF,
                                      [[PIT16, 64], [1, 1024]])
                        psc = bass.AP(ps.ap().tensor, rq * 4096 + scb * 512,
                                      [[4096, 64], [1, 1024]])
                        pdd = bass.AP(ps.ap().tensor, rq * 4096 + 4 * 512,
                                      [[4096, 64], [1, 1024]])
                        v.tensor_tensor(sfv, psc, mstrf[rq:rq + 64, :], ALU.mult)
                        v.drain()
                        P.inc(v.select(sfv, mdiaf[rq:rq + 64, :], pdd, sfv,
                                       add_drain=True), "v", 1)
                    sel_m[c] = P.cnt["v"]
                    if dstop == "sel":
                        oev_m[c] = P.cnt["s"]; stup_m[c] = P.cnt["v"]
                        continue

                    @blk.tensor
                    def _(t, c=c, q=q, rq=rq, scb=scb):
                        P.wait(t, "pe", "v", sel_m[c])
                        P.wait(t, "pe", "v", vtok_v)
                        P.wait(t, "pe", "qkht", kht_q1)
                        if c >= 1:
                            P.wait(t, "pe", "v", oev_m[c - 1])
                            P.wait(t, "pe", "v", stup_m[c - 1])
                        g = c // 2
                        for rc in range(NRC):
                            vA = bass.AP(AR16.tensor,
                                         rq * PIT16 + 2 * O_B7 + g * 1024 + rc * 128,
                                         [[PIT16, 64], [1, 64]])
                            vB = bass.AP(AR16.tensor,
                                         rq * PIT16 + 2 * O_B7 + g * 1024 + rc * 128 + 64,
                                         [[PIT16, 64], [1, 64]])
                            sfA = bass.AP(AR16.tensor, rq * PIT16 + 2 * O_SF + rc * 64,
                                          [[PIT16, 64], [1, 64]])
                            sfB = bass.AP(AR16.tensor,
                                          rq * PIT16 + 2 * O_SF + 512 + rc * 64,
                                          [[PIT16, 64], [1, 64]])
                            t.matmul(ps[0:64, 6 * 512 + rc * 64: 6 * 512 + rc * 64 + 64],
                                     state[0:64, rc * 64: rc * 64 + 64],
                                     bass.AP(AR16.tensor, 2 * O_B6 + rc * 512 + c * 64,
                                             [[PIT16, 64], [1, 64]]),
                                     start=True, stop=True, tile_position=(0, 0))
                            t.matmul(ps[0:64, 7 * 512 + rc * 64: 7 * 512 + rc * 64 + 64],
                                     vA, sfA, start=True, stop=True,
                                     tile_position=(rq, 0))
                            t.matmul(ps[64:128, 6 * 512 + rc * 64: 6 * 512 + rc * 64 + 64],
                                     state[64:128, rc * 64: rc * 64 + 64],
                                     bass.AP(AR16.tensor,
                                             64 * PIT16 + 2 * O_B6 + rc * 512 + c * 64,
                                             [[PIT16, 64], [1, 64]]),
                                     start=True, stop=True, tile_position=(64, 64))
                            t.matmul(ps[64:128, 7 * 512 + rc * 64: 7 * 512 + rc * 64 + 64],
                                     vB, sfB, start=True, stop=True,
                                     tile_position=(rq, 64))
                            kA = bass.AP(AR16.tensor,
                                         rq * PIT16 + 2 * O_B1 + rc * 512 + (c // 2) * 128,
                                         [[PIT16, 64], [1, 64]])
                            kB = bass.AP(AR16.tensor,
                                         rq * PIT16 + 2 * O_B1 + rc * 512 + (c // 2) * 128 + 64,
                                         [[PIT16, 64], [1, 64]])
                            t.matmul(ps[0:64, scb * 512 + rc * 64: scb * 512 + rc * 64 + 64],
                                     kA, vA, start=True, stop=True,
                                     tile_position=(rq, 0))
                            m = t.matmul(ps[64:128, scb * 512 + rc * 64: scb * 512 + rc * 64 + 64],
                                         kB, vB, start=True, stop=True,
                                         tile_position=(rq, 64))
                        P.inc(m, "pe", 1)
                    av_pe = P.cnt["pe"]
                    if dstop == "av":
                        oev_m[c] = P.cnt["s"]; stup_m[c] = P.cnt["v"]
                        continue

                    @blk.scalar
                    def _(sc, c=c, av_pe=av_pe):
                        P.wait(sc, "sc", "pe", av_pe)
                        srcC = bass.AP(ps.ap().tensor, 6 * 512,
                                       [[4096, 128], [64, 8], [1, 64]])
                        P.inc(sc.activation(f32v(O_TS, 512), srcC, AF.Identity),
                              "s", 1)
                    oevs = P.cnt["s"]

                    @blk.vector
                    def _(v, c=c, av_pe=av_pe, oevs=oevs):
                        P.wait(v, "v", "pe", av_pe)
                        P.wait(v, "v", "s", oevs)
                        dst = bass.AP(AR16.tensor, 2 * (O_B8) + c * 64,
                                      [[PIT16, 128], [512, 8], [1, 64]])
                        srcI = bass.AP(ps.ap().tensor, 7 * 512,
                                       [[4096, 128], [64, 8], [1, 64]])
                        P.inc(v.tensor_add(dst, f32v(O_TS, 512), srcI), "v", 1)
                    oev_m[c] = P.cnt["v"]

                    @blk.vector
                    def _(v, c=c, av_pe=av_pe, scb=scb):
                        P.wait(v, "v", "pe", av_pe)
                        e2c_rep = bass.AP(AR.tensor, O_SMALL + S_E2C + c,
                                          [[PIT, 128], [8, 8], [0, 64]])
                        v.drain()
                        v.tensor_tensor(f32d(O_A5, [[64, 8], [1, 64]]),
                                        state[:].rearrange("p (a b) -> p a b", a=8),
                                        e2c_rep, ALU.mult)
                        v.drain()
                        P.inc(v.tensor_add(state[:], f32v(O_A5, 512),
                                           ps[:, scb * 512: (scb + 1) * 512]), "v", 1)
                    stup_m[c] = P.cnt["v"]

                @blk.sync
                def _(sy):
                    P.wait(sy, "sy", "pe")
                    wload(sy, 2 * O_B1, 1024, wto_h, 0, D, 8, 1024, "qwo")
                wo_q1 = P.cnt["qwo"]
                if not phase_on("E"):
                    continue

                gst_done = None
                for rc in range(NRC):
                    @blk.scalar
                    def _(sc, rc=rc):
                        P.wait(sc, "sc", "v", oev_m[nch_lim - 1])
                        if rc > 0:
                            P.wait(sc, "sc", "pe", P.cnt["pe"])
                        P.inc(sc.activation(bass.AP(AR16.tensor, 2 * O_SF,
                                                    [[PIT16, 128], [1, 512]]),
                                            bfv(O_B8 + rc * 256, 512), AF.Square),
                              "s", 1)
                    o2_s = P.cnt["s"]

                    @blk.tensor
                    def _(t, rc=rc, o2_s=o2_s):
                        P.wait(t, "pe", "s", o2_s)
                        P.wait(t, "pe", "v", sel_m[nch_lim - 1])
                        P.wait(t, "pe", "v", stup_m[nch_lim - 1])
                        b1 = rc // 4
                        pb = 32 * (rc % 4)
                        t.matmul(ps[pb:pb + 2, b1 * 512:(b1 + 1) * 512],
                                 hsel[:, 0:2], bfv(O_B8 + rc * 256, 512),
                                 start=True, stop=True, tile_position=(0, pb))
                        m = t.matmul(ps[pb:pb + 2, (2 + b1) * 512:(3 + b1) * 512],
                                     hsel[:, 0:2],
                                     bass.AP(AR16.tensor, 2 * O_SF,
                                             [[PIT16, 128], [1, 512]]),
                                     start=True, stop=True, tile_position=(0, pb))
                        P.inc(m, "pe", 1)
                gst_pe = P.cnt["pe"]

                og_v = None
                for b1 in range(2):
                    @blk.vector
                    def _(v, b1=b1, gst_pe=gst_pe):
                        P.wait(v, "v", "pe", gst_pe)
                        mu = f32v(O_SM, 512)
                        v.tensor_scalar(mu, ps[:, b1 * 512:(b1 + 1) * 512], 1.0 / DH,
                                        1e6, ALU.mult, ALU.min)
                        v.drain()
                        v.tensor_scalar_max(mu, mu, -1e6)
                        v.drain()
                        v.tensor_mul(f32v(O_A5, 512), mu, mu)
                        v.drain()
                        P.inc(v.scalar_tensor_tensor(f32v(O_SM + 512, 512),
                                                     ps[:, (2 + b1) * 512:(3 + b1) * 512],
                                                     1.0 / DH, f32v(O_A5, 512),
                                                     ALU.mult, ALU.subtract), "v", 1)
                    gv_v = P.cnt["v"]

                    @blk.vector
                    def _(v, b1=b1, gv_v=gv_v):
                        v.drain()
                        P.inc(v.tensor_scalar(f32v(O_SM + 512, 512),
                                              f32v(O_SM + 512, 512), 0.0, 1e18,
                                              ALU.max, ALU.min), "v", 1)
                    gvc_v = P.cnt["v"]

                    @blk.scalar
                    def _(sc, b1=b1, gvc_v=gvc_v):
                        P.wait(sc, "sc", "v", gvc_v)
                        sc.activation(f32v(O_A5, 512), f32v(O_SM + 512, 512), AF.Ln,
                                      bias=eps_c)
                        sc.drain()
                        P.inc(sc.activation(f32v(O_SM + 512, 512), f32v(O_A5, 512),
                                            AF.Exp, scale=-0.5), "s", 1)
                    gs_s = P.cnt["s"]

                    @blk.vector
                    def _(v, b1=b1, gs_s=gs_s):
                        P.wait(v, "v", "s", gs_s)
                        P.inc(v.tensor_mul(f32v(O_SM, 512), f32v(O_SM, 512),
                                           f32v(O_SM + 512, 512)), "v", 1)
                    gm2_v = P.cnt["v"]

                    for rr in range(4):
                        rc = b1 * 4 + rr
                        pb = 32 * rr

                        @blk.tensor
                        def _(t, rc=rc, pb=pb, gm2_v=gm2_v):
                            P.wait(t, "pe", "v", gm2_v)
                            if rc >= 2:
                                P.wait(t, "pe", "v", P.cnt["v"])
                            bs = 4 + (rc % 2)
                            bm = 6 + (rc % 2)
                            t.matmul(ps[:, bs * 512:(bs + 1) * 512], sel2f[pb:pb + 2, :],
                                     f32v(O_SM + 512, 512, pb, 2),
                                     start=True, stop=True, tile_position=(pb, 0))
                            m = t.matmul(ps[:, bm * 512:(bm + 1) * 512],
                                         sel2f[pb:pb + 2, :], f32v(O_SM, 512, pb, 2),
                                         start=True, stop=True, tile_position=(pb, 0))
                            P.inc(m, "pe", 1)
                        gb_pe = P.cnt["pe"]

                        @blk.vector
                        def _(v, rc=rc, gb_pe=gb_pe):
                            P.wait(v, "v", "pe", gb_pe)
                            v.drain()
                            bs = 4 + (rc % 2)
                            bm = 6 + (rc % 2)
                            v.tensor_mul(f32v(O_A5, 512), bfv(O_B8 + rc * 256, 512),
                                         ps[:, bs * 512:(bs + 1) * 512])
                            v.drain()
                            P.inc(v.tensor_sub(f32v(O_A5 + 512, 512), f32v(O_A5, 512),
                                               ps[:, bm * 512:(bm + 1) * 512]), "v", 1)
                        ga_v = P.cnt["v"]

                        @blk.scalar
                        def _(sc, rc=rc, ga_v=ga_v):
                            P.wait(sc, "sc", "v", ga_v)
                            P.inc(sc.activation(f32v(O_A5, 512), f32v(O_A5 + 512, 512),
                                                AF.Identity, bias=lnwb[:, 1:2],
                                                scale=lnwb[:, 0:1]), "s", 1)
                        gf_s = P.cnt["s"]

                        @blk.vector
                        def _(v, rc=rc, gf_s=gf_s):
                            P.wait(v, "v", "s", gf_s)
                            P.inc(v.tensor_mul(bfv(O_B8 + rc * 256, 512),
                                               f32v(O_A5, 512),
                                               f32v(O_RF + rc * 512, 512)), "v", 1)
                og_v = P.cnt["v"]

                for g in range(NG):
                    @blk.tensor
                    def _(t, g=g, og_v=og_v, wo_q1=wo_q1):
                        P.wait(t, "pe", "v", og_v)
                        P.wait(t, "pe", "qwo", wo_q1)
                        if g >= 1:
                            P.wait(t, "pe", "v", P.cnt["v"])
                        for half in range(2):
                            b = half
                            for kc in range(8):
                                m = t.matmul(
                                    ps[:, b * 512:(b + 1) * 512],
                                    bass.AP(AR16.tensor,
                                            2 * O_B8 + kc * 512 + g * 128,
                                            [[PIT16, 128], [1, 128]]),
                                    bass.AP(AR16.tensor,
                                            2 * O_B1 + kc * 1024 + half * 512,
                                            [[PIT16, 128], [1, 512]]),
                                    start=(kc == 0), stop=(kc == 7))
                        P.inc(m, "pe", 1)
                    op_pe = P.cnt["pe"]

                    @blk.vector
                    def _(v, g=g, op_pe=op_pe):
                        P.wait(v, "v", "pe", op_pe)
                        P.inc(v.tensor_add(f32v(O_X + g * 1024, 1024),
                                           f32v(O_X + g * 1024, 1024),
                                           ps[:, 0:1024]), "v", 1)
                x1_v = P.cnt["v"]

                @blk.vector
                def _(v, x1_v=x1_v):
                    P.wait(v, "v", "v", x1_v)
                    for g in range(NG):
                        xg = f32v(O_X + g * 1024, 1024)
                        v.reduce_sum(smv(S_ST1 + g, 1), xg, axis=AX.X)
                        v.scalar_tensor_tensor(f32v(O_A5 + g * 1024, 1024), xg, 1.0, xg,
                                               ALU.mult, ALU.mult,
                                               accum_out=smv(S_ST2 + g, 1))
                    v.drain()
                    v.tensor_scalar(smv(S_LNB, NG), smv(S_ST1, NG), 1.0 / D, None,
                                    ALU.mult)
                    v.drain()
                    v.tensor_mul(smv(S_TINY, NG), smv(S_LNB, NG), smv(S_LNB, NG))
                    v.drain()
                    P.inc(v.scalar_tensor_tensor(smv(S_LNS, NG), smv(S_ST2, NG),
                                                 1.0 / D, smv(S_TINY, NG),
                                                 ALU.mult, ALU.subtract), "v", 1)
                l2st_v = P.cnt["v"]

                @blk.scalar
                def _(sc, l2st_v=l2st_v):
                    P.wait(sc, "sc", "v", l2st_v)
                    sc.activation(smv(S_TINY, NG), smv(S_LNS, NG), AF.Ln, bias=eps_c)
                    sc.drain()
                    P.inc(sc.activation(smv(S_LNS, NG), smv(S_TINY, NG), AF.Exp,
                                        scale=-0.5), "s", 1)

                @blk.vector
                def _(v):
                    P.wait(v, "v", "s")
                    P.inc(v.scalar_tensor_tensor(smv(S_LNB, NG), smv(S_LNB, NG), -1.0,
                                                 smv(S_LNS, NG), ALU.mult, ALU.mult),
                          "v", 1)
                l2ab_v = P.cnt["v"]

                tq_h = []
                for p in range(2):
                    ga, gb = 2 * p, 2 * p + 1

                    @blk.scalar
                    def _(sc, p=p, ga=ga, gb=gb, l2ab_v=l2ab_v):
                        P.wait(sc, "sc", "v", l2ab_v)
                        if p > 0:
                            P.wait(sc, "sc", "v", P.cnt["v"])
                        for i, g in enumerate((ga, gb)):
                            P.inc(sc.activation(f32v(O_A5 + i * 1024, 1024),
                                                f32v(O_X + g * 1024, 1024),
                                                AF.Identity,
                                                bias=smv(S_LNB + g, 1),
                                                scale=smv(S_LNS + g, 1)), "s", 1)
                    h_s = P.cnt["s"]

                    @blk.vector
                    def _(v, p=p, ga=ga, gb=gb, h_s=h_s):
                        P.wait(v, "v", "s", h_s)
                        for i, g in enumerate((ga, gb)):
                            v.reduce_max(smv(S_MAX + g, 1),
                                         f32v(O_A5 + i * 1024, 1024), axis=AX.X,
                                         apply_absolute_value=True)
                        v.drain()
                        v.tensor_scalar_max(smv(S_MAX + ga, 2), smv(S_MAX + ga, 2),
                                            1e-5)
                        v.drain()
                        v.reciprocal(smv(S_TINY + 4, 2), smv(S_MAX + ga, 2))
                        v.drain()
                        v.tensor_scalar(smv(S_TINY + 6, 2), smv(S_TINY + 4, 2),
                                        127.0, None, ALU.mult)
                        for g in (ga, gb):
                            v.tensor_mul(smv(S_DQ + g, 1), smv(S_MAX + g, 1),
                                         smv(S_TINY2, 1))
                            v.tensor_mul(smv(S_DQ2 + g, 1), smv(S_MAX + g, 1),
                                         smv(S_TINY2 + 1, 1))
                        v.drain()
                        P.inc(v.engine_nop(), "v", 1)
                    hm_v = P.cnt["v"]

                    @blk.scalar
                    def _(sc, p=p, ga=ga, gb=gb, hm_v=hm_v):
                        P.wait(sc, "sc", "v", hm_v)
                        P.wait(sc, "sc", "gp", P.cnt["gp"])
                        P.wait(sc, "sc", "pe")
                        for i in range(2):
                            P.inc(sc.activation(f32v(O_A4 + i * 1024, 1024),
                                                f32v(O_A5 + i * 1024, 1024),
                                                AF.Identity,
                                                scale=smv(S_TINY + 6 + i, 1),
                                                bias=cpool[:, 8:9]), "s", 1)
                    hv_s = P.cnt["s"]

                    @blk.vector
                    def _(v, p=p, ga=ga, gb=gb, hv_s=hv_s):
                        P.wait(v, "v", "s", hv_s)
                        Y0 = [f32v(O_A4, 1024), f32v(O_A4 + 1024, 1024)]
                        YF = [f32v(O_A5 + 2048, 1024), f32v(O_A5 + 3072, 1024)]
                        BF = [f32v(O_A4 + 2048, 1024), f32v(O_A4 + 3072, 1024)]
                        for i in range(2):
                            v.tensor_scalar(BF[i], Y0[i], 128.0, -128.0,
                                            ALU.is_ge, ALU.mult)
                        v.drain()
                        for i in range(2):
                            v.tensor_add(YF[i], Y0[i], BF[i])
                        for m in (64.0, 32.0, 16.0, 8.0, 4.0, 2.0):
                            v.drain()
                            for i in range(2):
                                v.tensor_scalar(BF[i], YF[i], m, -m, ALU.is_ge,
                                                ALU.mult)
                            v.drain()
                            for i in range(2):
                                v.tensor_add(YF[i], YF[i], BF[i])
                        v.drain()
                        for i in range(2):
                            v.scalar_tensor_tensor(BF[i], YF[i], 1.0, YF[i],
                                                   ALU.is_ge, ALU.subtract)
                        v.drain()
                        for i, g in enumerate((ga, gb)):
                            P.inc(v.scalar_tensor_tensor(
                                bfv(O_B4 + g * 512, 1024), Y0[i], -128.0, BF[i],
                                ALU.add, ALU.add), "v", 1)
                    lad_h = P.cnt["v"]

                    for g in (ga, gb):
                        @blk.sync
                        def _(sy, g=g, lad_h=lad_h):
                            P.wait(sy, "sy", "v", lad_h)
                            if g == 0:
                                P.wait(sy, "sy", "pe")
                            P.inc(sy.dma_start_transpose(
                                out=bass.AP(AR16.tensor, 2 * O_B6 + g * 128,
                                            [[PIT16, 128], [512, 8], [1, 128]]),
                                in_=bass.AP(AR16.tensor, 2 * O_B4 + g * 1024,
                                            [[PIT16, 128], [128, 8], [1, 128]])),
                                "qht%d" % g, 16)
                        tq_h.append(P.cnt["qht%d" % g])
                hq_v = P.cnt["v"]
                if not phase_on("F"):
                    continue

                win_of = [(O_B7, O_B8), (O_B3, O_B4)]
                for fc in range(8):
                    wgo, wuo = win_of[fc % 2]
                    qwin = "qwinA" if fc % 2 == 0 else "qwinB"

                    @blk.sync
                    def _(sy, fc=fc, wgo=wgo, wuo=wuo, qwin=qwin):
                        P.wait(sy, "sy", "q1", ffnw_q1)
                        P.wait(sy, "sy", "pe")
                        wload(sy, 2 * wgo, 512, wqtg_h, fc * 512, FF, 8, 512,
                              qwin)
                        wload(sy, 2 * wuo, 512, wqtu_h, fc * 512, FF, 8, 512,
                              qwin)
                    win_q1 = P.cnt[qwin]

                    for g in range(NG):
                        @blk.tensor
                        def _(t, fc=fc, g=g, wgo=wgo, wuo=wuo, win_q1=win_q1,
                              tqh=tq_h[g], qwin=qwin):
                            P.wait(t, "pe", qwin, win_q1)
                            P.wait(t, "pe", "qht%d" % g, tqh)
                            bg = g % 2
                            if fc + g >= 1:
                                P.wait(t, "pe", "v", P.cnt["v"])
                            for kc in range(8):
                                t.matmul(ps[:, bg * 512:(bg + 1) * 512],
                                         bass.AP(AR16.tensor,
                                                 2 * O_B6 + kc * 512 + g * 128,
                                                 [[PIT16, 128], [1, 128]]),
                                         bass.AP(AR16.tensor, 2 * wgo + kc * 512, [[PIT16, 128], [1, 512]]),
                                         start=(kc == 0), stop=(kc == 7))
                            for kc in range(8):
                                m = t.matmul(ps[:, (2 + bg) * 512:(3 + bg) * 512],
                                             bass.AP(AR16.tensor,
                                                     2 * O_B6 + kc * 512 + g * 128,
                                                     [[PIT16, 128], [1, 128]]),
                                             bass.AP(AR16.tensor, 2 * wuo + kc * 512, [[PIT16, 128], [1, 512]]),
                                             start=(kc == 0), stop=(kc == 7))
                            P.inc(m, "pe", 1)
                        gu_pe = P.cnt["pe"]

                        @blk.scalar
                        def _(sc, g=g, gu_pe=gu_pe):
                            P.wait(sc, "sc", "pe", gu_pe)
                            bg = g % 2
                            P.inc(sc.activation(f32v(O_TS, 512),
                                                ps[:, bg * 512:(bg + 1) * 512],
                                                AF.Silu, scale=smv(S_DQ + g, 1)),
                                  "s", 1)
                        si_s = P.cnt["s"]

                        @blk.vector
                        def _(v, fc=fc, g=g, si_s=si_s):
                            P.wait(v, "v", "s", si_s)
                            bg = g % 2
                            P.inc(v.scalar_tensor_tensor(
                                f32v(GBIG[g] + fc * 512, 512),
                                ps[:, (2 + bg) * 512:(3 + bg) * 512],
                                smv(S_DQ2 + g, 1), f32v(O_TS, 512),
                                ALU.mult, ALU.mult), "v", 1)
                gu_done_v = P.cnt["v"]

                @blk.sync
                def _(sy):
                    P.wait(sy, "sy", "pe")
                    for qr in range(NG):
                        wload(sy, 2 * WDQ[qr], 512, wqtd_h, qr * 1024 * D,
                              D, 8, 512, "qwd", pieces=1)
                wdw_h0 = P.cnt["qwd"]

                GT = [O_B6, O_B1, O_B2, O_B3]
                tq_g = []

                for g in range(NG):
                    @blk.vector
                    def _(v, g=g, gu_done_v=gu_done_v):
                        P.wait(v, "v", "v", gu_done_v)
                        v.reduce_max(smv(S_MAX + g, 1), f32v(GBIG[g], 4096), axis=AX.X,
                                     apply_absolute_value=True)
                        v.drain()
                        v.tensor_scalar_max(smv(S_MAX + g, 1), smv(S_MAX + g, 1),
                                            1e-5)
                        v.drain()
                        v.reciprocal(smv(S_TINY + 4, 1), smv(S_MAX + g, 1))
                        v.drain()
                        v.tensor_scalar(smv(S_TINY + 5, 1), smv(S_TINY + 4, 1),
                                        127.0, None, ALU.mult)
                        v.tensor_mul(smv(S_DQ + g, 1), smv(S_MAX + g, 1),
                                     smv(S_TINY2 + 2, 1))
                        v.drain()
                        P.inc(v.engine_nop(), "v", 1)
                    gm_v = P.cnt["v"]
                    gsc_s = P.cnt["s"]

                    for hf in range(2):
                        @blk.scalar
                        def _(sc, g=g, hf=hf, gm_v=gm_v):
                            if hf == 0:
                                P.wait(sc, "sc", "v", gm_v)
                                P.wait(sc, "sc", "gp", P.cnt["gp"])
                            sc.drain()
                            P.inc(sc.activation(f32v(GBIG[g] + hf * 2048, 2048),
                                                f32v(GBIG[g] + hf * 2048, 2048),
                                                AF.Identity, scale=smv(S_TINY + 5, 1),
                                                bias=cpool[:, 8:9]), "s", 1)
                    gv_s = P.cnt["s"]

                    @blk.vector
                    def _(v, g=g, gv_s=gv_s):
                        P.wait(v, "v", "s", gv_s)
                        Y0 = [f32v(GBIG[g], 2048), f32v(GBIG[g] + 2048, 2048)]
                        YF = [f32v(O_A5, 2048), f32v(O_A5 + 2048, 2048)]
                        BF = [f32v(O_A4, 2048), f32v(O_A4 + 2048, 2048)]
                        for h in range(2):
                            v.tensor_scalar(BF[h], Y0[h], 128.0, -128.0,
                                            ALU.is_ge, ALU.mult)
                        v.drain()
                        for h in range(2):
                            v.tensor_add(YF[h], Y0[h], BF[h])
                        for m in (64.0, 32.0, 16.0, 8.0, 4.0, 2.0):
                            v.drain()
                            for h in range(2):
                                v.tensor_scalar(BF[h], YF[h], m, -m, ALU.is_ge,
                                                ALU.mult)
                            v.drain()
                            for h in range(2):
                                v.tensor_add(YF[h], YF[h], BF[h])
                        v.drain()
                        for h in range(2):
                            v.scalar_tensor_tensor(BF[h], YF[h], 1.0, YF[h],
                                                   ALU.is_ge, ALU.subtract)
                        v.drain()
                        for h in range(2):
                            P.inc(v.scalar_tensor_tensor(
                                bass.AP(AR16.tensor, 2 * GQB[g] + h * 2048,
                                        [[PIT16, 128], [1, 2048]]), Y0[h],
                                -128.0, BF[h], ALU.add, ALU.add),
                                "v", 1)
                    lad_v = P.cnt["v"]

                    @blk.sync
                    def _(sy, g=g, lad_v=lad_v):
                        P.wait(sy, "sy", "v", lad_v)
                        if g == 0:
                            P.wait(sy, "sy", "pe")
                        else:
                            P.wait(sy, "sy", "qgqt", P.cnt["qgqt"])
                        P.inc(sy.dma_start_transpose(
                            out=bass.AP(AR16.tensor, 2 * GT[g],
                                        [[PIT16, 128], [128, 32], [1, 128]]),
                            in_=bass.AP(AR16.tensor, 2 * GQB[g],
                                        [[PIT16, 128], [128, 32], [1, 128]])),
                            "qgqt", 16)
                    tq_g.append(P.cnt["qgqt"])

                for half in range(2):
                    if half == 0:
                        wdw_q1 = wdw_h0
                    else:
                        @blk.sync
                        def _(sy):
                            P.wait(sy, "sy", "pe")
                            for qr in range(NG):
                                wload(sy, 2 * WDQ[qr], 512, wqtd_h,
                                      qr * 1024 * D + 512, D, 8, 512, "qwd")
                        wdw_q1 = P.cnt["qwd"]

                    for g in range(NG):
                        @blk.tensor
                        def _(t, g=g, half=half, tq=tq_g[g], wdw_q1=wdw_q1):
                            P.wait(t, "pe", "qgqt", tq)
                            P.wait(t, "pe", "qwd", wdw_q1)
                            if g >= 2:
                                P.wait(t, "pe", "v", P.cnt["v"])
                            b = 4 + (g % 2) * 2 + half
                            for kc in range(32):
                                m = t.matmul(
                                    ps[:, b * 512:(b + 1) * 512],
                                    bass.AP(AR16.tensor, 2 * GT[g] + kc * 128,
                                            [[PIT16, 128], [1, 128]]),
                                    bass.AP(AR16.tensor,
                                            2 * WDQ[kc // 8] + (kc % 8) * 512,
                                            [[PIT16, 128], [1, 512]]),
                                    start=(kc == 0), stop=(kc == 31))
                            P.inc(m, "pe", 1)
                        dn_pe = P.cnt["pe"]

                        @blk.vector
                        def _(v, g=g, half=half, dn_pe=dn_pe):
                            P.wait(v, "v", "pe", dn_pe)
                            b = 4 + (g % 2) * 2 + half
                            P.inc(v.scalar_tensor_tensor(
                                f32v(O_X + g * 1024 + half * 512, 512),
                                ps[:, b * 512:(b + 1) * 512],
                                smv(S_DQ + g, 1),
                                f32v(O_X + g * 1024 + half * 512, 512),
                                ALU.mult, ALU.add), "v", 1)
                out_v = P.cnt["v"]
                if not phase_on("G"):
                    continue

                @blk.sync
                def _(sy, s_i=s_i, t_i=t_i, tok0=tok0, out_v=out_v):
                    P.wait(sy, "sy", "v", out_v)
                    dst = bass.AP(out_h.tensor, s_i * T * D + tok0 * D,
                                  [[D, 128], [128 * D, NG], [1, D]])
                    P.inc(sy.dma_start(out=dst, in_=f32d(O_X, [[1024, NG], [1, 1024]])),
                          "st", 16)

        if dbg_h:
            @blk.sync
            def _(sy):
                P.wait(sy, "sy", "v")
                P.wait(sy, "sy", "s")
                P.wait(sy, "sy", "gp")
                P.wait(sy, "sy", "pe")
                for n, ap in dbg_h.items():
                    kind, off = n.split("_", 1)
                    off = int(off)
                    cols = ap.ap[-1][1] if len(ap.ap) >= 2 else 1
                    nrow = ap.ap[0][1]
                    if kind == "f":
                        src = f32v(off, cols, 0, nrow)
                    elif kind == "b":
                        src = bfv(off, cols, 0, nrow)
                    else:
                        src = i32v(off, cols, 0, nrow)
                    P.inc(sy.dma_start(out=ap, in_=src), "st", 16)

        @blk.sync
        def _(sy):
            P.wait(sy, "sy", "st")

    return nc


def _prep_inputs(ip):
    x = np.asarray(ip["L0_attn_x"], np.float32)
    u = np.asarray(ip["L0_attn_u"], np.float32)
    u_p = np.zeros((128, 8), np.float32)
    for rc in range(8):
        u_p[0:64, rc] = u[2 * rc]
        u_p[64:128, rc] = u[2 * rc + 1]
    tm = np.zeros((128, 32), np.float32)
    for m, nm in enumerate(["tm_r", "tm_k", "tm_v", "tm_w"]):
        t = np.asarray(ip["L0_attn_" + nm], np.float32)
        for dc in range(8):
            tm[:, m * 8 + dc] = t[dc * 128:(dc + 1) * 128]
    lnwb = np.zeros((128, 2), np.float32)
    lnwb[:, 0] = np.tile(np.asarray(ip["L0_attn_ln_w"], np.float32), 2)
    lnwb[:, 1] = np.tile(np.asarray(ip["L0_attn_ln_b"], np.float32), 2)
    pp = np.arange(128) % 64
    cc = np.arange(64)
    mstrict = (cc[None, :] > pp[:, None]).astype(np.float32)
    mdiag = (cc[None, :] == pp[:, None]).astype(np.uint8)
    mstrf = np.tile(mstrict, (1, 16)).astype(ml_dtypes.bfloat16)
    mdiaf = np.tile(mdiag, (1, 16)).astype(np.uint8)
    bdA = np.zeros((128, 64), ml_dtypes.bfloat16)
    bdA[0:64, :] = 1
    bdB = np.zeros((128, 64), ml_dtypes.bfloat16)
    bdB[64:128, :] = 1
    sel2 = np.zeros((128, 128), ml_dtypes.bfloat16)
    for qq in range(4):
        sel2[32 * qq, 0:64] = 1
        sel2[32 * qq + 1, 64:128] = 1
    sel2f = sel2.astype(np.float32)
    hsel = np.zeros((128, 2), ml_dtypes.bfloat16)
    hsel[0:64, 0] = 1
    hsel[64:128, 1] = 1
    onesf = np.ones((128, 128), np.float32)
    cpool = np.zeros((128, 10), np.float32)
    cpool[:, 8] = 128.5
    cpool[:, 0] = 0.5
    cpool[:, 1] = -0.5
    cpool[:, 2] = 1e-5
    cpool[:, 3] = -LN127
    cpool[:, 4] = 1.0
    cpool[:, 5] = 0.0
    cpool[:, 6] = LN127
    cpool[:, 7] = -1.0
    base = {
        "wr": np.asarray(ip["L0_attn_W_r"], np.float32),
        "wk": np.asarray(ip["L0_attn_W_k"], np.float32),
        "wv": np.asarray(ip["L0_attn_W_v"], np.float32),
        "ww": np.asarray(ip["L0_attn_W_w"], np.float32),
        "wo": np.asarray(ip["L0_attn_W_o"], np.float32),
        "wg": np.asarray(ip["L0_ffn_w_gate"], np.float32),
        "wu": np.asarray(ip["L0_ffn_w_up"], np.float32),
        "wd": np.asarray(ip["L0_ffn_w_down"], np.float32),
        "u_p": u_p, "tm": tm, "lnwb": lnwb, "mstrf": mstrf, "mdiaf": mdiaf,
        "bdA": bdA, "bdB": bdB, "sel2f": sel2f, "hsel": hsel,
        "onesf": onesf, "cpool": cpool,
    }
    return [dict(base, x=np.ascontiguousarray(x[2 * c:2 * c + 2])) for c in range(8)]


def kernel(**inputs):
    in_maps = _prep_inputs(inputs)
    nc = build()
    res = run_bass_kernel_spmd(nc, in_maps, core_ids=list(range(8)))
    return np.concatenate([r["out"] for r in res.results], axis=0).astype(np.float32)


# version 19 (final)
# speedup vs baseline: 1.0747x; 1.0376x over previous
import contextlib
import numpy as np
import ml_dtypes

import concourse.bass as bass
import concourse.mybir as mybir
from concourse.bass_utils import run_bass_kernel_spmd

F32 = mybir.dt.float32
BF16 = mybir.dt.bfloat16
I32 = mybir.dt.int32
U8 = mybir.dt.uint8
AF = mybir.ActivationFunctionType
ALU = mybir.AluOpType
AX = mybir.AxisListType

BQ, T, D, H, DH = 2, 2048, 1024, 16, 64
NRC = 8
TILE, NTILE = 512, 4
NG, NCH = 4, 8
FF = 4096
LN127 = float(np.log(np.float64(127.0)))

O_X = 0
O_RF = 4096
O_KF = 8192
O_A4 = 12288
O_A5 = 16448
O_TS = 20544
O_QI = 20544
O_SM = 14336
O_ESC = 21056
O_SMALL = 21568
O_GB2 = 21952
ARNF_COLS = 30144
O_XLNF = 0
O_SF = 0
O_B1 = 2112
O_B2 = 4160
O_B3 = 6208
O_B4 = 8256
O_B5 = 10304
O_B6 = 12352
O_B7 = 14400
O_B8 = 16448
O_XSV = 18496
O_WWIN = O_B5
ARNB_COLS = 37008
S_TINY, S_TINY2, S_ST1, S_ST2, S_LNS, S_LNB = 0, 8, 16, 24, 32, 40
S_GAM, S_EGP, S_E2C, S_D1 = 48, 112, 176, 240
S_MAX, S_DQ, S_DQ2, S_XSV, S_ACC = 304, 312, 320, 328, 336
S_RCC = 370

GBIG = [O_RF, O_KF, O_GB2, O_GB2 + 4096]
GQB = [O_B1, O_B2, O_B3, O_B4]
WDQ = [O_B5, O_B7, O_B8, O_XLNF]


class Prog:
    def __init__(self, nc, es):
        self.nc, self.es = nc, es
        self.sem, self.cnt, self.hwm = {}, {}, {}

    def new(self, name):
        self.sem[name] = self.es.enter_context(self.nc.semaphore(name))
        self.cnt[name] = 0

    def inc(self, instr, name, k):
        self.cnt[name] += k
        instr.then_inc(self.sem[name], k)
        return self.cnt[name]

    def wait(self, eng, ekey, name, val=None):
        if val is None:
            val = self.cnt[name]
        assert val <= self.cnt[name], f"forward wait {name}:{val}>{self.cnt[name]}"
        if val <= 0:
            return
        if self.hwm.get((ekey, name), -1) >= val:
            return
        eng.wait_ge(self.sem[name], val)
        self.hwm[(ekey, name)] = val


def build(n_seq=BQ, n_tiles=NTILE, last_phase="G", dbg=()):
    nc = bass.Bass("TRN2", target_bir_lowering=False, debug=False)
    dt = nc.dram_tensor
    x_h = dt("x", [BQ, T, D], F32, kind="ExternalInput").ap()
    wr_h = dt("wr", [D, D], F32, kind="ExternalInput").ap()
    wk_h = dt("wk", [D, D], F32, kind="ExternalInput").ap()
    wv_h = dt("wv", [D, D], F32, kind="ExternalInput").ap()
    ww_h = dt("ww", [D, D], F32, kind="ExternalInput").ap()
    wo_h = dt("wo", [D, D], F32, kind="ExternalInput").ap()
    wg_h = dt("wg", [FF, D], F32, kind="ExternalInput").ap()
    wu_h = dt("wu", [FF, D], F32, kind="ExternalInput").ap()
    wd_h = dt("wd", [D, FF], F32, kind="ExternalInput").ap()
    u_h = dt("u_p", [128, 8], F32, kind="ExternalInput").ap()
    tm_h = dt("tm", [128, 32], F32, kind="ExternalInput").ap()
    lnwb_h = dt("lnwb", [128, 2], F32, kind="ExternalInput").ap()
    mstrf_h = dt("mstrf", [128, 1024], BF16, kind="ExternalInput").ap()
    mdiaf_h = dt("mdiaf", [128, 1024], U8, kind="ExternalInput").ap()
    bdA_h = dt("bdA", [128, 64], BF16, kind="ExternalInput").ap()
    bdB_h = dt("bdB", [128, 64], BF16, kind="ExternalInput").ap()
    sel2f_h = dt("sel2f", [128, 128], F32, kind="ExternalInput").ap()
    hsel_h = dt("hsel", [128, 2], BF16, kind="ExternalInput").ap()
    onesf_h = dt("onesf", [128, 128], F32, kind="ExternalInput").ap()
    cpool_h = dt("cpool", [128, 10], F32, kind="ExternalInput").ap()
    out_h = dt("out", [BQ, T, D], F32, kind="ExternalOutput").ap()
    wtr_h = dt("wtr", [D, D], BF16, kind="Internal").ap()
    wtk_h = dt("wtk", [D, D], BF16, kind="Internal").ap()
    wtv_h = dt("wtv", [D, D], BF16, kind="Internal").ap()
    wtw_h = dt("wtw", [D, D], BF16, kind="Internal").ap()
    wto_h = dt("wto", [D, D], BF16, kind="Internal").ap()
    wqtg_h = dt("wqtg", [D, FF], BF16, kind="Internal").ap()
    wqtu_h = dt("wqtu", [D, FF], BF16, kind="Internal").ap()
    wqtd_h = dt("wqtd", [FF, D], BF16, kind="Internal").ap()
    dbg_h = {n: dt("dbg_" + n, sh, dy, kind="ExternalOutput").ap() for n, sh, dy in dbg}

    es = contextlib.ExitStack()
    with es:
        sbt = lambda n, s, d: es.enter_context(nc.sbuf_tensor(n, s, d))
        arn = sbt("arn", [128, ARNF_COLS], F32)
        arnb = sbt("arnb", [128, ARNB_COLS], BF16)
        u_sb = sbt("u_sb", [128, 8], F32)
        tm_sb = sbt("tm_sb", [128, 32], F32)
        lnwb = sbt("lnwb_s", [128, 2], F32)
        mstrf = sbt("mstrf_s", [128, 1024], BF16)
        mdiaf = sbt("mdiaf_s", [128, 1024], U8)
        bdA = sbt("bdA_s", [128, 64], BF16)
        bdB = sbt("bdB_s", [128, 64], BF16)
        sel2f = sbt("sel2f_s", [128, 128], F32)
        hsel = sbt("hsel_s", [128, 2], BF16)
        onesf = sbt("onesf_s", [128, 128], F32)
        cpool = sbt("cpool_s", [128, 10], F32)
        state = sbt("state", [128, 512], BF16)
        thr2 = sbt("thr2", [128, 4], F32)
        lnmw = sbt("lnmw", [128, 4], F32)
        ps = es.enter_context(nc.psum_tensor("ps", [128, 4096], F32))

        AR = arn.ap()
        AR16 = arnb.ap()
        ARI = arn.bitcast(I32).ap()
        PIT = ARNF_COLS
        PIT16 = ARNB_COLS

        def f32v(off, cols, p0=0, pn=128):
            return bass.AP(AR.tensor, p0 * PIT + off, [[PIT, pn], [1, cols]])

        def f32d(off, dims, p0=0, pn=128):
            return bass.AP(AR.tensor, p0 * PIT + off, [[PIT, pn]] + dims)

        def bfv(off, cols, p0=0, pn=128):
            return bass.AP(AR16.tensor, p0 * PIT16 + 2 * off, [[PIT16, pn], [1, cols]])

        def bfd(off, dims, p0=0, pn=128):
            return bass.AP(AR16.tensor, p0 * PIT16 + 2 * off, [[PIT16, pn]] + dims)

        def i32v(off, cols, p0=0, pn=128):
            return bass.AP(ARI.tensor, p0 * PIT + off, [[PIT, pn], [1, cols]])

        def smv(col, n, p0=0, pn=128):
            return f32v(O_SMALL + col, n, p0, pn)

        def xsv_ap(n=8):
            return bass.AP(AR16.tensor, 2 * O_XSV, [[PIT16, 128], [1, n]])

        for i, v in enumerate([0.5, -0.5, 1e-5, -LN127, 1.0, 0.0, LN127, -1.0]):
            nc.const_aps.register(F32, float(v), cpool[:, i:i + 1])

        blk = es.enter_context(nc.Block())
        P = Prog(nc, es)
        for s in ("q1", "q2", "pe", "v", "s", "gp", "st", "qx", "qxl", "qwv",
                  "qww", "qwo", "qkht", "qhqt", "qwinA", "qwinB", "qwd", "qgqt",
                  "qldA", "qldB", "qc", "qat", "qwk", "qwr",
                  "qht0", "qht1", "qht2", "qht3"):
            P.new(s)

        eps_c = cpool[:, 2:3]
        one_c = cpool[:, 4:5]
        ln127_c = cpool[:, 6:7]
        neg1_c = cpool[:, 7:8]

        def wload(sy, dst_off, dst_bs, src_h, src_off, pstride, nblk, inner,
                  sem, pieces=1):
            per = max(1, nblk // pieces)
            for s in range(0, nblk, per):
                n = min(per, nblk - s)
                P.inc(sy.dma_start(
                    out=bass.AP(AR16.tensor, dst_off + s * dst_bs,
                                [[PIT16, 128], [dst_bs, n], [1, inner]]),
                    in_=bass.AP(src_h.tensor, src_off + s * 128 * pstride,
                                [[pstride, 128], [128 * pstride, n],
                                 [1, inner]])), sem, 16)

        @blk.sync
        def _(sy):
            for dst, src in ((u_sb, u_h), (tm_sb, tm_h), (lnwb, lnwb_h),
                             (mstrf, mstrf_h), (mdiaf, mdiaf_h), (bdA, bdA_h),
                             (bdB, bdB_h), (sel2f, sel2f_h), (hsel, hsel_h),
                             (onesf, onesf_h), (cpool, cpool_h)):
                P.inc(sy.dma_start(out=dst[:], in_=src), "qc", 16)
        consts_q1 = P.cnt["qc"]
        for eng, key in ((blk.vector, "v"), (blk.scalar, "sc"),
                         (blk.gpsimd, "gp"), (blk.tensor, "pe")):
            @eng
            def _(e, key=key):
                P.wait(e, key, "qc", consts_q1)

        @blk.vector
        def _(v):
            v.tensor_scalar(smv(S_RCC + 1, 1), cpool[:, 4:5], 127.0, None,
                            ALU.mult)
            v.drain()
            v.reciprocal(smv(S_RCC, 1), smv(S_RCC + 1, 1))
            v.drain()
            P.inc(v.engine_nop(), "v", 1)

        att_list = [(wr_h, None, wtr_h), (wk_h, None, wtk_h),
                    (wv_h, None, wtv_h), (ww_h, None, wtw_h), (wo_h, None, wto_h)]
        for i, (src_h, res_t, hbm_t) in enumerate(att_list):
            @blk.gpsimd
            def _(gp, i=i, src_h=src_h):
                if i > 0:
                    P.wait(gp, "gp", "qat")
                for oc in range(8):
                    P.inc(gp.dma_start(
                        out=bass.AP(AR16.tensor, 2 * O_B1 + oc * 1024,
                                    [[PIT16, 128], [1, 1024]]),
                        in_=src_h[oc * 128:(oc + 1) * 128, :]), "q2", 16)

            @blk.sync
            def _(sy, i=i, res_t=res_t, hbm_t=hbm_t):
                P.wait(sy, "sy", "q2")
                if True:
                    for oc in range(8):
                        P.wait(sy, "sy", "qat")
                        for dc in range(8):
                            P.inc(sy.dma_start_transpose(
                                out=bass.AP(AR16.tensor, 2 * O_B7 + dc * 128,
                                            [[PIT16, 128], [1, 128]]),
                                in_=bass.AP(AR16.tensor, 2 * O_B1 + oc * 1024 + dc * 128,
                                            [[PIT16, 128], [1, 128]])), "qat", 16)
                        P.wait(sy, "sy", "qat")
                        src = bass.AP(AR16.tensor, 2 * O_B7,
                                      [[PIT16, 128], [128, 8], [1, 128]])
                        dst = bass.AP(hbm_t.tensor, oc * 128,
                                      [[D, 128], [128 * D, 8], [1, 128]])
                        P.inc(sy.dma_start(out=dst, in_=src), "qat", 16)
        attw_q1 = P.cnt["qat"]

        wstg = [f32v(O_A4, 4096), f32v(O_A5, 4096)]
        absout = f32v(O_X, 4096)
        sqout = f32v(O_RF, 4096)
        sgout = bfv(O_B5, 4096)
        wqnat = bfv(O_B6, 4096)
        accw = smv(S_ACC, 32)
        ffn_specs = [(wg_h, wqtg_h, 32, 1024, FF), (wu_h, wqtu_h, 32, 1024, FF),
                     (wd_h, wqtd_h, 8, 4096, D)]
        for j, (src_h, dst_h, nchunk, ccols, wid) in enumerate(ffn_specs):
            for ch in range(nchunk):
                stg = wstg[ch % 2]

                qld = "qldA" if ch % 2 == 0 else "qldB"

                @blk.sync
                def _(sy, ch=ch, stg=stg, src_h=src_h, ccols=ccols, qld=qld):
                    if ch >= 2 or j > 0:
                        P.wait(sy, "sy", "v")
                        P.wait(sy, "sy", "s")
                    qc_ = ccols // 4
                    for q in range(4):
                        P.inc(sy.dma_start(
                            out=bass.AP(stg.tensor, stg.offset + q * qc_,
                                        [[PIT, 128], [1, qc_]]),
                            in_=src_h[ch * 128:(ch + 1) * 128,
                                      q * qc_:(q + 1) * qc_]), qld, 16)
                ld = P.cnt[qld]

                @blk.vector
                def _(v, ch=ch, stg=stg, ccols=ccols, ld=ld, qld=qld):
                    P.wait(v, "v", qld, ld)
                    stgv = bass.AP(stg.tensor, stg.offset, [[PIT, 128], [1, ccols]])
                    P.inc(v.scalar_tensor_tensor(stgv, stgv, -1.0, stgv,
                                                 ALU.mult, ALU.max,
                                                 accum_out=smv(S_ACC + ch, 1)),
                          "v", 1)

            @blk.vector
            def _(v, j=j, nchunk=nchunk):
                v.drain()
                P.inc(v.reduce_sum(smv(S_TINY, 1), smv(S_ACC, nchunk), axis=AX.X), "v", 1)

            @blk.tensor
            def _(t, j=j):
                P.wait(t, "pe", "v")
                P.wait(t, "pe", "qc", consts_q1)
                P.inc(t.matmul(ps[0:1, 0:1], smv(S_TINY, 1), cpool[:, 4:5],
                               start=True, stop=True), "pe", 1)

            @blk.vector
            def _(v, j=j):
                P.wait(v, "v", "pe")
                v.tensor_scalar(smv(S_TINY, 1, 0, 1), ps[0:1, 0:1],
                                1.0 / (FF * D), 1e-5, ALU.mult, ALU.max)
                v.drain()
                v.tensor_scalar(smv(S_TINY + 1, 1, 0, 1), smv(S_TINY, 1, 0, 1),
                                0.5, None, ALU.mult)
                v.drain()
                v.tensor_mul(smv(S_TINY + 2, 1, 0, 1), smv(S_TINY + 1, 1, 0, 1),
                             smv(S_TINY + 1, 1, 0, 1))
                P.inc(v.tensor_mul(smv(S_TINY + 3, 1, 0, 1), smv(S_TINY, 1, 0, 1),
                                   smv(S_RCC, 1, 0, 1)), "v", 1)

            @blk.tensor
            def _(t, j=j):
                P.wait(t, "pe", "s")
                P.wait(t, "pe", "v")
                P.inc(t.matmul(ps[:, 512:514], onesf[0:1, :], smv(S_TINY + 2, 2, 0, 1),
                               start=True, stop=True), "pe", 1)

            @blk.vector
            def _(v, j=j):
                P.wait(v, "v", "pe")
                v.tensor_copy(thr2[:, j:j + 1], ps[:, 512:513])
                v.tensor_copy(lnmw[:, j:j + 1], ps[:, 513:514])
                v.drain()
                P.inc(v.engine_nop(), "v", 1)
            thr_v = P.cnt["v"]

            npiece = ccols // 128
            for ch in range(nchunk):
                stg = wstg[ch % 2]

                qld = "qldA" if ch % 2 == 0 else "qldB"

                @blk.sync
                def _(sy, ch=ch, stg=stg, src_h=src_h, ccols=ccols, qld=qld):
                    P.wait(sy, "sy", "s")
                    qc_ = ccols // 4
                    for q in range(4):
                        P.inc(sy.dma_start(
                            out=bass.AP(stg.tensor, stg.offset + q * qc_,
                                        [[PIT, 128], [1, qc_]]),
                            in_=src_h[ch * 128:(ch + 1) * 128,
                                      q * qc_:(q + 1) * qc_]), qld, 16)
                ld = P.cnt[qld]

                @blk.scalar
                def _(sc, ch=ch, stg=stg, ccols=ccols, ld=ld, thr_v=thr_v, qld=qld):
                    P.wait(sc, "sc", qld, ld)
                    P.wait(sc, "sc", "v", thr_v)
                    if ch >= 1:
                        P.wait(sc, "sc", "v")
                    sc.activation(bass.AP(sqout.tensor, sqout.offset,
                                          [[PIT, 128], [1, ccols]]),
                                  bass.AP(stg.tensor, stg.offset,
                                          [[PIT, 128], [1, ccols]]), AF.Square)
                    P.inc(sc.activation(bass.AP(sgout.tensor, sgout.offset,
                                                [[PIT16, 128], [1, ccols]]),
                                        bass.AP(stg.tensor, stg.offset,
                                                [[PIT, 128], [1, ccols]]), AF.Sign),
                          "s", 1)
                sq_s = P.cnt["s"]

                @blk.vector
                def _(v, ch=ch, j=j, ccols=ccols, sq_s=sq_s, thr_v=thr_v):
                    P.wait(v, "v", "s", sq_s)
                    P.wait(v, "v", "v", thr_v)
                    P.wait(v, "v", "q1")
                    P.inc(v.scalar_tensor_tensor(
                        bass.AP(wqnat.tensor, wqnat.offset, [[PIT16, 128], [1, ccols]]),
                        bass.AP(sqout.tensor, sqout.offset, [[PIT, 128], [1, ccols]]),
                        thr2[:, j:j + 1],
                        bass.AP(sgout.tensor, sgout.offset, [[PIT16, 128], [1, ccols]]),
                        ALU.is_gt, ALU.mult), "v", 1)
                wq_v = P.cnt["v"]

                @blk.sync
                def _(sy, ch=ch, j=j, npiece=npiece, wid=wid, dst_h=dst_h, wq_v=wq_v):
                    P.wait(sy, "sy", "v", wq_v)
                    P.wait(sy, "sy", "qat")
                    P.wait(sy, "sy", "q1")
                    for pc in range(npiece):
                        P.inc(sy.dma_start_transpose(
                            out=bass.AP(AR16.tensor, 2 * O_B7 + pc * 128,
                                        [[PIT16, 128], [1, 128]]),
                            in_=bass.AP(AR16.tensor, 2 * O_B6 + pc * 128,
                                        [[PIT16, 128], [1, 128]])), "q1", 16)
                    P.wait(sy, "sy", "q1")
                    src = bass.AP(AR16.tensor, 2 * O_B7,
                                  [[PIT16, 128], [128, npiece], [1, 128]])
                    dst = bass.AP(dst_h.tensor, ch * 128,
                                  [[wid, 128], [128 * wid, npiece], [1, 128]])
                    P.inc(sy.dma_start(out=dst, in_=src), "q1", 16)
        ffnw_q1 = P.cnt["q1"]

        @blk.vector
        def _(v):
            P.wait(v, "v", "pe")
            P.inc(v.tensor_copy(smv(S_TINY2, 3), lnmw[:, 0:3]), "v", 1)
        tiny2_v = P.cnt["v"]

        PH = ["A", "B", "C", "D", "E", "F", "G"]
        phlim = PH.index(last_phase)

        def phase_on(p):
            return PH.index(p) <= phlim


        for s_i in range(n_seq):
            @blk.vector
            def _(v, s_i=s_i):
                if s_i > 0:
                    P.wait(v, "v", "pe")
                P.inc(v.memset(state[:], 0.0), "v", 1)

            for t_i in range(n_tiles):
                tok0 = t_i * TILE
                @blk.sync
                def _(sy, s_i=s_i, t_i=t_i, tok0=tok0):
                    P.wait(sy, "sy", "v")
                    P.wait(sy, "sy", "st")
                    src = bass.AP(x_h.tensor, s_i * T * D + tok0 * D,
                                  [[D, 128], [128 * D, NG], [1, D]])
                    P.inc(sy.dma_start(out=f32d(O_X, [[1024, NG], [1, 1024]]), in_=src),
                          "qx", 16)
                xload = P.cnt["qx"]

                @blk.vector
                def _(v, xload=xload):
                    P.wait(v, "v", "qx", xload)
                    P.wait(v, "v", "v", tiny2_v)
                    for g in range(NG):
                        xg = f32v(O_X + g * 1024, 1024)
                        v.reduce_sum(smv(S_ST1 + g, 1), xg, axis=AX.X)
                        v.scalar_tensor_tensor(f32v(O_A5 + g * 1024, 1024), xg, 1.0, xg,
                                               ALU.mult, ALU.mult,
                                               accum_out=smv(S_ST2 + g, 1))
                    v.drain()
                    v.tensor_scalar(smv(S_LNB, NG), smv(S_ST1, NG), 1.0 / D, None,
                                    ALU.mult)
                    v.drain()
                    v.tensor_mul(smv(S_TINY, NG), smv(S_LNB, NG), smv(S_LNB, NG))
                    v.drain()
                    P.inc(v.scalar_tensor_tensor(smv(S_LNS, NG), smv(S_ST2, NG),
                                                 1.0 / D, smv(S_TINY, NG),
                                                 ALU.mult, ALU.subtract), "v", 1)
                lnst_v = P.cnt["v"]

                @blk.scalar
                def _(sc, lnst_v=lnst_v):
                    P.wait(sc, "sc", "v", lnst_v)
                    sc.activation(smv(S_TINY, NG), smv(S_LNS, NG), AF.Ln, bias=eps_c)
                    sc.drain()
                    P.inc(sc.activation(smv(S_LNS, NG), smv(S_TINY, NG), AF.Exp,
                                        scale=-0.5), "s", 1)

                @blk.vector
                def _(v):
                    P.wait(v, "v", "s")
                    P.inc(v.scalar_tensor_tensor(smv(S_LNB, NG), smv(S_LNB, NG), -1.0,
                                                 smv(S_LNS, NG), ALU.mult, ALU.mult),
                          "v", 1)
                lnab_v = P.cnt["v"]

                @blk.scalar
                def _(sc, lnab_v=lnab_v):
                    P.wait(sc, "sc", "v", lnab_v)
                    P.wait(sc, "sc", "pe")
                    for g in range(NG):
                        P.inc(sc.activation(bfv(O_B1 + g * 512, 1024),
                                            f32v(O_X + g * 1024, 1024), AF.Identity,
                                            bias=smv(S_LNB + g, 1),
                                            scale=smv(S_LNS + g, 1)), "s", 1)
                lnap_s = P.cnt["s"]

                @blk.vector
                def _(v, t_i=t_i):
                    bnd = bass.AP(AR16.tensor, 2 * O_XLNF + 15,
                                  [[PIT16, 128], [528, 8], [1, 1]])
                    if t_i == 0:
                        P.inc(v.memset(bnd, 0.0), "v", 1)
                    else:
                        P.inc(v.tensor_copy(bnd, xsv_ap()), "v", 1)
                bnd_v = P.cnt["v"]

                @blk.sync
                def _(sy, lnap_s=lnap_s, bnd_v=bnd_v):
                    P.wait(sy, "sy", "s", lnap_s)
                    P.wait(sy, "sy", "v", bnd_v)
                    P.wait(sy, "sy", "gp")
                    for g in range(NG):
                        P.inc(sy.dma_start_transpose(
                            out=bass.AP(AR16.tensor,
                                        2 * O_XLNF + 16 + g * 128,
                                        [[PIT16, 128], [528, 8], [1, 128]]),
                            in_=bass.AP(AR16.tensor, 2 * O_B1 + g * 1024,
                                        [[PIT16, 128], [128, 8], [1, 128]])),
                            "qxl", 16)
                    P.wait(sy, "sy", "pe")
                    P.wait(sy, "sy", "q1", ffnw_q1)
                    wload(sy, 2 * O_WWIN, 1024, wtr_h, 0, D, 8, 1024, "qwr")
                    P.wait(sy, "sy", "qxl")
                    P.wait(sy, "sy", "qgqt")
                    wload(sy, 2 * O_B1, 1024, wtk_h, 0, D, 4, 1024, "qwk")
                    wload(sy, 2 * O_B8, 1024, wtk_h, 512 * D, D, 4, 1024, "qwk")
                xlnf_q1 = P.cnt["qxl"]
                wr_q1 = P.cnt["qwr"]
                wk_q1 = P.cnt["qwk"]
                if not phase_on("B"):
                    continue

                @blk.vector
                def _(v, xlnf_q1=xlnf_q1):
                    P.wait(v, "v", "qxl", xlnf_q1)
                    last = bass.AP(AR16.tensor, 2 * O_XLNF + 527,
                                   [[PIT16, 128], [528, 8], [1, 1]])
                    v.tensor_copy(xsv_ap(), last)
                    cur = bass.AP(AR16.tensor, 2 * O_XLNF + 16,
                                  [[PIT16, 128], [528, 8], [1, 512]])
                    prv = bass.AP(AR16.tensor, 2 * O_XLNF + 15,
                                  [[PIT16, 128], [528, 8], [1, 512]])
                    P.inc(v.tensor_sub(bfd(O_B4, [[512, 8], [1, 512]]), cur, prv),
                          "v", 1)
                dlt_v = P.cnt["v"]

                def emit_mix(eng, dst_off, mcol, is_gp=False):
                    for dc in range(8):
                        xprev = bass.AP(AR16.tensor, 2 * O_XLNF + dc * 528 + 15,
                                        [[PIT16, 128], [1, 512]])
                        if is_gp:
                            tmp = f32v(O_ESC, 512)
                            eng.tensor_scalar(tmp, bfv(O_B4 + dc * 256, 512),
                                              tm_sb[:, mcol * 8 + dc: mcol * 8 + dc + 1],
                                              None, ALU.mult)
                            ins = eng.tensor_add(bfv(dst_off + dc * 256, 512),
                                                 tmp, xprev)
                        else:
                            ins = eng.scalar_tensor_tensor(
                                bfv(dst_off + dc * 256, 512),
                                bfv(O_B4 + dc * 256, 512),
                                tm_sb[:, mcol * 8 + dc: mcol * 8 + dc + 1],
                                xprev, ALU.mult, ALU.add)
                    return ins

                @blk.vector
                def _(v, dlt_v=dlt_v):
                    v.drain()
                    P.inc(emit_mix(v, O_B2, 0), "v", 1)
                mixr_v = P.cnt["v"]

                @blk.vector
                def _(v):
                    P.inc(emit_mix(v, O_B3, 1), "v", 1)
                mixk_gp = P.cnt["v"]

                def emit_proj(wsrc, mix_off, msem, mval, evac_fn, extra_pe_wait=None):
                    s_entry = P.cnt["s"]
                    for oc in range(8):
                        b = oc % 2

                        @blk.tensor
                        def _(t, oc=oc, b=b, wsrc=wsrc, mix_off=mix_off,
                              msem=msem, mval=mval, extra_pe_wait=extra_pe_wait,
                              s_entry=s_entry):
                            P.wait(t, "pe", msem, mval)
                            P.wait(t, "pe", "qat", attw_q1)
                            if extra_pe_wait is not None:
                                P.wait(t, "pe", *extra_pe_wait)
                            if oc < 2:
                                P.wait(t, "pe", "s", s_entry)
                            else:
                                P.wait(t, "pe", "s", P.cnt["s"] - 1)
                            for kc in range(8):
                                m = t.matmul(
                                    ps[:, b * 512:(b + 1) * 512],
                                    wsrc(kc, oc),
                                    bfv(mix_off + kc * 256, 512),
                                    start=(kc == 0), stop=(kc == 7))
                            P.inc(m, "pe", 1)
                        pm = P.cnt["pe"]

                        @blk.scalar
                        def _(sc, oc=oc, b=b, pm=pm):
                            P.wait(sc, "sc", "pe", pm)
                            P.inc(evac_fn(sc, oc, ps[:, b * 512:(b + 1) * 512]), "s", 1)
                    return P.cnt["s"]

                wsrc_win = lambda kc, oc: bass.AP(AR16.tensor,
                                                  2 * O_WWIN + kc * 1024 + oc * 128,
                                                  [[PIT16, 128], [1, 128]])
                wsrc_win2 = lambda kc, oc: bass.AP(
                    AR16.tensor,
                    2 * (O_B1 if kc < 4 else O_B8) + (kc % 4) * 1024 + oc * 128,
                    [[PIT16, 128], [1, 128]])

                r_done = emit_proj(wsrc_win, O_B2, "v", mixr_v,
                                   lambda sc, oc, pb: sc.activation(
                                       f32v(O_RF + oc * 512, 512), pb, AF.Sigmoid),
                                   extra_pe_wait=("qwr", wr_q1))

                @blk.sync
                def _(sy):
                    P.wait(sy, "sy", "pe")
                    wload(sy, 2 * O_WWIN, 1024, wtv_h, 0, D, 8, 1024, "qwv")
                wv_q1 = P.cnt["qwv"]
                k_done = emit_proj(wsrc_win2, O_B3, "v", mixk_gp,
                                   lambda sc, oc, pb: sc.activation(
                                       f32v(O_KF + oc * 512, 512), pb, AF.Identity),
                                   extra_pe_wait=("qwk", wk_q1))

                @blk.sync
                def _(sy):
                    P.wait(sy, "sy", "pe")
                    wload(sy, 2 * O_B1, 1024, wtw_h, 0, D, 4, 1024, "qww")
                    wload(sy, 2 * O_B8, 1024, wtw_h, 512 * D, D, 4, 1024, "qww")
                ww_q1 = P.cnt["qww"]

                @blk.vector
                def _(v):
                    P.wait(v, "v", "pe")
                    v.drain()
                    P.inc(emit_mix(v, O_B3, 2), "v", 1)
                mixv_gp = P.cnt["v"]


                for g in range(NG):
                    @blk.tensor
                    def _(t, g=g, mixv_gp=mixv_gp, wv_q1=wv_q1):
                        P.wait(t, "pe", "v", mixv_gp)
                        P.wait(t, "pe", "qwv", wv_q1)
                        if g >= 1:
                            P.wait(t, "pe", "v", P.cnt["v"])
                        for half in range(2):
                            b = 2 + half
                            for kc in range(8):
                                m = t.matmul(
                                    ps[:, b * 512:(b + 1) * 512],
                                    bass.AP(AR16.tensor,
                                            2 * O_B3 + kc * 512 + g * 128,
                                            [[PIT16, 128], [1, 128]]),
                                    bass.AP(AR16.tensor,
                                            2 * O_WWIN + kc * 1024 + half * 512,
                                            [[PIT16, 128], [1, 512]]),
                                    start=(kc == 0), stop=(kc == 7))
                        P.inc(m, "pe", 1)
                    vg_pe = P.cnt["pe"]

                    @blk.vector
                    def _(v, g=g, vg_pe=vg_pe):
                        P.wait(v, "v", "pe", vg_pe)
                        P.wait(v, "v", "q1", ffnw_q1)
                        P.inc(v.tensor_copy(bfv(O_B7 + g * 512, 1024),
                                            ps[:, 1024:2048]), "v", 1)
                vtok_v = P.cnt["v"]

                @blk.vector
                def _(v):
                    P.wait(v, "v", "pe")
                    v.drain()
                    P.inc(emit_mix(v, O_B2, 3), "v", 1)
                mixw_v = P.cnt["v"]

                w_done = emit_proj(wsrc_win2, O_B2, "v", mixw_v,
                                   lambda sc, oc, pb: sc.activation(
                                       f32v(O_A5 + oc * 512, 512), pb, AF.Exp,
                                       scale=-1.0),
                                   extra_pe_wait=("qww", ww_q1))
                if not phase_on("C"):
                    continue

                @blk.scalar
                def _(sc, w_done=w_done):
                    P.wait(sc, "sc", "s", w_done)
                    sc.drain()
                    P.inc(sc.activation(f32v(O_A5, 4096), f32v(O_A5, 4096), AF.Ln,
                                        bias=one_c), "s", 1)
                lp_s = P.cnt["s"]

                @blk.vector
                def _(v, lp_s=lp_s):
                    P.wait(v, "v", "s", lp_s)
                    P.wait(v, "v", "pe")
                    v.tensor_scalar(f32v(O_A4 + 1, 4096), f32v(O_A5, 4096), -1.0,
                                    -1e-4, ALU.mult, ALU.add)
                    v.memset(f32v(O_A4, 1), 0.0)
                    v.drain()
                    P.inc(v.tensor_tensor_scan(f32v(O_A4 + 1, 4096), f32v(O_A4 + 1, 4096),
                                               cpool[:, 5:6].broadcast_to((128, 4096)),
                                               0.0, ALU.add, ALU.add), "v", 1)
                    a_end = f32d(O_A4 + 64, [[64, 64]])
                    a_prev = f32d(O_A4, [[64, 64]])
                    v.drain()
                    v.tensor_sub(smv(S_D1, 64), a_end, a_prev)
                    v.drain()
                    P.inc(v.scalar_tensor_tensor(smv(S_GAM, 64), smv(S_D1, 64), 0.5,
                                                 a_prev, ALU.mult, ALU.add), "v", 1)
                scan_v = P.cnt["v"]

                @blk.scalar
                def _(sc, scan_v=scan_v):
                    P.wait(sc, "sc", "v", scan_v)
                    sc.activation(smv(S_EGP, 64), smv(S_D1, 64), AF.Exp, scale=0.5)
                    P.inc(sc.activation(smv(S_E2C, 64), smv(S_D1, 64), AF.Exp), "s", 1)
                egp_s = P.cnt["s"]

                gam_rep = bass.AP(AR.tensor, O_SMALL + S_GAM,
                                  [[PIT, 128], [1, 64], [0, 64]])
                egp_rep = bass.AP(AR.tensor, O_SMALL + S_EGP,
                                  [[PIT, 128], [1, 64], [0, 64]])

                @blk.vector
                def _(v, egp_s=egp_s):
                    P.wait(v, "v", "s", egp_s)
                    P.inc(v.tensor_sub(f32d(O_A5, [[64, 64], [1, 64]]),
                                       f32d(O_A4, [[64, 64], [1, 64]]), gam_rep), "v", 1)
                arg1_v = P.cnt["v"]

                @blk.scalar
                def _(sc, arg1_v=arg1_v):
                    P.wait(sc, "sc", "v", arg1_v)
                    P.inc(sc.activation(f32v(O_A5, 4096), f32v(O_A5, 4096), AF.Exp),
                          "s", 1)
                ea_s = P.cnt["s"]

                @blk.vector
                def _(v, ea_s=ea_s, r_done=r_done):
                    P.wait(v, "v", "s", ea_s)
                    P.wait(v, "v", "s", r_done)
                    P.inc(v.tensor_mul(bfv(O_B4, 4096), f32v(O_RF, 4096),
                                       f32v(O_A5, 4096)), "v", 1)
                rt_v = P.cnt["v"]

                @blk.gpsimd
                def _(gp, rt_v=rt_v):
                    P.wait(gp, "gp", "v", rt_v)
                    P.wait(gp, "gp", "q1", ffnw_q1)
                    P.inc(gp.tensor_tensor(bfd(O_B6, [[64, 64], [1, 64]]),
                                           bfd(O_B4, [[64, 64], [1, 64]]),
                                           egp_rep, ALU.mult), "gp", 1)
                rh_gp = P.cnt["gp"]

                @blk.vector
                def _(v):
                    v.drain()
                    P.inc(v.tensor_sub(f32d(O_A5, [[64, 64], [1, 64]]), gam_rep,
                                       f32d(O_A4 + 1, [[64, 64], [1, 64]])), "v", 1)
                arg2_v = P.cnt["v"]

                @blk.scalar
                def _(sc, arg2_v=arg2_v):
                    P.wait(sc, "sc", "v", arg2_v)
                    P.inc(sc.activation(f32v(O_A5, 4096), f32v(O_A5, 4096), AF.Exp),
                          "s", 1)
                ek_s = P.cnt["s"]

                @blk.vector
                def _(v, ek_s=ek_s, k_done=k_done):
                    P.wait(v, "v", "s", ek_s)
                    P.wait(v, "v", "s", k_done)
                    P.inc(v.tensor_mul(bfv(O_B5, 4096), f32v(O_KF, 4096),
                                       f32v(O_A5, 4096)), "v", 1)
                kt_v = P.cnt["v"]

                @blk.gpsimd
                def _(gp, kt_v=kt_v):
                    P.wait(gp, "gp", "v", kt_v)
                    P.inc(gp.tensor_tensor(bfd(O_B2, [[64, 64], [1, 64]]),
                                           bfd(O_B5, [[64, 64], [1, 64]]),
                                           egp_rep, ALU.mult), "gp", 1)
                kh_gp = P.cnt["gp"]

                for rc in range(NRC):
                    @blk.scalar
                    def _(sc, rc=rc, k_done=k_done):
                        P.wait(sc, "sc", "s", k_done)
                        if rc > 0:
                            P.wait(sc, "sc", "v", P.cnt["v"])
                        P.inc(sc.activation(f32v(O_ESC, 512),
                                            f32v(O_KF + rc * 512, 512), AF.Exp,
                                            bias=u_sb[:, rc:rc + 1]), "s", 1)
                    e_s = P.cnt["s"]

                    @blk.vector
                    def _(v, rc=rc, e_s=e_s, r_done=r_done):
                        P.wait(v, "v", "s", e_s)
                        P.wait(v, "v", "s", r_done)
                        P.inc(v.tensor_mul(bfv(O_B3 + rc * 256, 512),
                                           f32v(O_RF + rc * 512, 512),
                                           f32v(O_ESC, 512)), "v", 1)
                re_v = P.cnt["v"]

                @blk.sync
                def _(sy, kh_gp=kh_gp):
                    P.wait(sy, "sy", "gp", kh_gp)
                    P.wait(sy, "sy", "pe")
                    P.inc(sy.dma_start_transpose(
                        out=bass.AP(AR16.tensor, 2 * O_B1,
                                    [[PIT16, 128], [128, 32], [1, 128]]),
                        in_=bass.AP(AR16.tensor, 2 * O_B2,
                                    [[PIT16, 128], [128, 32], [1, 128]])),
                        "qkht", 16)
                kht_q1 = P.cnt["qkht"]
                if not phase_on("D"):
                    continue

                import os
                nch_lim = int(os.environ.get("KCH", NCH))
                dstop = os.environ.get("KDSTOP", "")
                sel_m, oev_m, stup_m = {}, {}, {}
                for c in range(nch_lim):
                    q = c % 2
                    rq = 64 * q
                    scb = 0 if q == 0 else 2

                    @blk.tensor
                    def _(t, c=c, q=q, rq=rq, scb=scb):
                        P.wait(t, "pe", "v", rt_v)
                        P.wait(t, "pe", "v", kt_v)
                        P.wait(t, "pe", "v", re_v)
                        if c >= 2:
                            P.wait(t, "pe", "v", sel_m[c - 2])
                            P.wait(t, "pe", "v", stup_m[c - 2])
                        for rc in range(NRC):
                            co = rc * 512 + c * 64
                            t.matmul(ps[rq:rq + 64, scb * 512 + rc * 64: scb * 512 + rc * 64 + 64],
                                     bass.AP(AR16.tensor, 2 * O_B5 + co,
                                             [[PIT16, 64], [1, 64]]),
                                     bass.AP(AR16.tensor, 2 * O_B4 + co,
                                             [[PIT16, 64], [1, 64]]),
                                     start=True, stop=True, tile_position=(0, rq))
                            t.matmul(ps[rq:rq + 64, (scb + 1) * 512 + rc * 64: (scb + 1) * 512 + rc * 64 + 64],
                                     bass.AP(AR16.tensor, 64 * PIT16 + 2 * O_B5 + co,
                                             [[PIT16, 64], [1, 64]]),
                                     bass.AP(AR16.tensor, 64 * PIT16 + 2 * O_B4 + co,
                                             [[PIT16, 64], [1, 64]]),
                                     start=True, stop=True, tile_position=(64, rq))
                        if c >= 1:
                            P.wait(t, "pe", "v", sel_m[c - 1])
                        for rc in range(NRC):
                            co = rc * 512 + c * 64
                            t.matmul(ps[rq:rq + 64, 4 * 512 + rc * 64: 4 * 512 + rc * 64 + 64],
                                     bdA[:, 0:64],
                                     bass.AP(AR16.tensor, 2 * O_B3 + co,
                                             [[PIT16, 128], [1, 64]]),
                                     start=True, stop=True, tile_position=(0, rq))
                            m = t.matmul(ps[rq:rq + 64, 5 * 512 + rc * 64: 5 * 512 + rc * 64 + 64],
                                         bdB[:, 0:64],
                                         bass.AP(AR16.tensor, 2 * O_B3 + co,
                                                 [[PIT16, 128], [1, 64]]),
                                         start=True, stop=True, tile_position=(0, rq))
                        P.inc(m, "pe", 1)
                    sc_pe = P.cnt["pe"]
                    if dstop == "scores":
                        sel_m[c] = P.cnt["v"]; oev_m[c] = P.cnt["s"]; stup_m[c] = P.cnt["v"]
                        continue

                    @blk.vector
                    def _(v, c=c, rq=rq, scb=scb, sc_pe=sc_pe):
                        P.wait(v, "v", "pe", sc_pe)
                        sfv = bass.AP(AR16.tensor, rq * PIT16 + 2 * O_SF,
                                      [[PIT16, 64], [1, 1024]])
                        psc = bass.AP(ps.ap().tensor, rq * 4096 + scb * 512,
                                      [[4096, 64], [1, 1024]])
                        pdd = bass.AP(ps.ap().tensor, rq * 4096 + 4 * 512,
                                      [[4096, 64], [1, 1024]])
                        v.tensor_tensor(sfv, psc, mstrf[rq:rq + 64, :], ALU.mult)
                        v.drain()
                        P.inc(v.select(sfv, mdiaf[rq:rq + 64, :], pdd, sfv,
                                       add_drain=True), "v", 1)
                    sel_m[c] = P.cnt["v"]
                    if dstop == "sel":
                        oev_m[c] = P.cnt["s"]; stup_m[c] = P.cnt["v"]
                        continue

                    @blk.tensor
                    def _(t, c=c, q=q, rq=rq, scb=scb):
                        P.wait(t, "pe", "v", sel_m[c])
                        P.wait(t, "pe", "v", vtok_v)
                        P.wait(t, "pe", "qkht", kht_q1)
                        if c >= 1:
                            P.wait(t, "pe", "v", oev_m[c - 1])
                            P.wait(t, "pe", "v", stup_m[c - 1])
                        g = c // 2
                        for rc in range(NRC):
                            vA = bass.AP(AR16.tensor,
                                         rq * PIT16 + 2 * O_B7 + g * 1024 + rc * 128,
                                         [[PIT16, 64], [1, 64]])
                            vB = bass.AP(AR16.tensor,
                                         rq * PIT16 + 2 * O_B7 + g * 1024 + rc * 128 + 64,
                                         [[PIT16, 64], [1, 64]])
                            sfA = bass.AP(AR16.tensor, rq * PIT16 + 2 * O_SF + rc * 64,
                                          [[PIT16, 64], [1, 64]])
                            sfB = bass.AP(AR16.tensor,
                                          rq * PIT16 + 2 * O_SF + 512 + rc * 64,
                                          [[PIT16, 64], [1, 64]])
                            t.matmul(ps[0:64, 6 * 512 + rc * 64: 6 * 512 + rc * 64 + 64],
                                     state[0:64, rc * 64: rc * 64 + 64],
                                     bass.AP(AR16.tensor, 2 * O_B6 + rc * 512 + c * 64,
                                             [[PIT16, 64], [1, 64]]),
                                     start=True, stop=True, tile_position=(0, 0))
                            t.matmul(ps[0:64, 7 * 512 + rc * 64: 7 * 512 + rc * 64 + 64],
                                     vA, sfA, start=True, stop=True,
                                     tile_position=(rq, 0))
                            t.matmul(ps[64:128, 6 * 512 + rc * 64: 6 * 512 + rc * 64 + 64],
                                     state[64:128, rc * 64: rc * 64 + 64],
                                     bass.AP(AR16.tensor,
                                             64 * PIT16 + 2 * O_B6 + rc * 512 + c * 64,
                                             [[PIT16, 64], [1, 64]]),
                                     start=True, stop=True, tile_position=(64, 64))
                            t.matmul(ps[64:128, 7 * 512 + rc * 64: 7 * 512 + rc * 64 + 64],
                                     vB, sfB, start=True, stop=True,
                                     tile_position=(rq, 64))
                            kA = bass.AP(AR16.tensor,
                                         rq * PIT16 + 2 * O_B1 + rc * 512 + (c // 2) * 128,
                                         [[PIT16, 64], [1, 64]])
                            kB = bass.AP(AR16.tensor,
                                         rq * PIT16 + 2 * O_B1 + rc * 512 + (c // 2) * 128 + 64,
                                         [[PIT16, 64], [1, 64]])
                            t.matmul(ps[0:64, scb * 512 + rc * 64: scb * 512 + rc * 64 + 64],
                                     kA, vA, start=True, stop=True,
                                     tile_position=(rq, 0))
                            m = t.matmul(ps[64:128, scb * 512 + rc * 64: scb * 512 + rc * 64 + 64],
                                         kB, vB, start=True, stop=True,
                                         tile_position=(rq, 64))
                        P.inc(m, "pe", 1)
                    av_pe = P.cnt["pe"]
                    if dstop == "av":
                        oev_m[c] = P.cnt["s"]; stup_m[c] = P.cnt["v"]
                        continue

                    @blk.scalar
                    def _(sc, c=c, av_pe=av_pe):
                        P.wait(sc, "sc", "pe", av_pe)
                        srcC = bass.AP(ps.ap().tensor, 6 * 512,
                                       [[4096, 128], [64, 8], [1, 64]])
                        P.inc(sc.activation(f32v(O_TS, 512), srcC, AF.Identity),
                              "s", 1)
                    oevs = P.cnt["s"]

                    @blk.vector
                    def _(v, c=c, av_pe=av_pe, oevs=oevs):
                        P.wait(v, "v", "pe", av_pe)
                        P.wait(v, "v", "s", oevs)
                        dst = bass.AP(AR16.tensor, 2 * (O_B8) + c * 64,
                                      [[PIT16, 128], [512, 8], [1, 64]])
                        srcI = bass.AP(ps.ap().tensor, 7 * 512,
                                       [[4096, 128], [64, 8], [1, 64]])
                        P.inc(v.tensor_add(dst, f32v(O_TS, 512), srcI), "v", 1)
                    oev_m[c] = P.cnt["v"]

                    @blk.vector
                    def _(v, c=c, av_pe=av_pe, scb=scb):
                        P.wait(v, "v", "pe", av_pe)
                        e2c_rep = bass.AP(AR.tensor, O_SMALL + S_E2C + c,
                                          [[PIT, 128], [8, 8], [0, 64]])
                        v.drain()
                        v.tensor_tensor(f32d(O_A5, [[64, 8], [1, 64]]),
                                        state[:].rearrange("p (a b) -> p a b", a=8),
                                        e2c_rep, ALU.mult)
                        v.drain()
                        P.inc(v.tensor_add(state[:], f32v(O_A5, 512),
                                           ps[:, scb * 512: (scb + 1) * 512]), "v", 1)
                    stup_m[c] = P.cnt["v"]

                @blk.sync
                def _(sy):
                    P.wait(sy, "sy", "pe")
                    wload(sy, 2 * O_B1, 1024, wto_h, 0, D, 8, 1024, "qwo")
                wo_q1 = P.cnt["qwo"]
                if not phase_on("E"):
                    continue

                gst_done = None
                for rc in range(NRC):
                    @blk.scalar
                    def _(sc, rc=rc):
                        P.wait(sc, "sc", "v", oev_m[nch_lim - 1])
                        if rc > 0:
                            P.wait(sc, "sc", "pe", P.cnt["pe"])
                        P.inc(sc.activation(bass.AP(AR16.tensor, 2 * O_SF,
                                                    [[PIT16, 128], [1, 512]]),
                                            bfv(O_B8 + rc * 256, 512), AF.Square),
                              "s", 1)
                    o2_s = P.cnt["s"]

                    @blk.tensor
                    def _(t, rc=rc, o2_s=o2_s):
                        P.wait(t, "pe", "s", o2_s)
                        P.wait(t, "pe", "v", sel_m[nch_lim - 1])
                        P.wait(t, "pe", "v", stup_m[nch_lim - 1])
                        b1 = rc // 4
                        pb = 32 * (rc % 4)
                        t.matmul(ps[pb:pb + 2, b1 * 512:(b1 + 1) * 512],
                                 hsel[:, 0:2], bfv(O_B8 + rc * 256, 512),
                                 start=True, stop=True, tile_position=(0, pb))
                        m = t.matmul(ps[pb:pb + 2, (2 + b1) * 512:(3 + b1) * 512],
                                     hsel[:, 0:2],
                                     bass.AP(AR16.tensor, 2 * O_SF,
                                             [[PIT16, 128], [1, 512]]),
                                     start=True, stop=True, tile_position=(0, pb))
                        P.inc(m, "pe", 1)
                gst_pe = P.cnt["pe"]

                og_v = None
                for b1 in range(2):
                    @blk.vector
                    def _(v, b1=b1, gst_pe=gst_pe):
                        P.wait(v, "v", "pe", gst_pe)
                        mu = f32v(O_SM, 512)
                        v.tensor_scalar(mu, ps[:, b1 * 512:(b1 + 1) * 512], 1.0 / DH,
                                        1e6, ALU.mult, ALU.min)
                        v.drain()
                        v.tensor_scalar_max(mu, mu, -1e6)
                        v.drain()
                        v.tensor_mul(f32v(O_A5, 512), mu, mu)
                        v.drain()
                        P.inc(v.scalar_tensor_tensor(f32v(O_SM + 512, 512),
                                                     ps[:, (2 + b1) * 512:(3 + b1) * 512],
                                                     1.0 / DH, f32v(O_A5, 512),
                                                     ALU.mult, ALU.subtract), "v", 1)
                    gv_v = P.cnt["v"]

                    @blk.vector
                    def _(v, b1=b1, gv_v=gv_v):
                        v.drain()
                        P.inc(v.tensor_scalar(f32v(O_SM + 512, 512),
                                              f32v(O_SM + 512, 512), 0.0, 1e18,
                                              ALU.max, ALU.min), "v", 1)
                    gvc_v = P.cnt["v"]

                    @blk.scalar
                    def _(sc, b1=b1, gvc_v=gvc_v):
                        P.wait(sc, "sc", "v", gvc_v)
                        sc.activation(f32v(O_A5, 512), f32v(O_SM + 512, 512), AF.Ln,
                                      bias=eps_c)
                        sc.drain()
                        P.inc(sc.activation(f32v(O_SM + 512, 512), f32v(O_A5, 512),
                                            AF.Exp, scale=-0.5), "s", 1)
                    gs_s = P.cnt["s"]

                    @blk.vector
                    def _(v, b1=b1, gs_s=gs_s):
                        P.wait(v, "v", "s", gs_s)
                        P.inc(v.tensor_mul(f32v(O_SM, 512), f32v(O_SM, 512),
                                           f32v(O_SM + 512, 512)), "v", 1)
                    gm2_v = P.cnt["v"]

                    for rr in range(4):
                        rc = b1 * 4 + rr
                        pb = 32 * rr

                        @blk.tensor
                        def _(t, rc=rc, pb=pb, gm2_v=gm2_v):
                            P.wait(t, "pe", "v", gm2_v)
                            if rc >= 2:
                                P.wait(t, "pe", "v", P.cnt["v"])
                            bs = 4 + (rc % 2)
                            bm = 6 + (rc % 2)
                            t.matmul(ps[:, bs * 512:(bs + 1) * 512], sel2f[pb:pb + 2, :],
                                     f32v(O_SM + 512, 512, pb, 2),
                                     start=True, stop=True, tile_position=(pb, 0))
                            m = t.matmul(ps[:, bm * 512:(bm + 1) * 512],
                                         sel2f[pb:pb + 2, :], f32v(O_SM, 512, pb, 2),
                                         start=True, stop=True, tile_position=(pb, 0))
                            P.inc(m, "pe", 1)
                        gb_pe = P.cnt["pe"]

                        @blk.vector
                        def _(v, rc=rc, gb_pe=gb_pe):
                            P.wait(v, "v", "pe", gb_pe)
                            v.drain()
                            bs = 4 + (rc % 2)
                            bm = 6 + (rc % 2)
                            v.tensor_mul(f32v(O_A5, 512), bfv(O_B8 + rc * 256, 512),
                                         ps[:, bs * 512:(bs + 1) * 512])
                            v.drain()
                            P.inc(v.tensor_sub(f32v(O_A5 + 512, 512), f32v(O_A5, 512),
                                               ps[:, bm * 512:(bm + 1) * 512]), "v", 1)
                        ga_v = P.cnt["v"]

                        @blk.scalar
                        def _(sc, rc=rc, ga_v=ga_v):
                            P.wait(sc, "sc", "v", ga_v)
                            P.inc(sc.activation(f32v(O_A5, 512), f32v(O_A5 + 512, 512),
                                                AF.Identity, bias=lnwb[:, 1:2],
                                                scale=lnwb[:, 0:1]), "s", 1)
                        gf_s = P.cnt["s"]

                        @blk.vector
                        def _(v, rc=rc, gf_s=gf_s):
                            P.wait(v, "v", "s", gf_s)
                            P.inc(v.tensor_mul(bfv(O_B8 + rc * 256, 512),
                                               f32v(O_A5, 512),
                                               f32v(O_RF + rc * 512, 512)), "v", 1)
                og_v = P.cnt["v"]

                for g in range(NG):
                    @blk.tensor
                    def _(t, g=g, og_v=og_v, wo_q1=wo_q1):
                        P.wait(t, "pe", "v", og_v)
                        P.wait(t, "pe", "qwo", wo_q1)
                        if g >= 1:
                            P.wait(t, "pe", "v", P.cnt["v"])
                        for half in range(2):
                            b = half
                            for kc in range(8):
                                m = t.matmul(
                                    ps[:, b * 512:(b + 1) * 512],
                                    bass.AP(AR16.tensor,
                                            2 * O_B8 + kc * 512 + g * 128,
                                            [[PIT16, 128], [1, 128]]),
                                    bass.AP(AR16.tensor,
                                            2 * O_B1 + kc * 1024 + half * 512,
                                            [[PIT16, 128], [1, 512]]),
                                    start=(kc == 0), stop=(kc == 7))
                        P.inc(m, "pe", 1)
                    op_pe = P.cnt["pe"]

                    @blk.vector
                    def _(v, g=g, op_pe=op_pe):
                        P.wait(v, "v", "pe", op_pe)
                        P.inc(v.tensor_add(f32v(O_X + g * 1024, 1024),
                                           f32v(O_X + g * 1024, 1024),
                                           ps[:, 0:1024]), "v", 1)
                x1_v = P.cnt["v"]

                @blk.vector
                def _(v, x1_v=x1_v):
                    P.wait(v, "v", "v", x1_v)
                    for g in range(NG):
                        xg = f32v(O_X + g * 1024, 1024)
                        v.reduce_sum(smv(S_ST1 + g, 1), xg, axis=AX.X)
                        v.scalar_tensor_tensor(f32v(O_A5 + g * 1024, 1024), xg, 1.0, xg,
                                               ALU.mult, ALU.mult,
                                               accum_out=smv(S_ST2 + g, 1))
                    v.drain()
                    v.tensor_scalar(smv(S_LNB, NG), smv(S_ST1, NG), 1.0 / D, None,
                                    ALU.mult)
                    v.drain()
                    v.tensor_mul(smv(S_TINY, NG), smv(S_LNB, NG), smv(S_LNB, NG))
                    v.drain()
                    P.inc(v.scalar_tensor_tensor(smv(S_LNS, NG), smv(S_ST2, NG),
                                                 1.0 / D, smv(S_TINY, NG),
                                                 ALU.mult, ALU.subtract), "v", 1)
                l2st_v = P.cnt["v"]

                @blk.scalar
                def _(sc, l2st_v=l2st_v):
                    P.wait(sc, "sc", "v", l2st_v)
                    sc.activation(smv(S_TINY, NG), smv(S_LNS, NG), AF.Ln, bias=eps_c)
                    sc.drain()
                    P.inc(sc.activation(smv(S_LNS, NG), smv(S_TINY, NG), AF.Exp,
                                        scale=-0.5), "s", 1)

                @blk.vector
                def _(v):
                    P.wait(v, "v", "s")
                    P.inc(v.scalar_tensor_tensor(smv(S_LNB, NG), smv(S_LNB, NG), -1.0,
                                                 smv(S_LNS, NG), ALU.mult, ALU.mult),
                          "v", 1)
                l2ab_v = P.cnt["v"]

                tq_h = []
                for p in range(2):
                    ga, gb = 2 * p, 2 * p + 1

                    @blk.scalar
                    def _(sc, p=p, ga=ga, gb=gb, l2ab_v=l2ab_v):
                        P.wait(sc, "sc", "v", l2ab_v)
                        if p > 0:
                            P.wait(sc, "sc", "v", P.cnt["v"])
                        for i, g in enumerate((ga, gb)):
                            P.inc(sc.activation(f32v(O_A5 + i * 1024, 1024),
                                                f32v(O_X + g * 1024, 1024),
                                                AF.Identity,
                                                bias=smv(S_LNB + g, 1),
                                                scale=smv(S_LNS + g, 1)), "s", 1)
                    h_s = P.cnt["s"]

                    @blk.vector
                    def _(v, p=p, ga=ga, gb=gb, h_s=h_s):
                        P.wait(v, "v", "s", h_s)
                        for i, g in enumerate((ga, gb)):
                            v.reduce_max(smv(S_MAX + g, 1),
                                         f32v(O_A5 + i * 1024, 1024), axis=AX.X,
                                         apply_absolute_value=True)
                        v.drain()
                        v.tensor_scalar_max(smv(S_MAX + ga, 2), smv(S_MAX + ga, 2),
                                            1e-5)
                        v.drain()
                        v.reciprocal(smv(S_TINY + 4, 2), smv(S_MAX + ga, 2))
                        v.drain()
                        v.tensor_scalar(smv(S_TINY + 6, 2), smv(S_TINY + 4, 2),
                                        127.0, None, ALU.mult)
                        for g in (ga, gb):
                            v.tensor_mul(smv(S_DQ + g, 1), smv(S_MAX + g, 1),
                                         smv(S_TINY2, 1))
                            v.tensor_mul(smv(S_DQ2 + g, 1), smv(S_MAX + g, 1),
                                         smv(S_TINY2 + 1, 1))
                        v.drain()
                        P.inc(v.engine_nop(), "v", 1)
                    hm_v = P.cnt["v"]

                    @blk.scalar
                    def _(sc, p=p, ga=ga, gb=gb, hm_v=hm_v):
                        P.wait(sc, "sc", "v", hm_v)
                        P.wait(sc, "sc", "gp", P.cnt["gp"])
                        P.wait(sc, "sc", "pe")
                        for i in range(2):
                            P.inc(sc.activation(f32v(O_A4 + i * 1024, 1024),
                                                f32v(O_A5 + i * 1024, 1024),
                                                AF.Identity,
                                                scale=smv(S_TINY + 6 + i, 1),
                                                bias=cpool[:, 8:9]), "s", 1)
                    hv_s = P.cnt["s"]

                    @blk.vector
                    def _(v, p=p, ga=ga, gb=gb, hv_s=hv_s):
                        P.wait(v, "v", "s", hv_s)
                        Y0 = [f32v(O_A4, 1024), f32v(O_A4 + 1024, 1024)]
                        YF = [f32v(O_A5 + 2048, 1024), f32v(O_A5 + 3072, 1024)]
                        BF = [f32v(O_A4 + 2048, 1024), f32v(O_A4 + 3072, 1024)]
                        for i in range(2):
                            v.tensor_scalar(BF[i], Y0[i], 128.0, -128.0,
                                            ALU.is_ge, ALU.mult)
                        v.drain()
                        for i in range(2):
                            v.tensor_add(YF[i], Y0[i], BF[i])
                        for m in (64.0, 32.0, 16.0, 8.0, 4.0, 2.0):
                            v.drain()
                            for i in range(2):
                                v.tensor_scalar(BF[i], YF[i], m, -m, ALU.is_ge,
                                                ALU.mult)
                            v.drain()
                            for i in range(2):
                                v.tensor_add(YF[i], YF[i], BF[i])
                        v.drain()
                        for i in range(2):
                            v.scalar_tensor_tensor(BF[i], YF[i], 1.0, YF[i],
                                                   ALU.is_ge, ALU.subtract)
                        v.drain()
                        for i, g in enumerate((ga, gb)):
                            P.inc(v.scalar_tensor_tensor(
                                bfv(O_B4 + g * 512, 1024), Y0[i], -128.0, BF[i],
                                ALU.add, ALU.add), "v", 1)
                    lad_h = P.cnt["v"]

                    for g in (ga, gb):
                        @blk.sync
                        def _(sy, g=g, lad_h=lad_h):
                            P.wait(sy, "sy", "v", lad_h)
                            if g == 0:
                                P.wait(sy, "sy", "pe")
                            P.inc(sy.dma_start_transpose(
                                out=bass.AP(AR16.tensor, 2 * O_B6 + g * 128,
                                            [[PIT16, 128], [512, 8], [1, 128]]),
                                in_=bass.AP(AR16.tensor, 2 * O_B4 + g * 1024,
                                            [[PIT16, 128], [128, 8], [1, 128]])),
                                "qht%d" % g, 16)
                        tq_h.append(P.cnt["qht%d" % g])
                hq_v = P.cnt["v"]
                if not phase_on("F"):
                    continue

                win_of = [(O_B7, O_B8), (O_B3, O_B4)]
                for fc in range(8):
                    wgo, wuo = win_of[fc % 2]
                    qwin = "qwinA" if fc % 2 == 0 else "qwinB"

                    @blk.sync
                    def _(sy, fc=fc, wgo=wgo, wuo=wuo, qwin=qwin):
                        P.wait(sy, "sy", "q1", ffnw_q1)
                        P.wait(sy, "sy", "pe")
                        wload(sy, 2 * wgo, 512, wqtg_h, fc * 512, FF, 8, 512,
                              qwin)
                        wload(sy, 2 * wuo, 512, wqtu_h, fc * 512, FF, 8, 512,
                              qwin)
                    win_q1 = P.cnt[qwin]

                    for g in range(NG):
                        @blk.tensor
                        def _(t, fc=fc, g=g, wgo=wgo, wuo=wuo, win_q1=win_q1,
                              tqh=tq_h[g], qwin=qwin):
                            P.wait(t, "pe", qwin, win_q1)
                            P.wait(t, "pe", "qht%d" % g, tqh)
                            bg = g % 2
                            if fc + g >= 1:
                                P.wait(t, "pe", "v", P.cnt["v"])
                            for kc in range(8):
                                t.matmul(ps[:, bg * 512:(bg + 1) * 512],
                                         bass.AP(AR16.tensor,
                                                 2 * O_B6 + kc * 512 + g * 128,
                                                 [[PIT16, 128], [1, 128]]),
                                         bass.AP(AR16.tensor, 2 * wgo + kc * 512, [[PIT16, 128], [1, 512]]),
                                         start=(kc == 0), stop=(kc == 7))
                            for kc in range(8):
                                m = t.matmul(ps[:, (2 + bg) * 512:(3 + bg) * 512],
                                             bass.AP(AR16.tensor,
                                                     2 * O_B6 + kc * 512 + g * 128,
                                                     [[PIT16, 128], [1, 128]]),
                                             bass.AP(AR16.tensor, 2 * wuo + kc * 512, [[PIT16, 128], [1, 512]]),
                                             start=(kc == 0), stop=(kc == 7))
                            P.inc(m, "pe", 1)
                        gu_pe = P.cnt["pe"]

                        @blk.scalar
                        def _(sc, g=g, gu_pe=gu_pe):
                            P.wait(sc, "sc", "pe", gu_pe)
                            bg = g % 2
                            P.inc(sc.activation(f32v(O_TS, 512),
                                                ps[:, bg * 512:(bg + 1) * 512],
                                                AF.Silu, scale=smv(S_DQ + g, 1)),
                                  "s", 1)
                        si_s = P.cnt["s"]

                        @blk.vector
                        def _(v, fc=fc, g=g, si_s=si_s):
                            P.wait(v, "v", "s", si_s)
                            bg = g % 2
                            P.inc(v.scalar_tensor_tensor(
                                f32v(GBIG[g] + fc * 512, 512),
                                ps[:, (2 + bg) * 512:(3 + bg) * 512],
                                smv(S_DQ2 + g, 1), f32v(O_TS, 512),
                                ALU.mult, ALU.mult), "v", 1)
                gu_done_v = P.cnt["v"]

                @blk.sync
                def _(sy):
                    P.wait(sy, "sy", "pe")
                    for qr in range(NG):
                        wload(sy, 2 * WDQ[qr], 512, wqtd_h, qr * 1024 * D,
                              D, 8, 512, "qwd", pieces=1)
                wdw_h0 = P.cnt["qwd"]

                GT = [O_B6, O_B1, O_B2, O_B3]
                tq_g = []

                for g in range(NG):
                    @blk.vector
                    def _(v, g=g, gu_done_v=gu_done_v):
                        P.wait(v, "v", "v", gu_done_v)
                        v.reduce_max(smv(S_MAX + g, 1), f32v(GBIG[g], 4096), axis=AX.X,
                                     apply_absolute_value=True)
                        v.drain()
                        v.tensor_scalar_max(smv(S_MAX + g, 1), smv(S_MAX + g, 1),
                                            1e-5)
                        v.drain()
                        v.reciprocal(smv(S_TINY + 4, 1), smv(S_MAX + g, 1))
                        v.drain()
                        v.tensor_scalar(smv(S_TINY + 5, 1), smv(S_TINY + 4, 1),
                                        127.0, None, ALU.mult)
                        v.tensor_mul(smv(S_DQ + g, 1), smv(S_MAX + g, 1),
                                     smv(S_TINY2 + 2, 1))
                        v.drain()
                        P.inc(v.engine_nop(), "v", 1)
                    gm_v = P.cnt["v"]
                    gsc_s = P.cnt["s"]

                    for hf in range(2):
                        @blk.scalar
                        def _(sc, g=g, hf=hf, gm_v=gm_v):
                            if hf == 0:
                                P.wait(sc, "sc", "v", gm_v)
                                P.wait(sc, "sc", "gp", P.cnt["gp"])
                            sc.drain()
                            P.inc(sc.activation(f32v(GBIG[g] + hf * 2048, 2048),
                                                f32v(GBIG[g] + hf * 2048, 2048),
                                                AF.Identity, scale=smv(S_TINY + 5, 1),
                                                bias=cpool[:, 8:9]), "s", 1)
                    gv_s = P.cnt["s"]

                    @blk.vector
                    def _(v, g=g, gv_s=gv_s):
                        P.wait(v, "v", "s", gv_s)
                        Y0 = [f32v(GBIG[g], 2048), f32v(GBIG[g] + 2048, 2048)]
                        YF = [f32v(O_A5, 2048), f32v(O_A5 + 2048, 2048)]
                        BF = [f32v(O_A4, 2048), f32v(O_A4 + 2048, 2048)]
                        for h in range(2):
                            v.tensor_scalar(BF[h], Y0[h], 128.0, -128.0,
                                            ALU.is_ge, ALU.mult)
                        v.drain()
                        for h in range(2):
                            v.tensor_add(YF[h], Y0[h], BF[h])
                        for m in (64.0, 32.0, 16.0, 8.0, 4.0, 2.0):
                            v.drain()
                            for h in range(2):
                                v.tensor_scalar(BF[h], YF[h], m, -m, ALU.is_ge,
                                                ALU.mult)
                            v.drain()
                            for h in range(2):
                                v.tensor_add(YF[h], YF[h], BF[h])
                        v.drain()
                        for h in range(2):
                            v.scalar_tensor_tensor(BF[h], YF[h], 1.0, YF[h],
                                                   ALU.is_ge, ALU.subtract)
                        v.drain()
                        for h in range(2):
                            P.inc(v.scalar_tensor_tensor(
                                bass.AP(AR16.tensor, 2 * GQB[g] + h * 2048,
                                        [[PIT16, 128], [1, 2048]]), Y0[h],
                                -128.0, BF[h], ALU.add, ALU.add),
                                "v", 1)
                    lad_v = P.cnt["v"]

                    @blk.sync
                    def _(sy, g=g, lad_v=lad_v):
                        P.wait(sy, "sy", "v", lad_v)
                        if g == 0:
                            P.wait(sy, "sy", "pe")
                        else:
                            P.wait(sy, "sy", "qgqt", P.cnt["qgqt"])
                        P.inc(sy.dma_start_transpose(
                            out=bass.AP(AR16.tensor, 2 * GT[g],
                                        [[PIT16, 128], [128, 32], [1, 128]]),
                            in_=bass.AP(AR16.tensor, 2 * GQB[g],
                                        [[PIT16, 128], [128, 32], [1, 128]])),
                            "qgqt", 16)
                    tq_g.append(P.cnt["qgqt"])

                for half in range(2):
                    if half == 0:
                        wdw_q1 = wdw_h0
                    else:
                        @blk.sync
                        def _(sy):
                            P.wait(sy, "sy", "pe")
                            for qr in range(NG):
                                wload(sy, 2 * WDQ[qr], 512, wqtd_h,
                                      qr * 1024 * D + 512, D, 8, 512, "qwd")
                        wdw_q1 = P.cnt["qwd"]

                    for g in range(NG):
                        @blk.tensor
                        def _(t, g=g, half=half, tq=tq_g[g], wdw_q1=wdw_q1):
                            P.wait(t, "pe", "qgqt", tq)
                            P.wait(t, "pe", "qwd", wdw_q1)
                            if g >= 2:
                                P.wait(t, "pe", "v", P.cnt["v"])
                            b = 4 + (g % 2) * 2 + half
                            for kc in range(32):
                                m = t.matmul(
                                    ps[:, b * 512:(b + 1) * 512],
                                    bass.AP(AR16.tensor, 2 * GT[g] + kc * 128,
                                            [[PIT16, 128], [1, 128]]),
                                    bass.AP(AR16.tensor,
                                            2 * WDQ[kc // 8] + (kc % 8) * 512,
                                            [[PIT16, 128], [1, 512]]),
                                    start=(kc == 0), stop=(kc == 31))
                            P.inc(m, "pe", 1)
                        dn_pe = P.cnt["pe"]

                        @blk.vector
                        def _(v, g=g, half=half, dn_pe=dn_pe):
                            P.wait(v, "v", "pe", dn_pe)
                            b = 4 + (g % 2) * 2 + half
                            P.inc(v.scalar_tensor_tensor(
                                f32v(O_X + g * 1024 + half * 512, 512),
                                ps[:, b * 512:(b + 1) * 512],
                                smv(S_DQ + g, 1),
                                f32v(O_X + g * 1024 + half * 512, 512),
                                ALU.mult, ALU.add), "v", 1)
                out_v = P.cnt["v"]
                if not phase_on("G"):
                    continue

                @blk.sync
                def _(sy, s_i=s_i, t_i=t_i, tok0=tok0, out_v=out_v):
                    P.wait(sy, "sy", "v", out_v)
                    dst = bass.AP(out_h.tensor, s_i * T * D + tok0 * D,
                                  [[D, 128], [128 * D, NG], [1, D]])
                    P.inc(sy.dma_start(out=dst, in_=f32d(O_X, [[1024, NG], [1, 1024]])),
                          "st", 16)

        if dbg_h:
            @blk.sync
            def _(sy):
                P.wait(sy, "sy", "v")
                P.wait(sy, "sy", "s")
                P.wait(sy, "sy", "gp")
                P.wait(sy, "sy", "pe")
                for n, ap in dbg_h.items():
                    kind, off = n.split("_", 1)
                    off = int(off)
                    cols = ap.ap[-1][1] if len(ap.ap) >= 2 else 1
                    nrow = ap.ap[0][1]
                    if kind == "f":
                        src = f32v(off, cols, 0, nrow)
                    elif kind == "b":
                        src = bfv(off, cols, 0, nrow)
                    else:
                        src = i32v(off, cols, 0, nrow)
                    P.inc(sy.dma_start(out=ap, in_=src), "st", 16)

        @blk.sync
        def _(sy):
            P.wait(sy, "sy", "st")

    return nc


def _prep_inputs(ip):
    x = np.asarray(ip["L0_attn_x"], np.float32)
    u = np.asarray(ip["L0_attn_u"], np.float32)
    u_p = np.zeros((128, 8), np.float32)
    for rc in range(8):
        u_p[0:64, rc] = u[2 * rc]
        u_p[64:128, rc] = u[2 * rc + 1]
    tm = np.zeros((128, 32), np.float32)
    for m, nm in enumerate(["tm_r", "tm_k", "tm_v", "tm_w"]):
        t = np.asarray(ip["L0_attn_" + nm], np.float32)
        for dc in range(8):
            tm[:, m * 8 + dc] = t[dc * 128:(dc + 1) * 128]
    lnwb = np.zeros((128, 2), np.float32)
    lnwb[:, 0] = np.tile(np.asarray(ip["L0_attn_ln_w"], np.float32), 2)
    lnwb[:, 1] = np.tile(np.asarray(ip["L0_attn_ln_b"], np.float32), 2)
    pp = np.arange(128) % 64
    cc = np.arange(64)
    mstrict = (cc[None, :] > pp[:, None]).astype(np.float32)
    mdiag = (cc[None, :] == pp[:, None]).astype(np.uint8)
    mstrf = np.tile(mstrict, (1, 16)).astype(ml_dtypes.bfloat16)
    mdiaf = np.tile(mdiag, (1, 16)).astype(np.uint8)
    bdA = np.zeros((128, 64), ml_dtypes.bfloat16)
    bdA[0:64, :] = 1
    bdB = np.zeros((128, 64), ml_dtypes.bfloat16)
    bdB[64:128, :] = 1
    sel2 = np.zeros((128, 128), ml_dtypes.bfloat16)
    for qq in range(4):
        sel2[32 * qq, 0:64] = 1
        sel2[32 * qq + 1, 64:128] = 1
    sel2f = sel2.astype(np.float32)
    hsel = np.zeros((128, 2), ml_dtypes.bfloat16)
    hsel[0:64, 0] = 1
    hsel[64:128, 1] = 1
    onesf = np.ones((128, 128), np.float32)
    cpool = np.zeros((128, 10), np.float32)
    cpool[:, 8] = 128.5
    cpool[:, 0] = 0.5
    cpool[:, 1] = -0.5
    cpool[:, 2] = 1e-5
    cpool[:, 3] = -LN127
    cpool[:, 4] = 1.0
    cpool[:, 5] = 0.0
    cpool[:, 6] = LN127
    cpool[:, 7] = -1.0
    base = {
        "wr": np.asarray(ip["L0_attn_W_r"], np.float32),
        "wk": np.asarray(ip["L0_attn_W_k"], np.float32),
        "wv": np.asarray(ip["L0_attn_W_v"], np.float32),
        "ww": np.asarray(ip["L0_attn_W_w"], np.float32),
        "wo": np.asarray(ip["L0_attn_W_o"], np.float32),
        "wg": np.asarray(ip["L0_ffn_w_gate"], np.float32),
        "wu": np.asarray(ip["L0_ffn_w_up"], np.float32),
        "wd": np.asarray(ip["L0_ffn_w_down"], np.float32),
        "u_p": u_p, "tm": tm, "lnwb": lnwb, "mstrf": mstrf, "mdiaf": mdiaf,
        "bdA": bdA, "bdB": bdB, "sel2f": sel2f, "hsel": hsel,
        "onesf": onesf, "cpool": cpool,
    }
    return [dict(base, x=np.ascontiguousarray(x[2 * c:2 * c + 2])) for c in range(8)]


def kernel(**inputs):
    in_maps = _prep_inputs(inputs)
    nc = build()
    res = run_bass_kernel_spmd(nc, in_maps, core_ids=list(range(8)))
    return np.concatenate([r["out"] for r in res.results], axis=0).astype(np.float32)
